# version 04
# baseline (speedup 1.0000x reference)
import contextlib
import numpy as np
import concourse.bass as bass
import concourse.mybir as mybir
from concourse.bass_utils import run_bass_kernel_spmd

F32 = mybir.dt.float32
BF16 = mybir.dt.bfloat16
AF = mybir.ActivationFunctionType
ALU = mybir.AluOpType
AX = mybir.AxisListType

NB = 4
SEQ = 2048
D = 1024
NTOK = NB * SEQ
NC512 = NTOK // 512
NT128 = NTOK // 128
DI = 3072
DH = 4096
RBIG = 12582912.0
QMAX = 32767.0

DBG = ()
PE_LAZY = False



def _host_consts():
    c = {}
    c["ident"] = np.eye(128, dtype=np.float32)
    ar = np.arange(128, dtype=np.float32)
    ramps = [ar % 32, ar % 32 + 1.0, -(ar % 32), 31.0 - ar % 32,
             63.0 - ar % 32, 95.0 - ar % 32, ar + 1.0]
    c["rampm"] = np.ascontiguousarray(np.concatenate(
        [np.tile(r.reshape(1, 128), (128, 1)) for r in ramps], axis=1))
    c["ramp127_col"] = (127.0 - ar).reshape(128, 1)
    c["mask_keep"] = np.triu(np.ones((128, 128), dtype=np.float32))
    sh = np.zeros((128, 4 * 128), dtype=np.float32)
    shp = np.zeros((128, 4 * 128), dtype=np.float32)
    for j in range(4):
        for b in range(128):
            if b + j < 128:
                sh[b, 128 * j + b + j] = 1.0
            if b + j - 128 >= 0:
                shp[b, 128 * j + b + j - 128] = 1.0
    c["shift4"] = sh
    c["shiftprev4"] = shp
    c["ones_col"] = np.ones((128, 1), dtype=np.float32)
    c["ones_row"] = np.ones((1, 128), dtype=np.float32)
    rep = np.zeros((32, 128), dtype=np.float32)
    for p in range(128):
        rep[p // 32, p] = 1.0
    c["rep32"] = rep
    s2 = np.zeros((65, 128), dtype=np.float32)
    s2[0, 1:33] = 1.0
    s2[64, 65:97] = 1.0
    c["s2_65"] = s2
    bd = np.zeros((128, 4), dtype=np.float32)
    for j in range(4):
        bd[32 * j:32 * (j + 1), j] = 1.0
    c["blockdiag4"] = bd
    ind = np.zeros((8, 8 * 128), dtype=np.float32)
    for e in range(8):
        ind[e, e * 128:(e + 1) * 128] = 1.0
    c["ind8"] = ind
    return c


def _prep_weights(inp):
    w = {}
    T = lambda a: np.ascontiguousarray(np.asarray(a, np.float32).T)
    A = lambda a: np.ascontiguousarray(np.asarray(a, np.float32))
    w["uv"] = np.ascontiguousarray(np.concatenate(
        [np.asarray(inp["L0_attn_u"], np.float32),
         np.asarray(inp["L0_attn_v"], np.float32)], axis=1))
    w["projwT"] = T(inp["L0_attn_proj_w"])
    w["mlpw1"] = A(inp["L0_attn_mlp_w1"])
    w["mlpw2"] = A(inp["L0_attn_mlp_w2"])
    w["wup_nat"] = A(inp["L0_ffn_w_up"])
    w["wgate_nat"] = A(inp["L0_ffn_w_gate"])
    w["wdown_nat"] = A(inp["L0_ffn_w_down"])
    w["wupT"] = T(inp["L0_ffn_w_up"])
    w["wgateT"] = T(inp["L0_ffn_w_gate"])
    w["wdownT"] = T(inp["L0_ffn_w_down"])
    w["wqT"] = T(inp["L1_attn_Wq"])
    w["wkT"] = T(inp["L1_attn_Wk"])
    w["wvT"] = T(inp["L1_attn_Wv"])
    woT = np.asarray(inp["L1_attn_Wo"], np.float32).T
    wop = np.zeros((8 * 128, 1024), dtype=np.float32)
    for p in range(8):
        wop[p * 128 + 1:p * 128 + 33] = woT[64 * p:64 * p + 32]
        wop[p * 128 + 65:p * 128 + 97] = woT[64 * p + 32:64 * p + 64]
    w["woTp"] = wop
    w["router_w"] = A(inp["L1_ffn_router"])
    w["eg"] = A(inp["L1_ffn_experts_gate"])
    w["eu"] = A(inp["L1_ffn_experts_up"])
    w["ed"] = A(inp["L1_ffn_experts_down"])
    w["sg"] = A(inp["L1_ffn_shared_gate"])
    w["su"] = A(inp["L1_ffn_shared_up"])
    w["sd"] = A(inp["L1_ffn_shared_down"])
    fmc = lambda v, n: np.ascontiguousarray(np.asarray(v, np.float32).reshape(n, 128).T)
    w["projb"] = fmc(inp["L0_attn_proj_b"], 8)
    w["n1w"] = fmc(inp["L0_attn_norm1_w"], 8)
    w["n2w"] = fmc(inp["L0_attn_norm2_w"], 8)
    w["mlpb1"] = fmc(inp["L0_attn_mlp_b1"], 32)
    w["mlpb2"] = fmc(inp["L0_attn_mlp_b2"], 8)
    w["bo"] = fmc(inp["L1_attn_bo"], 8)
    pr = np.zeros((1, 64), dtype=np.float32)
    pr[0, 0:16] = np.asarray(inp["L0_attn_decay_logit"], np.float32)
    pr[0, 16:32] = np.asarray(inp["L0_attn_alpha_logit"], np.float32)
    pr[0, 32] = np.float32(np.asarray(inp["L0_attn_k_base_gate_logit"]))
    pr[0, 33:37] = np.asarray(inp["L0_attn_k_base_kernel"], np.float32)
    w["params"] = pr
    return w



_ENG_SELF = {"vector": "sv", "scalar": "ss", "tensor": "sm", "gpsimd": "sg"}


class E:

    def __init__(self, c, name, real):
        self.c = c
        self.name = name
        self.real = real
        self.selfname = _ENG_SELF.get(name)
        self.pending = None

    @property
    def cnt(self):
        return self.c.ecnt.get(self.name, 0)

    def wait(self, sem_name, val):
        if self.real is not None and val > 0:
            self.real.wait_ge(self.c.sem[sem_name], val)

    def waitm(self, key):
        m = self.c.marks.get(key)
        if m is not None:
            self.wait(m[0], m[1])

    def wait_stream_all(self, stream):
        st = self.c.streams[stream]
        for sn in st["sems"]:
            v = self.c.dcnt.get(sn, 0)
            if v:
                self.wait(sn, v)

    def dma(self, out, in_, stream="ld", **kw):
        st = self.c.streams[stream]
        k = st["n"]
        sem_name = st["sems"][k % len(st["sems"])]
        prev = (k // len(st["sems"])) * 16
        st["n"] = k + 1
        self.c.dcnt[sem_name] = prev + 16
        if self.real is not None:
            if prev > 0:
                self.real.wait_ge(self.c.sem[sem_name], prev)
            self.real.dma_start(out=out, in_=in_, **kw).then_inc(
                self.c.sem[sem_name], 16)
        return (sem_name, prev + 16)

    def dma_t(self, out, in_, stream="ld"):
        st = self.c.streams[stream]
        k = st["n"]
        sem_name = st["sems"][k % len(st["sems"])]
        prev = (k // len(st["sems"])) * 16
        st["n"] = k + 1
        self.c.dcnt[sem_name] = prev + 16
        if self.real is not None:
            if prev > 0:
                self.real.wait_ge(self.c.sem[sem_name], prev)
            self.real.dma_start(out=out, in_=in_, transpose=True).then_inc(
                self.c.sem[sem_name], 16)
        return (sem_name, prev + 16)

    def collective_max(self, ins_ap, outs_ap):
        self.c.dcnt["cc"] += 1
        if self.real is not None:
            self.real.collective_compute(
                "AllReduce", ALU.max, replica_groups=[list(range(8))],
                ins=[ins_ap], outs=[outs_ap]).then_inc(self.c.sem["cc"], 1)
        return ("cc", self.c.dcnt["cc"])

    def __getattr__(self, op):
        if op.startswith("_"):
            raise AttributeError(op)

        if self.name == "tensor" and PE_LAZY:
            def emit_pe(*args, **kw):
                if self.real is not None:
                    self.pending = getattr(self.real, op)(*args, **kw)
                else:
                    self.pending = True
                return None
            return emit_pe

        def emit(*args, **kw):
            cur = self.c.ecnt[self.name]
            self.c.ecnt[self.name] = cur + 1
            if self.real is not None:
                if cur > 0:
                    self.real.wait_ge(self.c.sem[self.selfname], cur)
                getattr(self.real, op)(*args, **kw).then_inc(
                    self.c.sem[self.selfname], 1)
            return cur + 1
        return emit


class Ctx:
    def __init__(self, nc, stack, debug):
        self.nc = nc
        self.stack = stack
        self.debug = set(debug)
        self.t = {}
        self.sem = {}
        self.ecnt = {"vector": 0, "scalar": 0, "tensor": 0, "gpsimd": 0}
        self.dcnt = {"cc": 0}
        self.streams = {}
        self.marks = {}

    def add_sems(self, *names):
        for n in names:
            self.sem[n] = self.stack.enter_context(self.nc.semaphore(n))

    def add_stream(self, name, depth):
        sems = []
        for i in range(depth):
            sn = "D_%s%d" % (name, i)
            self.sem[sn] = self.stack.enter_context(self.nc.semaphore(sn))
            self.dcnt[sn] = 0
            sems.append(sn)
        self.streams[name] = {"sems": sems, "n": 0}

    def sb(self, name, shape, dtype):
        h = self.stack.enter_context(self.nc.sbuf_tensor(name, list(shape), dtype))
        self.t[name] = h
        return h

    def mark(self, key, eng):
        if eng.name == "tensor" and PE_LAZY:
            self.flush_pe(eng)
        self.marks[key] = (eng.selfname, self.ecnt[eng.name])

    def flush_pe(self, eng):
        if eng.pending is not None:
            self.ecnt["tensor"] += 1
            if eng.real is not None:
                eng.pending.then_inc(self.sem["sm"], 1)
            eng.pending = None

    def mark_tok(self, key, tok):
        self.marks[key] = tok

    def stage(self, name, bodies, barrier=True):
        snap_e = dict(self.ecnt)
        snap_d = dict(self.dcnt)
        snap_s = {k: dict(v) for k, v in self.streams.items()}

        def wrap(fn, en):
            def run(eng):
                if barrier:
                    for xn, sn in _ENG_SELF.items():
                        eng.wait(sn, snap_e[xn])
                    for sn, v in snap_d.items():
                        eng.wait(sn, v)
                fn(eng)
            return run

        for en, fn in bodies:
            wrap(fn, en)(E(self, en, None))
        count_e = dict(self.ecnt)
        count_d = dict(self.dcnt)
        self.ecnt = dict(snap_e)
        self.dcnt = dict(snap_d)
        self.streams = {k: dict(v) for k, v in snap_s.items()}
        with self.nc.Block(name) as blk:
            for en, fn in bodies:
                getattr(blk, en)(
                    lambda real, fn=fn, en=en: wrap(fn, en)(E(self, en, real)))
        assert self.ecnt == count_e, (name, self.ecnt, count_e)
        assert self.dcnt == count_d, (name, self.dcnt, count_d)



def build_nc(debug=()):
    nc = bass.Bass("TRN2", target_bir_lowering=False)
    stack = contextlib.ExitStack()
    c = Ctx(nc, stack, debug)

    def din(name, shape, dtype=F32):
        c.t[name] = nc.dram_tensor(name, list(shape), dtype, kind="ExternalInput").ap()

    def dint(name, shape, dtype=F32):
        kind = "ExternalOutput" if name in c.debug else "Internal"
        c.t[name] = nc.dram_tensor(name, list(shape), dtype, kind=kind).ap()

    din("x_fm", [D, NTOK])
    for nm, sh in (("uv", [D, 32]), ("projwT", [D, D]), ("mlpw1", [D, DH]),
                   ("mlpw2", [DH, D]), ("wup_nat", [DI, D]), ("wgate_nat", [DI, D]),
                   ("wdown_nat", [D, DI]), ("wupT", [D, DI]), ("wgateT", [D, DI]),
                   ("wdownT", [DI, D]), ("wqT", [512, 512]), ("wkT", [512, 512]),
                   ("wvT", [512, 512]), ("woTp", [1024, 1024]), ("router_w", [D, 8]),
                   ("eg", [8, D, 512]), ("eu", [8, D, 512]), ("ed", [8, 512, D]),
                   ("sg", [D, D]), ("su", [D, D]), ("sd", [D, D]),
                   ("projb", [128, 8]), ("n1w", [128, 8]), ("n2w", [128, 8]),
                   ("mlpb1", [128, 32]), ("mlpb2", [128, 8]), ("bo", [128, 8]),
                   ("params", [1, 64]), ("ident", [128, 128]),
                   ("rampm", [128, 7 * 128]), ("ramp127_col", [128, 1]),
                   ("mask_keep", [128, 128]), ("shift4", [128, 512]),
                   ("shiftprev4", [128, 512]), ("ones_col", [128, 1]),
                   ("ones_row", [1, 128]), ("rep32", [32, 128]),
                   ("s2_65", [65, 128]), ("blockdiag4", [128, 4]),
                   ("ind8", [8, 1024])):
        din(nm, sh)

    c.t["out_fm"] = nc.dram_tensor("out_fm", [D, NTOK], F32, kind="ExternalOutput").ap()

    dint("xn_sp", [D, NTOK])
    dint("htm_sp", [NTOK, D], BF16)
    dint("h2_sp", [D, NTOK])
    dint("s0_sp", [D, NTOK])
    dint("g0_sp", [D, NTOK])
    dint("x1_sp", [D, NTOK])
    dint("xn1_sp", [D, NTOK], BF16)
    dint("s1_sp", [D, NTOK])
    dint("sh_sp", [D, NTOK])
    dint("upraw_sp", [DI, NTOK])
    dint("gateraw_sp", [DI, NTOK])
    dint("t_sp", [DI, NTOK])
    dint("z_sp", [DI, NTOK])
    dint("wqup_sp", [DI, D], BF16)
    dint("wqgate_sp", [DI, D], BF16)
    dint("wqdown_sp", [D, DI], BF16)
    dint("xn1f_sp", [D, NTOK])
    dint("he_sp", [4096, NTOK], BF16)
    dint("cc_in", [1, 16])
    dint("cc_out", [1, 16])
    dint("mixer_dbg", [D, NTOK], BF16)

    c.sb("ident_b", [128, 128], BF16)
    c.sb("ident_f", [128, 128], F32)
    c.sb("onescol_b", [128, 1], BF16)
    c.sb("onesrow_f", [1, 128], F32)
    c.sb("n1w_s", [128, 8], F32)
    c.sb("n2w_s", [128, 8], F32)
    c.sb("projb_s", [128, 8], F32)
    c.sb("mlpb1_s", [128, 32], F32)
    c.sb("mlpb2_s", [128, 8], F32)
    c.sb("bo_s", [128, 8], F32)
    c.sb("scol", [128, 16], F32)
    c.sb("mx_acc", [128, 8], F32)
    c.sb("ccv", [1, 32], F32)

    c.add_sems("cc", "sv", "ss", "sm", "sg")
    c.add_stream("ld", 3)
    c.add_stream("spx", 2)
    c.add_stream("sph", 2)
    c.add_stream("w", 2)
    c.add_stream("out", 2)

    with contextlib.ExitStack() as l0st:
        for nm, sh, dt_ in (("maskk", [128, 128], F32), ("bandA", [128, 128], BF16),
                            ("bandP", [128, 128], BF16), ("tb_qd", [128, 128], F32),
                            ("tb_qh", [128, 128], F32), ("tb_kd", [128, 128], F32),
                            ("tb_kh", [128, 128], F32), ("tb_kh32", [128, 128], F32),
                            ("tb_kh64", [128, 128], F32), ("tb_q128", [128, 128], F32),
                            ("tb_k128tm", [128, 16], F32), ("gam128", [128, 1], F32),
                            ("uv_s", [128, 8 * 32], BF16),
                            ("qk_tm", [128, NT128 * 32], F32),
                            ("prm", [1, 64], F32), ("srow", [1, 128], F32)):
            c.t[nm] = l0st.enter_context(nc.sbuf_tensor(nm, sh, dt_))
        _prep(c)
        _ln0(c)
        _coll_scales(c, [(0, 0, 1, 1.0)], "ln0.maxdone", "cs0")
        _decay(c)
    _mlp(c)
    _ffn_a(c)
    _coll_scales(c, [(1, 2, 3, 1.0), (2, 4, 5, 1.0)], "fa.red.5.15", "cs1")
    _ffn_b(c)
    _coll_scales(c, [(3, 6, 7, 65536.0)], "fb.t.47", "cs2")
    _ffn_c(c)
    _coll_scales(c, [(4, 8, 9, 1.0)], "fc.z.47", "cs3")
    _ffn_d(c)
    _gate_ln1(c)
    _attn1(c)
    _sharedffn(c)
    _moe_a(c)
    _moe_b(c)

    return nc, c



def _prep(c):
    t = c.t
    with (
        c.nc.sbuf_tensor("rampm_s", [128, 7 * 128], F32) as rampm_s,
        c.nc.sbuf_tensor("r127c", [128, 1], F32) as r127c,
        c.nc.sbuf_tensor("sh_s", [128, 512], F32) as sh_s,
        c.nc.sbuf_tensor("shp_s", [128, 512], F32) as shp_s,
        c.nc.sbuf_tensor("ptmp", [128, 384], F32) as ptmp,
        c.nc.psum_tensor("pprep", [128, 512], F32) as pprep,
    ):
        _prep_body(c, rampm_s, r127c, sh_s, shp_s, ptmp, pprep)


def _prep_body(c, rampm_s, r127c, sh_s, shp_s, ptmp, pprep):
    t = c.t

    def body_sync(e):
        for dst, src in ((t["ident_f"][:, :], t["ident"]),
                         (t["maskk"][:, :], t["mask_keep"]),
                         (t["onesrow_f"][:, :], t["ones_row"]),
                         (t["n1w_s"][:, :], t["n1w"]),
                         (t["n2w_s"][:, :], t["n2w"]),
                         (t["projb_s"][:, :], t["projb"]),
                         (t["mlpb1_s"][:, :], t["mlpb1"]),
                         (t["mlpb2_s"][:, :], t["mlpb2"]),
                         (t["bo_s"][:, :], t["bo"]),
                         (t["prm"][:, :], t["params"]),
                         (rampm_s[:, :], t["rampm"]),
                         (r127c[:, :], t["ramp127_col"]),
                         (sh_s[:, :], t["shift4"]),
                         (shp_s[:, :], t["shiftprev4"])):
            tok = e.dma(dst, src, stream="ld")
        c.mark_tok("prep.ld", tok)

    def body_gp(e):
        e.dma(t["ident_b"][:, :], t["ident"], stream="w")
        e.dma(t["onescol_b"][:, :], t["ones_col"], stream="w")
        tok = e.dma(t["uv_s"][:, :].rearrange("p (ki o) -> p ki o", ki=8),
                    t["uv"].rearrange("(ki p) o -> p ki o", p=128), stream="w")
        c.mark_tok("prep.ldw", tok)

    def body_scalar(e):
        e.wait_stream_all("ld")
        e.activation(ptmp[0:1, 0:33], t["prm"][0:1, 0:33], AF.Sigmoid)
        c.mark("prep.sig", e)
        e.waitm("prep.gam")
        e.activation(t["srow"][0:1, 16:32], t["srow"][0:1, 0:16], AF.Ln)
        c.mark("prep.lng", e)
        e.waitm("prep.cols")
        e.activation(t["gam128"][:, 0:1], t["scol"][:, 1:2], AF.Exp, scale=128.0)
        for k, nm in enumerate(("tb_qd", "tb_qh", "tb_kd", "tb_kh", "tb_kh32",
                                "tb_kh64", "tb_q128")):
            e.activation(t[nm][:, :], rampm_s[:, 128 * k:128 * (k + 1)],
                         AF.Exp, scale=t["scol"][:, 1:2])
        c.mark("prep.tables", e)
        e.waitm("prep.lngb")
        e.activation(t["tb_k128tm"][:, :], pprep[:, 16:32], AF.Exp,
                     scale=r127c[:, 0:1])
        c.mark("prep.k128", e)

    def body_vector(e):
        e.memset(t["ccv"][0:1, :], 0.0)
        e.waitm("prep.sig")
        e.tensor_scalar(t["srow"][0:1, 0:16], ptmp[0:1, 0:16], 0.15, 1.0,
                        ALU.max, ALU.min)
        c.mark("prep.gam", e)
        e.waitm("prep.lng")
        for b in range(4):
            e.tensor_copy(ptmp[0:1, 64 + 32 * b:80 + 32 * b], t["srow"][0:1, 16:32])
            e.memset(ptmp[0:1, 80 + 32 * b:96 + 32 * b], 0.0)
            e.tensor_copy(ptmp[0:1, 192 + 32 * b:208 + 32 * b], ptmp[0:1, 16:32])
            e.memset(ptmp[0:1, 208 + 32 * b:224 + 32 * b], 0.0)
        c.mark("prep.strips", e)
        e.waitm("prep.colmm")
        e.tensor_copy(t["scol"][:, 1:2], pprep[:, 0:1])
        e.tensor_copy(t["scol"][:, 0:1], pprep[:, 1:2])
        c.mark("prep.cols", e)
        e.waitm("prep.tables")
        for nm in ("tb_qd", "tb_qh", "tb_q128"):
            e.tensor_scalar(t[nm][:, :], t[nm][:, :], t["scol"][:, 0:1],
                            None, ALU.mult)
        c.mark("prep.fold", e)
        e.tensor_scalar(t["srow"][0:1, 32:36], t["prm"][0:1, 33:37],
                        ptmp[0:1, 32:33], None, ALU.mult)
        c.mark("prep.gk", e)
        e.waitm("prep.gkmm")
        e.tensor_copy(t["scol"][:, 4:8], pprep[:, 32:36])
        for (src, dst) in ((sh_s, t["bandA"]), (shp_s, t["bandP"])):
            e.tensor_scalar(ptmp[:, 0:128], src[:, 0:128], t["scol"][:, 4:5],
                            None, ALU.mult)
            for j in range(1, 4):
                e.tensor_scalar(ptmp[:, 128:256], src[:, 128 * j:128 * (j + 1)],
                                t["scol"][:, 4 + j:5 + j], None, ALU.mult)
                e.tensor_add(ptmp[:, 0:128], ptmp[:, 0:128], ptmp[:, 128:256])
            e.tensor_copy(dst[:, :], ptmp[:, 0:128])
        c.mark("prep.bands", e)

    def body_tensor(e):
        e.waitm("prep.strips")
        e.wait_stream_all("w")
        e.matmul(pprep[:, 0:1], ptmp[0:1, 64:192], t["onesrow_f"][0:1, 0:1],
                 start=True, stop=True)
        e.matmul(pprep[:, 1:2], ptmp[0:1, 192:320], t["onesrow_f"][0:1, 0:1],
                 start=True, stop=True)
        c.mark("prep.colmm", e)
        e.matmul(pprep[:, 16:32], t["onesrow_f"][0:1, :], t["srow"][0:1, 16:32],
                 start=True, stop=True)
        c.mark("prep.lngb", e)
        e.waitm("prep.gk")
        e.matmul(pprep[:, 32:36], t["onesrow_f"][0:1, :], t["srow"][0:1, 32:36],
                 start=True, stop=True)
        c.mark("prep.gkmm", e)

    c.stage("prep", [("sync", body_sync), ("gpsimd", body_gp),
                     ("scalar", body_scalar), ("vector", body_vector),
                     ("tensor", body_tensor)])



def _ln0(c):
    nc = c.nc
    with (
        nc.sbuf_tensor("xbuf", [128, 2 * 4096], F32) as xbuf,
        nc.sbuf_tensor("xbb", [128, 4096], BF16) as xbb,
        nc.sbuf_tensor("xsq", [128, 4096], BF16) as xsq,
        nc.sbuf_tensor("xnbuf", [128, 2 * 4096], F32) as xnbuf,
        nc.sbuf_tensor("hnb", [128, 4096], BF16) as hnb,
        nc.sbuf_tensor("htmb", [128, 4096], BF16) as htmb,
        nc.sbuf_tensor("rows", [1, 10 * 512], F32) as rows,
        nc.sbuf_tensor("bcast", [128, 3 * 512], F32) as bcast,
        nc.sbuf_tensor("qk_sb", [32, 512], F32) as qk_sb,
        nc.sbuf_tensor("gtmp", [128, 9], F32) as gtmp,
        nc.psum_tensor("pstat", [1, 512], F32) as pstat,
        nc.psum_tensor("pbc", [128, 3 * 512], F32) as pbc,
        nc.psum_tensor("pqk", [32, 512], F32) as pqk,
        nc.psum_tensor("pqt", [128, 512], F32) as pqt,
        nc.psum_tensor("ptr", [128, 2 * 1024], BF16) as ptr,
    ):
        _ln0_body(c, xbuf, xbb, xsq, xnbuf, hnb, htmb, rows, bcast, qk_sb,
                  gtmp, pstat, pbc, pqk, pqt, ptr)


def _ln0_body(c, xbuf, xbb, xsq, xnbuf, hnb, htmb, rows, bcast, qk_sb, gtmp,
              pstat, pbc, pqk, pqt, ptr):
    t = c.t
    x_h = t["x_fm"].rearrange("(ki p) tok -> p ki tok", p=128)
    xn_h = t["xn_sp"].rearrange("(ki p) tok -> p ki tok", p=128)
    htm_h = t["htm_sp"].rearrange("(j p) d -> p j d", p=128)

    RW = {nm: slice(512 * i, 512 * (i + 1)) for i, nm in enumerate(
        ("m", "e2", "var", "vpe", "sd", "tmp", "q", "s2", "rs", "rrs"))}
    K = lambda nm, ci: "ln0.%s.%d" % (nm, ci)

    def body_sync(e):
        for it in range(NC512 + 1):
            if it >= 1:
                ci = it - 1
                e.waitm(K("apply", ci))
                xnb = xnbuf[:, (ci % 2) * 4096:(ci % 2 + 1) * 4096]
                tok = e.dma(xn_h[:, :, 512 * ci:512 * (ci + 1)],
                            xnb.rearrange("p (ki t) -> p ki t", ki=8),
                            stream="spx")
                c.mark_tok(K("spx", ci), tok)
                e.waitm(K("htm", ci))
                tok = e.dma(htm_h[:, 4 * ci:4 * (ci + 1), :],
                            htmb.rearrange("p (j d) -> p j d", j=4),
                            stream="sph")
                c.mark_tok(K("sph", ci), tok)
            if it < NC512:
                ci = it
                if ci >= 2:
                    e.waitm(K("apply", ci - 2))
                buf = xbuf[:, (ci % 2) * 4096:(ci % 2 + 1) * 4096]
                tok = e.dma(buf.rearrange("p (ki t) -> p ki t", ki=8),
                            x_h[:, :, 512 * ci:512 * (ci + 1)], stream="ld")
                c.mark_tok(K("ld", ci), tok)

    def body_vector(e):
        for ci in range(NC512):
            buf = xbuf[:, (ci % 2) * 4096:(ci % 2 + 1) * 4096]
            xnb = xnbuf[:, (ci % 2) * 4096:(ci % 2 + 1) * 4096]
            e.waitm(K("ld", ci))
            e.tensor_copy(xbb[:, :], buf)
            e.tensor_tensor(xsq[:, :], xbb[:, :], xbb[:, :], ALU.mult)
            c.mark(K("cast", ci), e)
            e.waitm(K("stats", ci))
            e.tensor_scalar(rows[0:1, RW["m"]], pstat[0:1, :], 1.0 / 1024.0,
                            None, ALU.mult)
            e.tensor_scalar(rows[0:1, RW["e2"]], pbc[0:1, 0:512], 1.0 / 1024.0,
                            None, ALU.mult)
            c.mark(K("statsfree", ci), e)
            e.tensor_tensor(rows[0:1, RW["tmp"]], rows[0:1, RW["m"]],
                            rows[0:1, RW["m"]], ALU.mult)
            e.tensor_tensor(rows[0:1, RW["var"]], rows[0:1, RW["e2"]],
                            rows[0:1, RW["tmp"]], ALU.subtract)
            e.tensor_scalar(rows[0:1, RW["vpe"]], rows[0:1, RW["var"]], 1e-5,
                            None, ALU.add)
            e.reciprocal(rows[0:1, RW["tmp"]], rows[0:1, RW["vpe"]])
            e.tensor_tensor(rows[0:1, RW["q"]], rows[0:1, RW["var"]],
                            rows[0:1, RW["tmp"]], ALU.mult)
            e.tensor_scalar(rows[0:1, RW["q"]], rows[0:1, RW["q"]], 1e-6,
                            None, ALU.add)
            c.mark(K("rows1", ci), e)
            e.waitm(K("sqrt", ci))
            e.reciprocal(rows[0:1, RW["sd"]], rows[0:1, RW["sd"]])
            e.tensor_tensor(rows[0:1, RW["tmp"]], rows[0:1, RW["sd"]],
                            rows[0:1, RW["sd"]], ALU.mult)
            e.tensor_tensor(rows[0:1, RW["tmp"]], rows[0:1, RW["tmp"]],
                            rows[0:1, RW["vpe"]], ALU.mult)
            e.tensor_scalar(rows[0:1, RW["tmp"]], rows[0:1, RW["tmp"]], -0.5,
                            1.5, ALU.mult, ALU.add)
            e.tensor_tensor(rows[0:1, RW["sd"]], rows[0:1, RW["sd"]],
                            rows[0:1, RW["tmp"]], ALU.mult)
            e.reciprocal(rows[0:1, RW["rs"]], rows[0:1, RW["s2"]])
            e.tensor_tensor(rows[0:1, RW["tmp"]], rows[0:1, RW["rs"]],
                            rows[0:1, RW["rs"]], ALU.mult)
            e.tensor_tensor(rows[0:1, RW["tmp"]], rows[0:1, RW["tmp"]],
                            rows[0:1, RW["q"]], ALU.mult)
            e.tensor_scalar(rows[0:1, RW["tmp"]], rows[0:1, RW["tmp"]], -0.5,
                            1.5, ALU.mult, ALU.add)
            e.tensor_tensor(rows[0:1, RW["rs"]], rows[0:1, RW["rs"]],
                            rows[0:1, RW["tmp"]], ALU.mult)
            e.tensor_tensor(rows[0:1, RW["rrs"]], rows[0:1, RW["sd"]],
                            rows[0:1, RW["rs"]], ALU.mult)
            c.mark(K("rows2", ci), e)
            e.waitm(K("bcmm", ci))
            if ci >= 2:
                e.waitm(K("spx", ci - 2))
            e.tensor_copy(bcast[:, 0:512], pbc[:, 0:512])
            e.tensor_copy(bcast[:, 512:1024], pbc[:, 512:1024])
            e.tensor_copy(bcast[:, 1024:1536], pbc[:, 1024:1536])
            c.mark(K("bcfree", ci), e)
            for ki in range(8):
                sl = slice(512 * ki, 512 * (ki + 1))
                e.tensor_tensor(buf[:, sl], buf[:, sl], bcast[:, 0:512],
                                ALU.subtract)
                e.tensor_tensor(xnb[:, sl], buf[:, sl], bcast[:, 512:1024],
                                ALU.mult)
                e.scalar_tensor_tensor(hnb[:, sl], buf[:, sl],
                                       t["n1w_s"][:, ki:ki + 1],
                                       bcast[:, 1024:1536], ALU.mult, ALU.mult)
            c.mark(K("apply", ci), e)
            e.tensor_reduce(gtmp[:, 8:9], xnb, AX.X, ALU.max,
                            apply_absolute_value=True)
            if ci == 0:
                e.tensor_copy(t["mx_acc"][:, 0:1], gtmp[:, 8:9])
            else:
                e.tensor_tensor(t["mx_acc"][:, 0:1], t["mx_acc"][:, 0:1],
                                gtmp[:, 8:9], ALU.max)
            c.mark(K("max", ci), e)
        c.mark("ln0.maxdone", e)

    def body_tensor(e):
        for ci in range(NC512):
            e.waitm(K("cast", ci))
            if ci >= 1:
                e.waitm(K("statsfree", ci - 1))
                e.waitm(K("bcfree", ci - 1))
            for ki in range(8):
                e.matmul(pstat[0:1, :], t["onescol_b"][:, 0:1],
                         xbb[:, 512 * ki:512 * (ki + 1)],
                         start=(ki == 0), stop=(ki == 7))
            for ki in range(8):
                e.matmul(pbc[0:1, 0:512], t["onescol_b"][:, 0:1],
                         xsq[:, 512 * ki:512 * (ki + 1)],
                         start=(ki == 0), stop=(ki == 7))
            c.mark(K("stats", ci), e)
            e.waitm(K("rows2", ci))
            e.matmul(pbc[:, 0:512], t["onesrow_f"][0:1, :], rows[0:1, RW["m"]],
                     start=True, stop=True)
            e.matmul(pbc[:, 512:1024], t["onesrow_f"][0:1, :],
                     rows[0:1, RW["sd"]], start=True, stop=True)
            e.matmul(pbc[:, 1024:1536], t["onesrow_f"][0:1, :],
                     rows[0:1, RW["rrs"]], start=True, stop=True)
            c.mark(K("bcmm", ci), e)
            e.waitm(K("apply", ci))
            if ci >= 1:
                e.waitm(K("qkevac", ci - 1))
            for ki in range(8):
                e.matmul(pqk[:, :], t["uv_s"][:, 32 * ki:32 * (ki + 1)],
                         hnb[:, 512 * ki:512 * (ki + 1)],
                         start=(ki == 0), stop=(ki == 7))
            c.mark(K("qkmm", ci), e)
            e.waitm(K("qkevac", ci))
            if ci >= 1:
                e.waitm(K("qktm", ci - 1))
            for j in range(4):
                e.transpose(pqt[:, 32 * j:32 * (j + 1)],
                            qk_sb[0:32, 128 * j:128 * (j + 1)],
                            t["ident_f"][0:32, 0:32])
            c.mark(K("qtr", ci), e)
            for j in range(4):
                g = 4 * ci + j
                if g >= 2:
                    gp = g - 2
                    e.waitm(K("htme.%d" % (gp % 4), gp // 4))
                for ki in range(8):
                    e.transpose(ptr[:, 1024 * (j % 2) + 128 * ki:
                                    1024 * (j % 2) + 128 * (ki + 1)],
                                hnb[:, 512 * ki + 128 * j:512 * ki + 128 * (j + 1)],
                                t["ident_b"][:, :])
                c.mark(K("htr.%d" % j, ci), e)

    def body_scalar(e):
        for ci in range(NC512):
            e.waitm(K("rows1", ci))
            e.activation(rows[0:1, RW["sd"]], rows[0:1, RW["vpe"]], AF.Ln)
            e.activation(rows[0:1, RW["sd"]], rows[0:1, RW["sd"]], AF.Exp,
                         scale=0.5)
            e.activation(rows[0:1, RW["s2"]], rows[0:1, RW["q"]], AF.Ln)
            e.activation(rows[0:1, RW["s2"]], rows[0:1, RW["s2"]], AF.Exp,
                         scale=0.5)
            c.mark(K("sqrt", ci), e)
            e.waitm(K("qkmm", ci))
            e.copy(qk_sb[0:32, :], pqk[0:32, :])
            c.mark(K("qkevac", ci), e)
            e.waitm(K("qtr", ci))
            e.copy(t["qk_tm"][:, 128 * ci:128 * (ci + 1)], pqt[:, 0:128])
            c.mark(K("qktm", ci), e)
            for j in range(4):
                e.waitm(K("htr.%d" % j, ci))
                if ci >= 1 and j == 0:
                    e.waitm(K("sph", ci - 1))
                e.copy(htmb[:, 1024 * j:1024 * (j + 1)],
                       ptr[:, 1024 * (j % 2):1024 * (j % 2 + 1)])
                c.mark(K("htme.%d" % j, ci), e)
            c.mark(K("htm", ci), e)

    c.stage("ln0", [("sync", body_sync), ("vector", body_vector),
                    ("tensor", body_tensor), ("scalar", body_scalar)])




def _decay(c):
    nc = c.nc
    with contextlib.ExitStack() as st:
        pw_sb = st.enter_context(nc.sbuf_tensor("pw_sb", [128, 8 * 1024], BF16))
        htm_t = st.enter_context(nc.sbuf_tensor("htm_t", [128, 2 * 4 * 1024], BF16))
        mixer_sb = st.enter_context(nc.sbuf_tensor("mixer_sb", [128, 16 * 1024], BF16))
        at_sb = st.enter_context(nc.sbuf_tensor("at_sb", [128, 512], BF16))
        sqb = st.enter_context(nc.sbuf_tensor("sqb", [128, 128], F32))
        nrm = st.enter_context(nc.sbuf_tensor("nrm", [128, 48], F32))
        tmq = st.enter_context(nc.sbuf_tensor("tmq", [128, 128], F32))
        tmk = st.enter_context(nc.sbuf_tensor("tmk", [128, 128], F32))
        fmq = st.enter_context(nc.sbuf_tensor("fmq", [128, 128], F32))
        fmk = st.enter_context(nc.sbuf_tensor("fmk", [128, 128], F32))
        vq_d = st.enter_context(nc.sbuf_tensor("vq_d", [128, 128], BF16))
        vq_h = st.enter_context(nc.sbuf_tensor("vq_h", [128, 128], BF16))
        vq_128 = st.enter_context(nc.sbuf_tensor("vq_128", [128, 128], BF16))
        vk_d = st.enter_context(nc.sbuf_tensor("vk_d", [128, 128], BF16))
        vk_h = st.enter_context(nc.sbuf_tensor("vk_h", [128, 128], BF16))
        vk_h32 = st.enter_context(nc.sbuf_tensor("vk_h32", [128, 128], BF16))
        vk_h64 = st.enter_context(nc.sbuf_tensor("vk_h64", [128, 128], BF16))
        vk128 = st.enter_context(nc.sbuf_tensor("vk128", [128, 512], BF16))
        zpad = st.enter_context(nc.sbuf_tensor("zpad", [128, 128], BF16))
        s_all = st.enter_context(nc.sbuf_tensor("s_all", [128, 1024], F32))
        sb16 = st.enter_context(nc.sbuf_tensor("sb16", [128, 1024], BF16))
        xnld = st.enter_context(nc.sbuf_tensor("xnld", [128, 2 * 4096], F32))
        pjbuf = st.enter_context(nc.sbuf_tensor("pjbuf", [128, 512], F32))
        h2buf = st.enter_context(nc.sbuf_tensor("h2buf", [128, 2 * 4096], F32))
        pfm = st.enter_context(nc.psum_tensor("pfm", [128, 512], F32))
        pat = st.enter_context(nc.psum_tensor("pat", [128, 512], F32))
        pmx = st.enter_context(nc.psum_tensor("pmx", [128, 2 * 1024], F32))
        pds = st.enter_context(nc.psum_tensor("pds", [128, 1024], F32))
        _decay_body(c, pw_sb, htm_t, mixer_sb, at_sb, sqb, nrm, tmq, tmk, fmq, fmk, vq_d, vq_h, vq_128, vk_d, vk_h, vk_h32, vk_h64, vk128, zpad, s_all, sb16, xnld, pjbuf, h2buf, pfm, pat, pmx, pds)



def _decay_body(c, pw_sb, htm_t, mixer_sb, at_sb, sqb, nrm, tmq, tmk, fmq,
                fmk, vq_d, vq_h, vq_128, vk_d, vk_h, vk_h32, vk_h64, vk128,
                zpad, s_all, sb16, xnld, pjbuf, h2buf, pfm, pat, pmx, pds):
    t = c.t
    htm_h = t["htm_sp"].rearrange("(j p) d -> p j d", p=128)
    xn_h = t["xn_sp"].rearrange("(mo p) tok -> p mo tok", p=128)
    h2_h = t["h2_sp"].rearrange("(mo p) tok -> p mo tok", p=128)
    mxd_h = t["mixer_dbg"].rearrange("(dc p) tok -> p dc tok", p=128)
    K = lambda nm, i: "dk.%s.%d" % (nm, i)
    VAR_OFF = {1: (vq_h, vk_h), 2: (vq_h, vk_h32), 3: (vq_h, vk_h64)}

    def qk_ap(i):
        return t["qk_tm"][:, :].rearrange("p (j o) -> p j o", o=32)[
            :, i:64:16, :]

    def body_sync(e):
        for i in range(NT128 // 4):
            e.waitm("ln0.sph.%d" % ((48 + i) // 4))
            if i >= 2:
                e.waitm(K("dsmm", i - 1))
            tok = e.dma(htm_t[:, (i % 2) * 4096:(i % 2 + 1) * 4096]
                        .rearrange("p (b d) -> p b d", b=4),
                        htm_h[:, i:64:16, :], stream="ld")
            c.mark_tok(K("ld", i), tok)
            if i % 4 == 3:
                g = i // 4
                for b in range(NB):
                    if (g, b) >= (1, 0) if False else (g >= 1 or b >= 2):
                        e.waitm(K("h2add.%d" % ((b - 2) % 4),
                                  g - (1 if b < 2 else 0)))
                    tok = e.dma(xnld[:, (b % 2) * 4096:(b % 2 + 1) * 4096]
                                .rearrange("p (mo t) -> p mo t", mo=8),
                                xn_h[:, :, 2048 * b + 512 * g:
                                     2048 * b + 512 * (g + 1)], stream="spx")
                    c.mark_tok(K("xnl.%d" % b, g), tok)
                    e.waitm(K("h2add.%d" % b, g))
                    tok = e.dma(h2_h[:, :, 2048 * b + 512 * g:
                                     2048 * b + 512 * (g + 1)],
                                h2buf[:, (b % 2) * 4096:(b % 2 + 1) * 4096]
                                .rearrange("p (mo t) -> p mo t", mo=8),
                                stream="sph")
                    c.mark_tok(K("h2s.%d" % b, g), tok)
            if "mixer_dbg" in c.debug:
                e.waitm(K("mxall", i))
                for b in range(NB):
                    e.dma(mxd_h[:, :, 2048 * b + 128 * i:2048 * b + 128 * (i + 1)],
                          mixer_sb[:, ((i % 4) * 4 + b) * 1024:
                                   ((i % 4) * 4 + b + 1) * 1024]
                          .rearrange("p (dc t) -> p dc t", dc=8), stream="out")

    def body_gp(e):
        tok = e.dma(pw_sb[:, :].rearrange("p (ki o) -> p ki o", ki=8),
                    t["projwT"].rearrange("(ki p) o -> p ki o", p=128),
                    stream="w")
        c.mark_tok("dk.pw", tok)

    def body_vector(e):
        e.memset(s_all[:, :], 0.0)
        e.memset(sb16[:, :], 0.0)
        e.memset(tmq[:, :], 0.0)
        e.memset(tmk[:, :], 0.0)
        e.memset(vk128[:, :], 0.0)
        e.memset(zpad[:, :], 0.0)
        c.mark("dk.init", e)
        for i in range(NT128 // 4):
            e.waitm("ln0.qktm.%d" % NC512 if False else "ln0.qktm.15")
            if i >= 1:
                e.waitm(K("fmtr", i - 1))
            qk = qk_ap(i)
            e.tensor_tensor(sqb[:, :], qk, qk, ALU.mult)
            e.tensor_reduce(nrm[:, 0:8],
                            sqb[:, :].rearrange("p (g o) -> p g o", o=16),
                            AX.X, ALU.add)
            c.mark(K("ss", i), e)
            e.waitm(K("sqr", i))
            e.reciprocal(nrm[:, 16:24], nrm[:, 8:16])
            e.tensor_tensor(nrm[:, 24:32], nrm[:, 0:8], nrm[:, 16:24], ALU.mult)
            e.tensor_tensor(nrm[:, 24:32], nrm[:, 24:32], nrm[:, 8:16], ALU.add)
            e.tensor_scalar(nrm[:, 24:32], nrm[:, 24:32], 0.5, 1e-8,
                            ALU.mult, ALU.max)
            e.reciprocal(nrm[:, 32:40], nrm[:, 24:32])
            for b in range(NB):
                e.tensor_scalar(tmq[:, 32 * b:32 * b + 16],
                                sqb[:, 0:1].tensor[0:0, 0:0] if False else
                                qk_ap(i)[:, b, 0:16],
                                nrm[:, 32 + 2 * b:33 + 2 * b], None, ALU.mult)
                e.tensor_scalar(tmk[:, 32 * b:32 * b + 16],
                                qk_ap(i)[:, b, 16:32],
                                nrm[:, 33 + 2 * b:34 + 2 * b], None, ALU.mult)
            c.mark(K("tmq", i), e)
            e.waitm(K("fme", i))
            e.tensor_tensor(vq_d[:, :], fmq[:, :], t["tb_qd"][:, :], ALU.mult)
            e.tensor_tensor(vq_h[:, :], fmq[:, :], t["tb_qh"][:, :], ALU.mult)
            e.tensor_tensor(vq_128[:, :], fmq[:, :], t["tb_q128"][:, :], ALU.mult)
            e.tensor_tensor(vk_d[:, :], fmk[:, :], t["tb_kd"][:, :], ALU.mult)
            e.tensor_tensor(vk_h[:, :], fmk[:, :], t["tb_kh"][:, :], ALU.mult)
            e.tensor_tensor(vk_h32[:, :], fmk[:, :], t["tb_kh32"][:, :], ALU.mult)
            e.tensor_tensor(vk_h64[:, :], fmk[:, :], t["tb_kh64"][:, :], ALU.mult)
            for b in range(NB):
                e.tensor_tensor(vk128[:, 128 * b + 32 * b:128 * b + 32 * b + 16],
                                tmk[:, 32 * b:32 * b + 16], t["tb_k128tm"][:, :],
                                ALU.mult)
            c.mark(K("var", i), e)
            e.waitm(K("atmm", i))
            for b in range(NB):
                e.scalar_tensor_tensor(at_sb[:, 128 * b:128 * (b + 1)],
                                       pat[:, 128 * b:128 * (b + 1)], 1.0,
                                       t["maskk"][:, :], ALU.mult, ALU.mult)
            c.mark(K("atev", i), e)
            e.waitm(K("dsmm", i))
            for b in range(NB):
                e.scalar_tensor_tensor(s_all[32 * b:32 * b + 16, :],
                                       s_all[32 * b:32 * b + 16, :],
                                       t["gam128"][32 * b:32 * b + 16, 0:1],
                                       pds[32 * b:32 * b + 16, :],
                                       ALU.mult, ALU.add)
            for b in range(NB):
                e.tensor_copy(sb16[32 * b:32 * b + 16, :],
                              s_all[32 * b:32 * b + 16, :])
            c.mark(K("st", i), e)
            if i % 4 == 3:
                g = i // 4
                for b in range(NB):
                    if b >= 2:
                        e.waitm(K("h2s.%d" % (b - 2), g))
                    elif g >= 1:
                        e.waitm(K("h2s.%d" % (b + 2), g - 1))
                    for mo in range(8):
                        e.waitm(K("pje.%d.%d" % (b, mo), g))
                        e.waitm(K("xnl.%d" % b, g))
                        e.tensor_tensor(
                            h2buf[:, (b % 2) * 4096 + 512 * mo:
                                  (b % 2) * 4096 + 512 * (mo + 1)],
                            pjbuf[:, :],
                            xnld[:, (b % 2) * 4096 + 512 * mo:
                                 (b % 2) * 4096 + 512 * (mo + 1)], ALU.add)
                        c.mark(K("h2a.%d.%d" % (b, mo), g), e)
                    c.mark(K("h2add.%d" % b, g), e)

    def body_tensor(e):
        for i in range(NT128 // 4):
            e.waitm(K("tmq", i))
            if i >= 1:
                e.waitm(K("fme", i - 1))
            e.transpose(pfm[:, 0:128], tmq[:, :], t["ident_f"][:, :])
            e.transpose(pfm[:, 128:256], tmk[:, :], t["ident_f"][:, :])
            c.mark(K("fmtr", i), e)
            e.waitm(K("var", i))
            if i >= 1:
                e.waitm(K("atev", i - 1))
            for b in range(NB):
                e.matmul(pat[:, 128 * b:128 * (b + 1)], t["ident_b"][:, :],
                         t["bandA"][:, :], start=(b == 0), stop=False)
            for b in range(NB):
                for ca in range(4):
                    for cb in range(ca + 1):
                        if cb == ca:
                            lh, rh = vk_d, vq_d
                        else:
                            rh, lh = VAR_OFF[ca - cb][0], VAR_OFF[ca - cb][1]
                        e.matmul(pat[32 * cb:32 * (cb + 1),
                                     128 * b + 32 * ca:128 * b + 32 * (ca + 1)],
                                 lh[32 * b:32 * b + 16, 32 * cb:32 * (cb + 1)],
                                 rh[32 * b:32 * b + 16, 32 * ca:32 * (ca + 1)],
                                 start=False, stop=False,
                                 tile_position=(32 * b, 32 * cb))
            e.matmul(pat[:, 0:512], zpad[:, :], vk128[:, :],
                     start=False, stop=True)
            c.mark(K("atmm", i), e)
            e.waitm(K("atev", i))
            e.waitm(K("ld", i))
            if i >= 1:
                e.waitm(K("st", i - 1))
            for b in range(NB):
                if b >= 2:
                    e.waitm(K("mxe.%d" % (b - 2), i))
                elif i >= 1:
                    e.waitm(K("mxe.%d" % (b + 2), i - 1))
                elif i % 4 == 0 and i >= 4:
                    pass
                pm = pmx[:, (b % 2) * 1024:(b % 2 + 1) * 1024]
                ht = htm_t[:, (i % 2) * 4096 + 1024 * b:
                           (i % 2) * 4096 + 1024 * (b + 1)]
                htp = htm_t[:, ((i - 1) % 2) * 4096 + 1024 * b:
                            ((i - 1) % 2) * 4096 + 1024 * (b + 1)]
                for dc in range(8):
                    e.matmul(pm[:, 128 * dc:128 * (dc + 1)],
                             ht[:, 128 * dc:128 * (dc + 1)],
                             at_sb[:, 128 * b:128 * (b + 1)],
                             start=(dc % 4 == 0), stop=False)
                if i >= 1:
                    for dc in range(8):
                        e.matmul(pm[:, 128 * dc:128 * (dc + 1)],
                                 htp[:, 128 * dc:128 * (dc + 1)],
                                 t["bandP"][:, :], start=False, stop=False)
                for dc in range(8):
                    e.matmul(pm[:, 128 * dc:128 * (dc + 1)],
                             sb16[32 * b:32 * b + 16, 128 * dc:128 * (dc + 1)],
                             vq_128[32 * b:32 * b + 16, :],
                             start=False, stop=(dc % 4 == 3),
                             tile_position=(32 * b, 0))
                c.mark(K("mxmm", i) + ".%d" % b, e)
            if i >= 1:
                e.waitm(K("st", i - 1))
            for b in range(NB):
                for hh in range(2):
                    e.matmul(pds[:, 512 * hh:512 * (hh + 1)],
                             vk128[:, 128 * b:128 * (b + 1)],
                             htm_t[:, (i % 2) * 4096 + 1024 * b + 512 * hh:
                                   (i % 2) * 4096 + 1024 * b + 512 * (hh + 1)],
                             start=(b == 0), stop=(b == NB - 1))
            c.mark(K("dsmm", i), e)
            if i % 4 == 3:
                g = i // 4
                e.waitm(K("mxall", i))
                e.waitm("dk.pw")
                for b in range(NB):
                    for mo in range(8):
                        if not (b == 0 and mo == 0):
                            e.waitm(K("pje.%d.%d" % ((b * 8 + mo - 2) // 8,
                                                     (b * 8 + mo - 2) % 8), g))
                        pp = pmx[:, (mo % 2) * 512:(mo % 2) * 512 + 512]
                        for ki in range(8):
                            rhs = mixer_sb[:, :].rearrange(
                                "p (s o) -> p s o", o=1024)[
                                :, b:16:4, 128 * ki:128 * (ki + 1)]
                            e.matmul(pp, pw_sb[:, 1024 * ki + 128 * mo:
                                               1024 * ki + 128 * (mo + 1)],
                                     rhs, start=(ki == 0), stop=(ki == 7))
                        c.mark(K("pj.%d.%d" % (b, mo), g), e)

    def body_scalar(e):
        for i in range(NT128 // 4):
            e.waitm(K("ss", i))
            e.activation(nrm[:, 8:16], nrm[:, 0:8], AF.Ln)
            e.activation(nrm[:, 8:16], nrm[:, 8:16], AF.Exp, scale=0.5)
            c.mark(K("sqr", i), e)
            e.waitm(K("fmtr", i))
            e.copy(fmq[:, :], pfm[:, 0:128])
            e.copy(fmk[:, :], pfm[:, 128:256])
            c.mark(K("fme", i), e)
            for b in range(NB):
                e.waitm(K("mxmm", i) + ".%d" % b)
                if i % 4 == 0 and i >= 4 and b == 0:
                    e.waitm(K("pj.3.7", i // 4 - 1))
                e.copy(mixer_sb[:, ((i % 4) * 4 + b) * 1024:
                                ((i % 4) * 4 + b + 1) * 1024],
                       pmx[:, (b % 2) * 1024:(b % 2 + 1) * 1024])
                c.mark(K("mxe.%d" % b, i), e)
            c.mark(K("mxall", i), e)
            if i % 4 == 3:
                g = i // 4
                for b in range(NB):
                    for mo in range(8):
                        e.waitm(K("pj.%d.%d" % (b, mo), g))
                        if not (b == 0 and mo == 0):
                            prev = b * 8 + mo - 1
                            e.waitm(K("h2a.%d.%d" % (prev // 8, prev % 8), g))
                        e.activation(pjbuf[:, :],
                                     pmx[:, (mo % 2) * 512:(mo % 2) * 512 + 512],
                                     AF.Identity, bias=t["projb_s"][:, mo:mo + 1])
                        c.mark(K("pje.%d.%d" % (b, mo), g), e)

    c.stage("decay", [("sync", body_sync), ("gpsimd", body_gp),
                      ("vector", body_vector), ("tensor", body_tensor),
                      ("scalar", body_scalar)])



def _mlp(c):
    nc = c.nc
    with contextlib.ExitStack() as st:
        w1_sb = st.enter_context(nc.sbuf_tensor("w1_sb", [128, 8 * 4096], BF16))
        w2_sb = st.enter_context(nc.sbuf_tensor("w2_sb", [128, 32 * 1024], BF16))
        h2ld = st.enter_context(nc.sbuf_tensor("h2ld", [128, 2 * 4096], F32))
        hbb = st.enter_context(nc.sbuf_tensor("hbb", [128, 4096], BF16))
        hid = st.enter_context(nc.sbuf_tensor("hid", [128, 32 * 512], BF16))
        mrows = st.enter_context(nc.sbuf_tensor("mrows", [1, 3 * 512], F32))
        pstat2 = st.enter_context(nc.psum_tensor("pstat2", [1, 512], F32))
        pbc2 = st.enter_context(nc.psum_tensor("pbc2", [128, 512], F32))
        pm1 = st.enter_context(nc.psum_tensor("pm1", [128, 2 * 512], F32))
        pm2 = st.enter_context(nc.psum_tensor("pm2", [128, 2 * 512], F32))
        _mlp_body(c, w1_sb, w2_sb, h2ld, hbb, hid, mrows,
                  pstat2, pbc2, pm1, pm2)


def _mlp_body(c, w1_sb, w2_sb, h2ld, hbb, hid, mrows,
              pstat2, pbc2, pm1, pm2):
    t = c.t
    h2_h = t["h2_sp"].rearrange("(mo p) tok -> p mo tok", p=128)
    s0_h = t["s0_sp"].rearrange("(mo p) tok -> p mo tok", p=128)
    K = lambda nm, ci: "mlp.%s.%d" % (nm, ci)
    RW = {"e2": slice(0, 512), "vpe": slice(0, 512), "sd": slice(512, 1024),
          "tmp": slice(1024, 1536)}

    def body_gp(e):
        tok = e.dma(w1_sb[:, :].rearrange("p (ki o) -> p ki o", ki=8),
                    t["mlpw1"].rearrange("(ki p) o -> p ki o", p=128), stream="w")
        c.mark_tok("mlp.w1", tok)
        tok = e.dma(w2_sb[:, :].rearrange("p (ki o) -> p ki o", ki=32),
                    t["mlpw2"].rearrange("(ki p) o -> p ki o", p=128), stream="w")
        c.mark_tok("mlp.w2", tok)
        for ci in range(NC512):
            buf = h2ld[:, (ci % 2) * 4096:(ci % 2 + 1) * 4096]
            e.waitm(K("ld", ci))
            if ci >= 1:
                e.waitm(K("hn2used", ci - 1))
            e.tensor_copy(hbb[:, :], buf)
            e.tensor_tensor(hbb[:, :], hbb[:, :], hbb[:, :], ALU.mult)
            c.mark(K("sq", ci), e)
            e.waitm(K("hn2", ci))

    def body_vector(e):
        for ci in range(NC512):
            e.waitm(K("stat", ci))
            e.tensor_scalar(mrows[0:1, RW["vpe"]], pstat2[0:1, :], 1.0 / 1024.0,
                            1e-6, ALU.mult, ALU.add)
            c.mark(K("statfree", ci), e)
            c.mark(K("rows1", ci), e)
            e.waitm(K("sqr", ci))
            e.reciprocal(mrows[0:1, RW["sd"]], mrows[0:1, RW["sd"]])
            e.tensor_tensor(mrows[0:1, RW["tmp"]], mrows[0:1, RW["sd"]],
                            mrows[0:1, RW["sd"]], ALU.mult)
            e.tensor_tensor(mrows[0:1, RW["tmp"]], mrows[0:1, RW["tmp"]],
                            mrows[0:1, RW["vpe"]], ALU.mult)
            e.tensor_scalar(mrows[0:1, RW["tmp"]], mrows[0:1, RW["tmp"]], -0.5,
                            1.5, ALU.mult, ALU.add)
            e.tensor_tensor(mrows[0:1, RW["sd"]], mrows[0:1, RW["sd"]],
                            mrows[0:1, RW["tmp"]], ALU.mult)
            c.mark(K("rows2", ci), e)
            e.waitm(K("bc", ci))
            e.waitm(K("ld", ci))
            if ci >= 1:
                e.waitm(K("hn2used", ci - 1))
            buf = h2ld[:, (ci % 2) * 4096:(ci % 2 + 1) * 4096]
            for ki in range(8):
                sl = slice(512 * ki, 512 * (ki + 1))
                e.scalar_tensor_tensor(hbb[:, sl], buf[:, sl],
                                       t["n2w_s"][:, ki:ki + 1], pbc2[:, :],
                                       ALU.mult, ALU.mult)
            c.mark(K("hn2", ci), e)
            c.mark(K("bce", ci), e)
            for mo in range(8):
                e.waitm(K("m2.%d" % mo, ci))
                e.scalar_tensor_tensor(buf[:, 512 * mo:512 * (mo + 1)],
                                       buf[:, 512 * mo:512 * (mo + 1)],
                                       t["mlpb2_s"][:, mo:mo + 1],
                                       pm2[:, (mo % 2) * 512:(mo % 2) * 512 + 512],
                                       ALU.add, ALU.add)
                c.mark(K("a2.%d" % mo, ci), e)
            c.mark(K("s0", ci), e)

    def body_tensor(e):
        for ci in range(NC512):
            e.waitm(K("sq", ci))
            if ci >= 1:
                e.waitm(K("statfree", ci - 1))
            for ki in range(8):
                e.matmul(pstat2[0:1, :], t["onescol_b"][:, 0:1],
                         hbb[:, 512 * ki:512 * (ki + 1)],
                         start=(ki == 0), stop=(ki == 7))
            c.mark(K("stat", ci), e)
            e.waitm(K("rows2", ci))
            if ci >= 1:
                e.waitm(K("bce", ci - 1))
            e.matmul(pbc2[:, :], t["onesrow_f"][0:1, :], mrows[0:1, RW["sd"]],
                     start=True, stop=True)
            c.mark(K("bc", ci), e)
            e.waitm(K("hn2", ci))
            e.waitm("mlp.w1")
            c.mark(K("hn2used", ci), e)
            for mo4 in range(32):
                if mo4 >= 2:
                    e.waitm(K("g1.%d" % (mo4 - 2), ci))
                elif ci >= 1 and mo4 == 0:
                    e.waitm(K("g1.%d" % 31, ci - 1))
                pp = pm1[:, (mo4 % 2) * 512:(mo4 % 2) * 512 + 512]
                for ki in range(8):
                    e.matmul(pp, w1_sb[:, 4096 * ki + 128 * mo4:
                                       4096 * ki + 128 * (mo4 + 1)],
                             hbb[:, 512 * ki:512 * (ki + 1)],
                             start=(ki == 0), stop=(ki == 7))
                c.mark(K("m1.%d" % mo4, ci), e)
            c.mark(K("hn2used", ci), e)
            e.waitm("mlp.w2")
            e.waitm(K("g1.%d" % 31, ci))
            for mo in range(8):
                if mo >= 2:
                    e.waitm(K("a2.%d" % (mo - 2), ci))
                elif ci >= 1 and mo == 0:
                    e.waitm(K("a2.%d" % 7, ci - 1))
                pp = pm2[:, (mo % 2) * 512:(mo % 2) * 512 + 512]
                for ki in range(32):
                    e.matmul(pp, w2_sb[:, 1024 * ki + 128 * mo:
                                       1024 * ki + 128 * (mo + 1)],
                             hid[:, 512 * ki:512 * (ki + 1)],
                             start=(ki == 0), stop=(ki == 31))
                c.mark(K("m2.%d" % mo, ci), e)

    def body_scalar(e):
        for ci in range(NC512):
            e.waitm(K("rows1", ci))
            e.activation(mrows[0:1, RW["sd"]], mrows[0:1, RW["vpe"]], AF.Ln)
            e.activation(mrows[0:1, RW["sd"]], mrows[0:1, RW["sd"]], AF.Exp,
                         scale=0.5)
            c.mark(K("sqr", ci), e)
            for mo4 in range(32):
                e.waitm(K("m1.%d" % mo4, ci))
                e.activation(hid[:, 512 * mo4:512 * (mo4 + 1)],
                             pm1[:, (mo4 % 2) * 512:(mo4 % 2) * 512 + 512],
                             AF.Gelu_apprx_tanh,
                             bias=t["mlpb1_s"][:, mo4:mo4 + 1])
                c.mark(K("g1.%d" % mo4, ci), e)


    def body_sync(e):
        for it in range(NC512 + 1):
            if it >= 1:
                ci = it - 1
                e.waitm(K("s0", ci))
                buf = h2ld[:, (ci % 2) * 4096:(ci % 2 + 1) * 4096]
                tok = e.dma(s0_h[:, :, 512 * ci:512 * (ci + 1)],
                            buf.rearrange("p (mo tk) -> p mo tk", mo=8),
                            stream="sph")
                c.mark_tok(K("sp", ci), tok)
            if it < NC512:
                ci = it
                if ci >= 2:
                    e.waitm(K("sp", ci - 2))
                buf = h2ld[:, (ci % 2) * 4096:(ci % 2 + 1) * 4096]
                tok = e.dma(buf.rearrange("p (mo tk) -> p mo tk", mo=8),
                            h2_h[:, :, 512 * ci:512 * (ci + 1)], stream="ld")
                c.mark_tok(K("ld", ci), tok)

    c.stage("mlp", [("gpsimd", body_gp), ("vector", body_vector),
                    ("tensor", body_tensor), ("scalar", body_scalar),
                    ("sync", body_sync)])



def _wdq(c, e_split, wnat_h, wsp_h, ot_tiles, iblocks, wld, sctmp, tag):
    pass


def _ffn_a(c):
    nc = c.nc
    with contextlib.ExitStack() as st:
        wld = st.enter_context(nc.sbuf_tensor("wld", [128, 2 * 1024], F32))
        sct = st.enter_context(nc.sbuf_tensor("sct", [128, 4 * 96], F32))
        wuT = st.enter_context(nc.sbuf_tensor("wuT", [128, 8 * 3072], BF16))
        wgT = st.enter_context(nc.sbuf_tensor("wgT", [128, 8 * 3072], BF16))
        xnl = st.enter_context(nc.sbuf_tensor("xnl", [128, 2 * 4096], F32))
        xq = st.enter_context(nc.sbuf_tensor("xq", [128, 4096], BF16))
        rawb = st.enter_context(nc.sbuf_tensor("rawb", [128, 2 * 4096], F32))
        redt = st.enter_context(nc.sbuf_tensor("redt", [128, 4], F32))
        pup = st.enter_context(nc.psum_tensor("pup", [128, 2 * 512], F32))
        _ffn_a_body(c, wld, sct, wuT, wgT, xnl, xq, rawb, redt, pup)


def _ffn_a_body(c, wld, sct, wuT, wgT, xnl, xq, rawb, redt, pup):
    t = c.t
    xn_h = t["xn_sp"].rearrange("(ki p) tok -> p ki tok", p=128)
    K = lambda nm, ci: "fa.%s.%d" % (nm, ci)
    WCFG = [("wup_nat", "wqup_sp", "upraw_sp", wuT, 24, 1),
            ("wgate_nat", "wqgate_sp", "gateraw_sp", wgT, 24, 2)]

    def body_sync(e):
        for wi, (wn, wq, _, _, nt, _) in enumerate(WCFG):
            wnat = t[wn].rearrange("(ot p) i -> p ot i", p=128)
            for ot in range(nt):
                g = wi * 24 + ot
                if g >= 2:
                    e.waitm(K("dqs", g - 2))
                tok = e.dma(wld[:, (g % 2) * 1024:(g % 2 + 1) * 1024],
                            wnat[:, ot, :], stream="ld")
                c.mark_tok(K("wnat", g), tok)
        for wi, (wn, wq, _, wT, nt, _) in enumerate(WCFG):
            for ki in range(8):
                e.waitm(K("dqs", wi * 24 + nt - 1))
                e.waitm(K("dqs", wi * 24 + nt - 2))
                tok = e.dma_t(wT[:, 3072 * ki:3072 * (ki + 1)],
                              t[wq][:, 128 * ki:128 * (ki + 1)])
                c.mark_tok(K("wT.%d" % wi, ki), tok)
        for it in range(NC512 + 1):
            if it >= 1:
                ci = it - 1
                for half in range(6):
                    e.waitm(K("ev.%d" % half, ci))
                    wq = half // 3
                    dst = t[WCFG[wq][2]].rearrange("(mo p) tok -> p mo tok", p=128)
                    tok = e.dma(dst[:, 8 * (half % 3):8 * (half % 3 + 1),
                                    512 * ci:512 * (ci + 1)],
                                rawb[:, (half % 2) * 4096:(half % 2 + 1) * 4096]
                                .rearrange("p (mo tk) -> p mo tk", mo=8),
                                stream="sph" if wq else "spx")
                    c.mark_tok(K("sp.%d" % half, ci), tok)
            if it < NC512:
                ci = it
                if ci >= 2:
                    e.waitm(K("xq", ci - 2))
                tok = e.dma(xnl[:, (ci % 2) * 4096:(ci % 2 + 1) * 4096]
                            .rearrange("p (ki tk) -> p ki tk", ki=8),
                            xn_h[:, :, 512 * ci:512 * (ci + 1)], stream="ld")
                c.mark_tok(K("xn", ci), tok)

    def body_vector(e):
        e.memset(t["mx_acc"][:, 1:5], 0.0)
        for wi, (wn, wq, _, _, nt, _) in enumerate(WCFG):
            for ot in range(nt):
                g = wi * 24 + ot
                buf = wld[:, (g % 2) * 1024:(g % 2 + 1) * 1024]
                e.waitm(K("wnat", g))
                e.tensor_reduce(sct[:, 0:32],
                                buf.rearrange("p (b q) -> p b q", q=32), AX.X,
                                ALU.max, apply_absolute_value=True)
                e.tensor_scalar(sct[:, 96:128], sct[:, 0:32], 1.0 / 7.0, 1e-12,
                                ALU.mult, ALU.max)
                e.reciprocal(sct[:, 192:224], sct[:, 96:128])
                b3 = buf.rearrange("p (b q) -> p b q", q=32)
                e.tensor_tensor(b3, b3,
                                sct[:, 192:224].rearrange("p (b o) -> p b o", o=1)
                                .broadcast_to([128, 32, 32]), ALU.mult)
                e.tensor_scalar(buf, buf, RBIG, RBIG, ALU.add, ALU.subtract)
                e.tensor_tensor(b3, b3,
                                sct[:, 96:128].rearrange("p (b o) -> p b o", o=1)
                                .broadcast_to([128, 32, 32]), ALU.mult)
                c.mark(K("dq", g), e)
        for ci in range(NC512):
            for half in range(6):
                e.waitm(K("ev.%d" % half, ci))
                e.tensor_reduce(redt[:, 0:1],
                                rawb[:, (half % 2) * 4096:(half % 2 + 1) * 4096],
                                AX.X, ALU.max, apply_absolute_value=True)
                e.tensor_tensor(t["mx_acc"][:, 1 + half // 3:2 + half // 3],
                                t["mx_acc"][:, 1 + half // 3:2 + half // 3],
                                redt[:, 0:1], ALU.max)
                c.mark(K("red.%d" % half, ci), e)

    def body_gp(e):
        for wi, (wn, wq, _, _, nt, _) in enumerate(WCFG):
            wqh = t[wq].rearrange("(ot p) i -> p ot i", p=128)
            for ot in range(nt):
                g = wi * 24 + ot
                e.waitm(K("dq", g))
                tok = e.dma(wqh[:, ot, :],
                            wld[:, (g % 2) * 1024:(g % 2 + 1) * 1024],
                            stream="w")
                c.mark_tok(K("dqs", g), tok)
        for ci in range(NC512):
            e.waitm(K("xn", ci))
            e.waitm("cs0.done")
            if ci >= 1:
                e.waitm(K("mm.5", ci - 1))
            buf = xnl[:, (ci % 2) * 4096:(ci % 2 + 1) * 4096]
            e.tensor_scalar(buf, buf, t["scol"][:, 1:2], RBIG, ALU.mult, ALU.add)
            e.tensor_scalar(xq[:, :], buf, RBIG, t["scol"][:, 0:1],
                            ALU.subtract, ALU.mult)
            c.mark(K("xq", ci), e)

    def body_scalar(e):
        for ci in range(NC512):
            for half in range(6):
                for mo8 in range(8):
                    e.waitm(K("mm.%d.%d" % (half, mo8), ci))
                    if (ci, half) not in ((0, 0), (0, 1)):
                        pc, ph = (ci, half - 2) if half >= 2 else (ci - 1, half + 4)
                        e.waitm(K("sp.%d" % ph, pc))
                        e.waitm(K("red.%d" % ph, pc))
                    pp = pup[:, ((half * 8 + mo8) % 2) * 512:
                             ((half * 8 + mo8) % 2 + 1) * 512]
                    e.copy(rawb[:, (half % 2) * 4096 + 512 * mo8:
                                (half % 2) * 4096 + 512 * (mo8 + 1)], pp)
                    c.mark(K("evt.%d.%d" % (half, mo8), ci), e)
                c.mark(K("ev.%d" % half, ci), e)

    def body_tensor(e):
        for ci in range(NC512):
            e.waitm(K("xq", ci))
            for half in range(6):
                wq = half // 3
                wT = WCFG[wq][3]
                for ki in range(8):
                    e.waitm(K("wT.%d" % wq, ki))
                for mo8 in range(8):
                    mo = (half % 3) * 8 + mo8
                    pp = pup[:, ((half * 8 + mo8) % 2) * 512:
                             ((half * 8 + mo8) % 2 + 1) * 512]
                    if (ci, half, mo8) != (0, 0, 0):
                        ptile = half * 8 + mo8 - 2
                        pci = ci
                        if ptile < 0:
                            ptile += 48
                            pci -= 1
                        if pci >= 0:
                            e.waitm(K("evt.%d.%d" % (ptile // 8, ptile % 8), pci))
                    for ki in range(8):
                        e.matmul(pp, wT[:, 3072 * ki + 128 * mo:
                                        3072 * ki + 128 * (mo + 1)],
                                 xq[:, 512 * ki:512 * (ki + 1)],
                                 start=(ki == 0), stop=(ki == 7))
                    c.mark(K("mm.%d.%d" % (half, mo8), ci), e)
                c.mark(K("mm.%d" % half, ci), e)

    c.stage("ffa", [("sync", body_sync), ("gpsimd", body_gp),
                    ("vector", body_vector), ("tensor", body_tensor),
                    ("scalar", body_scalar)])


def _coll_scales(c, slots, mark_in, name):
    t = c.t
    with c.nc.psum_tensor("pc_" + name, [128, 512], F32) as pc:
        P = lambda nm: "%s.%s" % (name, nm)

        def body_tensor(e):
            e.waitm(mark_in)
            for j, (mxc, _, _, _) in enumerate(slots):
                e.transpose(pc[0:1, 128 * j:128 * j + 128],
                            t["mx_acc"][:, mxc:mxc + 1], t["ident_f"][:, :])
            c.mark(P("tr"), e)
            e.waitm(P("val"))
            for j, (mxc, sc_col, inv_col, _) in enumerate(slots):
                e.matmul(pc[:, 256 + 2 * j:256 + 2 * j + 1],
                         t["onesrow_f"][0:1, :], t["ccv"][0:1, 8 + mxc:9 + mxc],
                         start=True, stop=True)
                e.matmul(pc[:, 257 + 2 * j:258 + 2 * j],
                         t["onesrow_f"][0:1, :], t["ccv"][0:1, 16 + mxc:17 + mxc],
                         start=True, stop=True)
            c.mark(P("bc"), e)

        def body_gp(e):
            e.waitm(P("red"))
            tok = e.dma(t["cc_in"][0:1, :], t["ccv"][0:1, 0:16], stream="w")
            e.wait(tok[0], tok[1])
            ctok = e.collective_max(t["cc_in"].opt(), t["cc_out"].opt())
            e.wait(ctok[0], ctok[1])
            tok = e.dma(t["ccv"][0:1, 0:16], t["cc_out"][0:1, :], stream="w")
            c.mark_tok(P("ccout"), tok)

        def body_vector(e):
            for j, (mxc, _, _, _) in enumerate(slots):
                e.waitm(P("tr"))
                e.tensor_reduce(t["ccv"][0:1, mxc:mxc + 1],
                                pc[0:1, 128 * j:128 * j + 128], AX.X, ALU.max)
            c.mark(P("red"), e)
            e.waitm(P("ccout"))
            for j, (mxc, _, _, xdiv) in enumerate(slots):
                e.tensor_scalar(t["ccv"][0:1, 8 + mxc:9 + mxc],
                                t["ccv"][0:1, mxc:mxc + 1],
                                1.0 / (QMAX * xdiv), 1e-12, ALU.mult, ALU.max)
                e.tensor_scalar(t["ccv"][0:1, 16 + mxc:17 + mxc],
                                t["ccv"][0:1, 8 + mxc:9 + mxc], float(xdiv),
                                None, ALU.mult)
                e.reciprocal(t["ccv"][0:1, 16 + mxc:17 + mxc],
                             t["ccv"][0:1, 16 + mxc:17 + mxc])
            c.mark(P("val"), e)
            e.waitm(P("bc"))
            for j, (mxc, sc_col, inv_col, _) in enumerate(slots):
                e.tensor_copy(t["scol"][:, sc_col:sc_col + 1],
                              pc[:, 256 + 2 * j:257 + 2 * j])
                e.tensor_copy(t["scol"][:, inv_col:inv_col + 1],
                              pc[:, 257 + 2 * j:258 + 2 * j])
            c.mark(P("done"), e)

        c.stage(name, [("tensor", body_tensor), ("gpsimd", body_gp),
                       ("vector", body_vector)])


def _ffn_bc(c, src_names, dst_name, mx_col, stage_name, ops):
    pass


def _ffn_b(c):
    nc = c.nc
    with contextlib.ExitStack() as st:
        gld = st.enter_context(nc.sbuf_tensor("gld", [128, 2 * 4096], F32))
        gfb = st.enter_context(nc.sbuf_tensor("gfb", [128, 4096], F32))
        sgb = st.enter_context(nc.sbuf_tensor("sgb", [128, 4096], F32))
        tsb = st.enter_context(nc.sbuf_tensor("tsb", [128, 2 * 4096], F32))
        redb = st.enter_context(nc.sbuf_tensor("redb", [128, 4], F32))
        _ffn_b_body(c, gld, gfb, sgb, tsb, redb)


def _ffn_b_body(c, gld, gfb, sgb, tsb, redb):
    t = c.t
    gr_h = t["gateraw_sp"].rearrange("(mo p) tok -> p mo tok", p=128)
    t_h = t["t_sp"].rearrange("(mo p) tok -> p mo tok", p=128)
    K = lambda nm, g: "fb.%s.%d" % (nm, g)
    NG = NC512 * 3

    def body_sync(e):
        for it in range(NG + 1):
            if it >= 1:
                g = it - 1
                e.waitm(K("t", g))
                tok = e.dma(t_h[:, 8 * (g % 3):8 * (g % 3 + 1),
                                512 * (g // 3):512 * (g // 3 + 1)],
                            tsb[:, (g % 2) * 4096:(g % 2 + 1) * 4096]
                            .rearrange("p (mo tk) -> p mo tk", mo=8),
                            stream="sph")
                c.mark_tok(K("sp", g), tok)
            if it < NG:
                g = it
                if g >= 2:
                    e.waitm(K("q", g - 2))
                tok = e.dma(gld[:, (g % 2) * 4096:(g % 2 + 1) * 4096]
                            .rearrange("p (mo tk) -> p mo tk", mo=8),
                            gr_h[:, 8 * (g % 3):8 * (g % 3 + 1),
                                 512 * (g // 3):512 * (g // 3 + 1)],
                            stream="ld")
                c.mark_tok(K("ld", g), tok)

    def body_gp(e):
        for g in range(NG):
            e.waitm(K("sig", g))
            e.tensor_scalar(sgb[:, :], sgb[:, :], 65536.0, RBIG, ALU.mult, ALU.add)
            e.tensor_scalar(sgb[:, :], sgb[:, :], RBIG, 65535.0,
                            ALU.subtract, ALU.min)
            c.mark(K("si", g), e)

    def body_scalar(e):
        for g in range(NG):
            e.waitm(K("q", g))
            e.activation(sgb[:, :], gfb[:, :], AF.Sigmoid)
            c.mark(K("sig", g), e)

    def body_vector(e):
        for g in range(NG):
            e.waitm(K("ld", g))
            e.waitm("cs1.done")
            buf = gld[:, (g % 2) * 4096:(g % 2 + 1) * 4096]
            e.tensor_scalar(gfb[:, :], buf, t["scol"][:, 5:6], RBIG,
                            ALU.mult, ALU.add)
            e.tensor_scalar(gfb[:, :], gfb[:, :], RBIG, t["scol"][:, 4:5],
                            ALU.subtract, ALU.mult)
            c.mark(K("q", g), e)
            e.waitm(K("si", g))
            if g >= 2:
                e.waitm(K("sp", g - 2))
            e.tensor_tensor(tsb[:, (g % 2) * 4096:(g % 2 + 1) * 4096],
                            gfb[:, :], sgb[:, :], ALU.mult)
            e.tensor_reduce(redb[:, 0:1],
                            tsb[:, (g % 2) * 4096:(g % 2 + 1) * 4096],
                            AX.X, ALU.max, apply_absolute_value=True)
            e.tensor_tensor(t["mx_acc"][:, 3:4], t["mx_acc"][:, 3:4],
                            redb[:, 0:1], ALU.max)
            c.mark(K("t", g), e)

    c.stage("ffb", [("sync", body_sync), ("gpsimd", body_gp),
                    ("scalar", body_scalar), ("vector", body_vector)])


def _ffn_c(c):
    nc = c.nc
    with contextlib.ExitStack() as st:
        tld = st.enter_context(nc.sbuf_tensor("tld", [128, 2 * 4096], F32))
        uld = st.enter_context(nc.sbuf_tensor("uld", [128, 2 * 4096], F32))
        g2b = st.enter_context(nc.sbuf_tensor("g2b", [128, 4096], F32))
        zbb = st.enter_context(nc.sbuf_tensor("zbb", [128, 2 * 4096], F32))
        redc = st.enter_context(nc.sbuf_tensor("redc", [128, 4], F32))
        _ffn_c_body(c, tld, uld, g2b, zbb, redc)


def _ffn_c_body(c, tld, uld, g2b, zbb, redc):
    t = c.t
    t_h = t["t_sp"].rearrange("(mo p) tok -> p mo tok", p=128)
    u_h = t["upraw_sp"].rearrange("(mo p) tok -> p mo tok", p=128)
    z_h = t["z_sp"].rearrange("(mo p) tok -> p mo tok", p=128)
    K = lambda nm, g: "fc.%s.%d" % (nm, g)
    NG = NC512 * 3

    def body_sync(e):
        for it in range(NG + 1):
            if it >= 1:
                g = it - 1
                e.waitm(K("z", g))
                tok = e.dma(z_h[:, 8 * (g % 3):8 * (g % 3 + 1),
                                512 * (g // 3):512 * (g // 3 + 1)],
                            zbb[:, (g % 2) * 4096:(g % 2 + 1) * 4096]
                            .rearrange("p (mo tk) -> p mo tk", mo=8),
                            stream="sph")
                c.mark_tok(K("sp", g), tok)
            if it < NG:
                g = it
                if g >= 2:
                    e.waitm(K("g2", g - 2))
                tok = e.dma(tld[:, (g % 2) * 4096:(g % 2 + 1) * 4096]
                            .rearrange("p (mo tk) -> p mo tk", mo=8),
                            t_h[:, 8 * (g % 3):8 * (g % 3 + 1),
                                512 * (g // 3):512 * (g // 3 + 1)], stream="ld")
                c.mark_tok(K("ldt", g), tok)
                if g >= 2:
                    e.waitm(K("z", g - 2))
                tok = e.dma(uld[:, (g % 2) * 4096:(g % 2 + 1) * 4096]
                            .rearrange("p (mo tk) -> p mo tk", mo=8),
                            u_h[:, 8 * (g % 3):8 * (g % 3 + 1),
                                512 * (g // 3):512 * (g // 3 + 1)], stream="spx")
                c.mark_tok(K("ldu", g), tok)

    def body_gp(e):
        for g in range(NG):
            e.waitm(K("ldu", g))
            e.waitm("cs1.done")
            buf = uld[:, (g % 2) * 4096:(g % 2 + 1) * 4096]
            e.tensor_scalar(buf, buf, t["scol"][:, 3:4], RBIG, ALU.mult, ALU.add)
            e.tensor_scalar(buf, buf, RBIG, t["scol"][:, 2:3],
                            ALU.subtract, ALU.mult)
            c.mark(K("u", g), e)

    def body_vector(e):
        for g in range(NG):
            e.waitm(K("ldt", g))
            e.waitm("cs2.done")
            e.tensor_scalar(g2b[:, :], tld[:, (g % 2) * 4096:(g % 2 + 1) * 4096],
                            t["scol"][:, 7:8], RBIG, ALU.mult, ALU.add)
            e.tensor_scalar(g2b[:, :], g2b[:, :], RBIG, t["scol"][:, 6:7],
                            ALU.subtract, ALU.mult)
            c.mark(K("g2", g), e)
            e.waitm(K("u", g))
            if g >= 2:
                e.waitm(K("sp", g - 2))
            e.tensor_tensor(zbb[:, (g % 2) * 4096:(g % 2 + 1) * 4096], g2b[:, :],
                            uld[:, (g % 2) * 4096:(g % 2 + 1) * 4096], ALU.mult)
            e.tensor_reduce(redc[:, 0:1],
                            zbb[:, (g % 2) * 4096:(g % 2 + 1) * 4096],
                            AX.X, ALU.max, apply_absolute_value=True)
            e.tensor_tensor(t["mx_acc"][:, 4:5], t["mx_acc"][:, 4:5],
                            redc[:, 0:1], ALU.max)
            c.mark(K("z", g), e)

    c.stage("ffc", [("sync", body_sync), ("gpsimd", body_gp),
                    ("vector", body_vector)])


def _ffn_d(c):
    nc = c.nc
    with contextlib.ExitStack() as st:
        wdld = st.enter_context(nc.sbuf_tensor("wdld", [128, 2 * 3072], F32))
        scd = st.enter_context(nc.sbuf_tensor("scd", [128, 4 * 96], F32))
        wdT = st.enter_context(nc.sbuf_tensor("wdT", [128, 24 * 1024], BF16))
        zld = st.enter_context(nc.sbuf_tensor("zld", [128, 2 * 4096], F32))
        ob = st.enter_context(nc.sbuf_tensor("ob", [128, 3 * 4096], BF16))
        g0b = st.enter_context(nc.sbuf_tensor("g0b", [128, 2 * 4096], F32))
        pdn = st.enter_context(nc.psum_tensor("pdn", [128, 2 * 512], F32))
        _ffn_d_body(c, wdld, scd, wdT, zld, ob, g0b, pdn)


def _ffn_d_body(c, wdld, scd, wdT, zld, ob, g0b, pdn):
    t = c.t
    z_h = t["z_sp"].rearrange("(mo p) tok -> p mo tok", p=128)
    g0_h = t["g0_sp"].rearrange("(mo p) tok -> p mo tok", p=128)
    wn_h = t["wdown_nat"].rearrange("(ot p) i -> p ot i", p=128)
    wq_h = t["wqdown_sp"].rearrange("(ot p) i -> p ot i", p=128)
    K = lambda nm, g: "fd.%s.%d" % (nm, g)

    def body_sync(e):
        for ot in range(8):
            if ot >= 2:
                e.waitm(K("dqs", ot - 2))
            tok = e.dma(wdld[:, (ot % 2) * 3072:(ot % 2 + 1) * 3072],
                        wn_h[:, ot, :], stream="ld")
            c.mark_tok(K("wnat", ot), tok)
        for ki in range(24):
            e.waitm(K("dqs", 7))
            e.waitm(K("dqs", 6))
            tok = e.dma_t(wdT[:, 1024 * ki:1024 * (ki + 1)],
                          t["wqdown_sp"][:, 128 * ki:128 * (ki + 1)])
            c.mark_tok(K("wT", ki), tok)
        for it in range(NC512 + 1):
            if it >= 1:
                ci = it - 1
                e.waitm(K("g0", ci))
                tok = e.dma(g0_h[:, :, 512 * ci:512 * (ci + 1)],
                            g0b[:, (ci % 2) * 4096:(ci % 2 + 1) * 4096]
                            .rearrange("p (mo tk) -> p mo tk", mo=8),
                            stream="sph")
                c.mark_tok(K("sp", ci), tok)
            if it < NC512:
                ci = it
                for half in range(3):
                    g = 3 * ci + half
                    if g >= 2:
                        e.waitm(K("oq.%d" % ((g - 2) % 3), (g - 2) // 3))
                    tok = e.dma(zld[:, (g % 2) * 4096:(g % 2 + 1) * 4096]
                                .rearrange("p (mo tk) -> p mo tk", mo=8),
                                z_h[:, 8 * half:8 * (half + 1),
                                    512 * ci:512 * (ci + 1)], stream="ld")
                    c.mark_tok(K("ldz.%d" % half, ci), tok)

    def body_gp(e):
        for ot in range(8):
            e.waitm(K("dq", ot))
            buf = wdld[:, (ot % 2) * 3072:(ot % 2 + 1) * 3072]
            tok = e.dma(wq_h[:, ot, :], buf, stream="w")
            c.mark_tok(K("dqs", ot), tok)

    def body_vector(e):
        for ot in range(8):
            buf = wdld[:, (ot % 2) * 3072:(ot % 2 + 1) * 3072]
            e.waitm(K("wnat", ot))
            if ot >= 2:
                e.waitm(K("dqs", ot - 2))
            e.tensor_reduce(scd[:, 0:96],
                            buf.rearrange("p (b q) -> p b q", q=32), AX.X,
                            ALU.max, apply_absolute_value=True)
            e.tensor_scalar(scd[:, 96:192], scd[:, 0:96], 1.0 / 7.0, 1e-12,
                            ALU.mult, ALU.max)
            e.reciprocal(scd[:, 192:288], scd[:, 96:192])
            e.tensor_tensor(buf.rearrange("p (b q) -> p b q", q=32),
                            buf.rearrange("p (b q) -> p b q", q=32),
                            scd[:, 192:288].rearrange("p (b o) -> p b o", o=1)
                            .broadcast_to([128, 96, 32]), ALU.mult)
            e.tensor_scalar(buf, buf, RBIG, RBIG, ALU.add, ALU.subtract)
            e.tensor_tensor(buf.rearrange("p (b q) -> p b q", q=32),
                            buf.rearrange("p (b q) -> p b q", q=32),
                            scd[:, 96:192].rearrange("p (b o) -> p b o", o=1)
                            .broadcast_to([128, 96, 32]), ALU.mult)
            c.mark(K("dq", ot), e)
        for ci in range(NC512):
            for half in range(3):
                g = 3 * ci + half
                e.waitm(K("ldz.%d" % half, ci))
                e.waitm("cs3.done")
                buf = zld[:, (g % 2) * 4096:(g % 2 + 1) * 4096]
                e.tensor_scalar(buf, buf, t["scol"][:, 9:10], RBIG,
                                ALU.mult, ALU.add)
                e.tensor_scalar(ob[:, half * 4096:(half + 1) * 4096], buf,
                                RBIG, t["scol"][:, 8:9], ALU.subtract, ALU.mult)
                c.mark(K("oq.%d" % half, ci), e)

    def body_tensor(e):
        for ki in range(24):
            e.waitm(K("wT", ki))
        for ci in range(NC512):
            for half in range(3):
                e.waitm(K("oq.%d" % half, ci))
            for mo in range(8):
                if (ci, mo) != (0, 0):
                    pm = mo - 2
                    pci = ci
                    if pm < 0:
                        pm += 8
                        pci -= 1
                    if pci >= 0:
                        e.waitm(K("g0e.%d" % pm, pci))
                pp = pdn[:, (mo % 2) * 512:(mo % 2 + 1) * 512]
                for ki in range(24):
                    e.matmul(pp, wdT[:, 1024 * ki + 128 * mo:
                                     1024 * ki + 128 * (mo + 1)],
                             ob[:, 512 * ki:512 * (ki + 1)],
                             start=(ki == 0), stop=(ki == 23))
                c.mark(K("mm.%d" % mo, ci), e)

    def body_scalar(e):
        for ci in range(NC512):
            for mo in range(8):
                e.waitm(K("mm.%d" % mo, ci))
                if ci >= 2 and mo == 0:
                    e.waitm(K("sp", ci - 2))
                e.activation(g0b[:, (ci % 2) * 4096 + 512 * mo:
                                 (ci % 2) * 4096 + 512 * (mo + 1)],
                             pdn[:, (mo % 2) * 512:(mo % 2 + 1) * 512], AF.Silu)
                c.mark(K("g0e.%d" % mo, ci), e)
            c.mark(K("g0", ci), e)

    c.stage("ffd", [("sync", body_sync), ("gpsimd", body_gp),
                    ("vector", body_vector), ("tensor", body_tensor),
                    ("scalar", body_scalar)])



def _gate_ln1(c):
    nc = c.nc
    with contextlib.ExitStack() as st:
        xld = st.enter_context(nc.sbuf_tensor("xld9", [128, 2 * 4096], F32))
        sld = st.enter_context(nc.sbuf_tensor("sld9", [128, 2 * 4096], F32))
        gld = st.enter_context(nc.sbuf_tensor("gld9", [128, 2 * 4096], F32))
        dbf = st.enter_context(nc.sbuf_tensor("dbf9", [128, 2 * 4096], F32))
        xbb = st.enter_context(nc.sbuf_tensor("xbb9", [128, 4096], BF16))
        xsq = st.enter_context(nc.sbuf_tensor("xsq9", [128, 4096], BF16))
        xn1b = st.enter_context(nc.sbuf_tensor("xn1b", [128, 2 * 4096], BF16))
        rws = st.enter_context(nc.sbuf_tensor("rws9", [1, 4 * 512], F32))
        bcs = st.enter_context(nc.sbuf_tensor("bcs9", [128, 2 * 512], F32))
        pst = st.enter_context(nc.psum_tensor("pst9", [1, 512], F32))
        psq = st.enter_context(nc.psum_tensor("psq9", [1, 512], F32))
        pbc = st.enter_context(nc.psum_tensor("pbc9", [128, 2 * 512], F32))
        _gate_ln1_body(c, xld, sld, gld, dbf, xbb, xsq, xn1b, rws, bcs,
                       pst, psq, pbc)


def _gate_ln1_body(c, xld, sld, gld, dbf, xbb, xsq, xn1b, rws, bcs,
                   pst, psq, pbc):
    t = c.t
    x_h = t["x_fm"].rearrange("(ki p) tok -> p ki tok", p=128)
    s_h = t["s0_sp"].rearrange("(ki p) tok -> p ki tok", p=128)
    g_h = t["g0_sp"].rearrange("(ki p) tok -> p ki tok", p=128)
    x1_h = t["x1_sp"].rearrange("(ki p) tok -> p ki tok", p=128)
    xn1_h = t["xn1_sp"].rearrange("(ki p) tok -> p ki tok", p=128)
    xf_h = t["xn1f_sp"].rearrange("(ki p) tok -> p ki tok", p=128)
    K = lambda nm, ci: "g9.%s.%d" % (nm, ci)
    RW = {"m": slice(0, 512), "e2": slice(512, 1024), "sd": slice(1024, 1536),
          "tmp": slice(1536, 2048)}

    def body_sync(e):
        for it in range(NC512 + 1):
            if it >= 1:
                ci = it - 1
                e.waitm(K("x1", ci))
                tok = e.dma(x1_h[:, :, 512 * ci:512 * (ci + 1)],
                            xld[:, (ci % 2) * 4096:(ci % 2 + 1) * 4096]
                            .rearrange("p (ki tk) -> p ki tk", ki=8),
                            stream="sph")
                c.mark_tok(K("spx1", ci), tok)
                e.waitm(K("xn1", ci))
                tok = e.dma(xn1_h[:, :, 512 * ci:512 * (ci + 1)],
                            xn1b[:, (ci % 2) * 4096:(ci % 2 + 1) * 4096]
                            .rearrange("p (ki tk) -> p ki tk", ki=8),
                            stream="spx")
                c.mark_tok(K("spn", ci), tok)
                e.waitm(K("xf", ci))
                tok = e.dma(xf_h[:, :, 512 * ci:512 * (ci + 1)],
                            dbf[:, (ci % 2) * 4096:(ci % 2 + 1) * 4096]
                            .rearrange("p (ki tk) -> p ki tk", ki=8),
                            stream="sph")
                c.mark_tok(K("spf", ci), tok)
            if it < NC512:
                ci = it
                if ci >= 2:
                    e.waitm(K("apl", ci - 2))
                    e.waitm(K("spx1", ci - 2))
                tok = e.dma(xld[:, (ci % 2) * 4096:(ci % 2 + 1) * 4096]
                            .rearrange("p (ki tk) -> p ki tk", ki=8),
                            x_h[:, :, 512 * ci:512 * (ci + 1)], stream="ld")
                c.mark_tok(K("ldx", ci), tok)
                if ci >= 2:
                    e.waitm(K("sg", ci - 2))
                    e.waitm(K("x1", ci - 2))
                tok = e.dma(sld[:, (ci % 2) * 4096:(ci % 2 + 1) * 4096]
                            .rearrange("p (ki tk) -> p ki tk", ki=8),
                            s_h[:, :, 512 * ci:512 * (ci + 1)], stream="ld")
                c.mark_tok(K("lds", ci), tok)
                tok = e.dma(gld[:, (ci % 2) * 4096:(ci % 2 + 1) * 4096]
                            .rearrange("p (ki tk) -> p ki tk", ki=8),
                            g_h[:, :, 512 * ci:512 * (ci + 1)], stream="ld")
                c.mark_tok(K("ldg", ci), tok)

    def body_gp(e):
        for ci in range(NC512):
            e.waitm(K("lds", ci))
            e.waitm(K("ldg", ci))
            buf = sld[:, (ci % 2) * 4096:(ci % 2 + 1) * 4096]
            e.tensor_tensor(buf, buf,
                            gld[:, (ci % 2) * 4096:(ci % 2 + 1) * 4096], ALU.mult)
            c.mark(K("sg", ci), e)

    def body_vector(e):
        for ci in range(NC512):
            e.waitm(K("ldx", ci))
            e.waitm(K("sg", ci))
            if ci >= 2:
                e.waitm(K("spx1", ci - 2))
                e.waitm(K("spn", ci - 2))
            buf = xld[:, (ci % 2) * 4096:(ci % 2 + 1) * 4096]
            e.tensor_tensor(buf, buf,
                            sld[:, (ci % 2) * 4096:(ci % 2 + 1) * 4096], ALU.add)
            c.mark(K("x1", ci), e)
            e.tensor_copy(xbb[:, :], buf)
            e.tensor_tensor(xsq[:, :], xbb[:, :], xbb[:, :], ALU.mult)
            c.mark(K("cast", ci), e)
            e.waitm(K("stat", ci))
            e.tensor_scalar(rws[0:1, RW["m"]], pst[0:1, :], 1.0 / 1024.0,
                            None, ALU.mult)
            e.tensor_scalar(rws[0:1, RW["e2"]], psq[0:1, :], 1.0 / 1024.0,
                            None, ALU.mult)
            c.mark(K("statfree", ci), e)
            e.tensor_tensor(rws[0:1, RW["tmp"]], rws[0:1, RW["m"]],
                            rws[0:1, RW["m"]], ALU.mult)
            e.tensor_tensor(rws[0:1, RW["e2"]], rws[0:1, RW["e2"]],
                            rws[0:1, RW["tmp"]], ALU.subtract)
            e.tensor_scalar(rws[0:1, RW["e2"]], rws[0:1, RW["e2"]], 1e-5,
                            None, ALU.add)
            c.mark(K("rows1", ci), e)
            e.waitm(K("sqr", ci))
            e.reciprocal(rws[0:1, RW["sd"]], rws[0:1, RW["sd"]])
            e.tensor_tensor(rws[0:1, RW["tmp"]], rws[0:1, RW["sd"]],
                            rws[0:1, RW["sd"]], ALU.mult)
            e.tensor_tensor(rws[0:1, RW["tmp"]], rws[0:1, RW["tmp"]],
                            rws[0:1, RW["e2"]], ALU.mult)
            e.tensor_scalar(rws[0:1, RW["tmp"]], rws[0:1, RW["tmp"]], -0.5,
                            1.5, ALU.mult, ALU.add)
            e.tensor_tensor(rws[0:1, RW["sd"]], rws[0:1, RW["sd"]],
                            rws[0:1, RW["tmp"]], ALU.mult)
            c.mark(K("rows2", ci), e)
            e.waitm(K("bc", ci))
            if ci >= 2:
                e.waitm(K("spf", ci - 2))
            nb = xn1b[:, (ci % 2) * 4096:(ci % 2 + 1) * 4096]
            db = dbf[:, (ci % 2) * 4096:(ci % 2 + 1) * 4096]
            for ki in range(8):
                sl = slice(512 * ki, 512 * (ki + 1))
                e.tensor_tensor(db[:, sl], buf[:, sl], pbc[:, 0:512],
                                ALU.subtract)
                e.tensor_tensor(nb[:, sl], db[:, sl], pbc[:, 512:1024],
                                ALU.mult)
                e.tensor_tensor(db[:, sl], db[:, sl], pbc[:, 512:1024],
                                ALU.mult)
            c.mark(K("apl", ci), e)
            c.mark(K("xn1", ci), e)
            c.mark(K("xf", ci), e)
            c.mark(K("bcfree", ci), e)

    def body_tensor(e):
        for ci in range(NC512):
            e.waitm(K("cast", ci))
            if ci >= 1:
                e.waitm(K("statfree", ci - 1))
            for ki in range(8):
                e.matmul(pst[0:1, :], t["onescol_b"][:, 0:1],
                         xbb[:, 512 * ki:512 * (ki + 1)],
                         start=(ki == 0), stop=(ki == 7))
            for ki in range(8):
                e.matmul(psq[0:1, :], t["onescol_b"][:, 0:1],
                         xsq[:, 512 * ki:512 * (ki + 1)],
                         start=(ki == 0), stop=(ki == 7))
            c.mark(K("stat", ci), e)
            e.waitm(K("rows2", ci))
            if ci >= 1:
                e.waitm(K("bcfree", ci - 1))
            e.matmul(pbc[:, 0:512], t["onesrow_f"][0:1, :], rws[0:1, RW["m"]],
                     start=True, stop=True)
            e.matmul(pbc[:, 512:1024], t["onesrow_f"][0:1, :],
                     rws[0:1, RW["sd"]], start=True, stop=True)
            c.mark(K("bc", ci), e)

    def body_scalar(e):
        for ci in range(NC512):
            e.waitm(K("rows1", ci))
            e.activation(rws[0:1, RW["sd"]], rws[0:1, RW["e2"]], AF.Ln)
            e.activation(rws[0:1, RW["sd"]], rws[0:1, RW["sd"]], AF.Exp,
                         scale=0.5)
            c.mark(K("sqr", ci), e)

    c.stage("gln1", [("sync", body_sync), ("gpsimd", body_gp),
                     ("vector", body_vector), ("tensor", body_tensor),
                     ("scalar", body_scalar)])



def _attn1(c):
    nc = c.nc
    with contextlib.ExitStack() as st:
        wq_sb = st.enter_context(nc.sbuf_tensor("wq_sb", [128, 4 * 512], BF16))
        wk_sb = st.enter_context(nc.sbuf_tensor("wk_sb", [128, 4 * 512], BF16))
        wv_sb = st.enter_context(nc.sbuf_tensor("wv_sb", [128, 4 * 512], BF16))
        wo_sb = st.enter_context(nc.sbuf_tensor("wo_sb", [128, 8 * 1024], BF16))
        xb = st.enter_context(nc.sbuf_tensor("xb", [128, 4 * 2048], BF16))
        qf = st.enter_context(nc.sbuf_tensor("qf", [128, 4 * 2048], BF16))
        kf = st.enter_context(nc.sbuf_tensor("kf", [128, 4 * 2048], BF16))
        vh = st.enter_context(nc.sbuf_tensor("vh", [128, 16 * 1024], BF16))
        atb = st.enter_context(nc.sbuf_tensor("atb", [128, 4 * 2048], BF16))
        ao = st.enter_context(nc.sbuf_tensor("ao", [128, 32 * 512], BF16))
        sq1 = st.enter_context(nc.sbuf_tensor("sq1", [128, 2048], BF16))
        s2_s = st.enter_context(nc.sbuf_tensor("s2_s", [65, 128], F32))
        bd4_b = st.enter_context(nc.sbuf_tensor("bd4_b", [128, 4], BF16))
        zp_b = st.enter_context(nc.sbuf_tensor("zp_b", [128, 128], BF16))
        macc = st.enter_context(nc.sbuf_tensor("macc", [128, 12], F32))
        mrow = st.enter_context(nc.sbuf_tensor("mrow", [1, 8], F32))
        mgc = st.enter_context(nc.sbuf_tensor("mgc", [128, 1], F32))
        rdb = st.enter_context(nc.sbuf_tensor("rdb", [128, 512], F32))
        rbb = st.enter_context(nc.sbuf_tensor("rbb", [128, 512], F32))
        s1b = st.enter_context(nc.sbuf_tensor("s1b", [128, 2 * 512], F32))
        psc = st.enter_context(nc.psum_tensor("psc", [128, 2048], F32))
        ppv = st.enter_context(nc.psum_tensor("ppv", [128, 1024], F32))
        pq1 = st.enter_context(nc.psum_tensor("pq1", [128, 1024], F32))
        _attn1_body(c, wq_sb, wk_sb, wv_sb, wo_sb, xb, qf, kf, vh, atb, ao,
                    sq1, s2_s, bd4_b, zp_b, macc, mrow, mgc, rdb, rbb, s1b,
                    psc, ppv, pq1)


def _attn1_body(c, wq_sb, wk_sb, wv_sb, wo_sb, xb, qf, kf, vh, atb, ao,
                sq1, s2_s, bd4_b, zp_b, macc, mrow, mgc, rdb, rbb, s1b,
                psc, ppv, pq1):
    t = c.t
    xn1_h = t["xn1_sp"].rearrange("(ki p) tok -> p ki tok", p=128)
    s1_h = t["s1_sp"].rearrange("(mo p) tok -> p mo tok", p=128)
    A = lambda *a: "a1." + ".".join(str(x) for x in a)

    def body_sync(e):
        tok = e.dma(s2_s[:, :], t["s2_65"], stream="ld")
        c.mark_tok(A("s2ld"), tok)
        for b in range(NB):
            if b >= 1:
                e.waitm(A("vmmd", b - 1))
            tok = e.dma(xb[:, :].rearrange("p (ki tk) -> p ki tk", ki=4),
                        xn1_h[:, 0:4, 2048 * b:2048 * (b + 1)], stream="ld")
            c.mark_tok(A("xb", b), tok)
            for mo in range(8):
                for qc in range(4):
                    idx = mo * 4 + qc
                    e.waitm(A("s1e", b, idx))
                    tok = e.dma(s1_h[:, mo, 2048 * b + 512 * qc:
                                     2048 * b + 512 * (qc + 1)],
                                s1b[:, (idx % 2) * 512:(idx % 2 + 1) * 512],
                                stream="out")
                    c.mark_tok(A("s1d", b, idx), tok)

    def body_gp(e):
        tok = e.dma(wq_sb[:, :].rearrange("p (ki o) -> p ki o", ki=4),
                    t["wqT"].rearrange("(ki p) o -> p ki o", p=128), stream="w")
        c.mark_tok(A("wq"), tok)
        tok = e.dma(wk_sb[:, :].rearrange("p (ki o) -> p ki o", ki=4),
                    t["wkT"].rearrange("(ki p) o -> p ki o", p=128), stream="w")
        c.mark_tok(A("wk"), tok)
        tok = e.dma(wv_sb[:, :].rearrange("p (ki o) -> p ki o", ki=4),
                    t["wvT"].rearrange("(ki p) o -> p ki o", p=128), stream="w")
        c.mark_tok(A("wv"), tok)
        tok = e.dma(wo_sb[:, :].rearrange("p (pp o) -> p pp o", pp=8),
                    t["woTp"].rearrange("(pp r) o -> r pp o", r=128), stream="w")
        c.mark_tok(A("wo"), tok)
        tok = e.dma(bd4_b[:, :], t["blockdiag4"], stream="w")
        c.mark_tok(A("bd4"), tok)
        e.memset(vh[:, :], 0.0)
        e.memset(vh[:, :].rearrange("p (s d) -> p s d", d=64)[:, :, 0:1], 1.0)
        e.memset(zp_b[:, :], 0.0)
        e.memset(rdb[:, :], 0.0)
        c.mark(A("init"), e)

    def body_tensor(e):
        e.waitm(A("wq"))
        e.waitm(A("wk"))
        e.waitm(A("wv"))
        e.waitm(A("wo"))
        e.waitm(A("bd4"))
        for b in range(NB):
            e.waitm(A("xb", b))
            idx = 0
            for w_sb, tag in ((wq_sb, "q"), (wk_sb, "k")):
                for mo in range(4):
                    for tc in range(4):
                        if idx >= 2:
                            e.waitm(A("qkev", b, idx - 2))
                        elif b >= 1:
                            e.waitm(A("s1e", b - 1, 30 + idx))
                        pp = pq1[:, (idx % 2) * 512:(idx % 2 + 1) * 512]
                        for ki in range(4):
                            e.matmul(pp, w_sb[:, 512 * ki + 128 * mo:
                                              512 * ki + 128 * (mo + 1)],
                                     xb[:, 2048 * ki + 512 * tc:
                                        2048 * ki + 512 * (tc + 1)],
                                     start=(ki == 0), stop=(ki == 3))
                        c.mark(A("qmm", b, idx), e)
                        idx += 1
            for tt in range(16):
                e.waitm(A("qkev", b, idx - 2))
                pp = pq1[:, (idx % 2) * 512:(idx % 2 + 1) * 512]
                for ki in range(4):
                    e.matmul(pp, xb[:, 2048 * ki + 128 * tt:
                                    2048 * ki + 128 * (tt + 1)],
                             wv_sb[:, 512 * ki:512 * (ki + 1)],
                             start=(ki == 0), stop=(ki == 3))
                c.mark(A("qmm", b, idx), e)
                if tt == 15:
                    c.mark(A("vmmd", b), e)
                idx += 1
            e.waitm(A("qkev", b, 46))
            e.waitm(A("qkev", b, 47))
            for si in range(2):
                for pg in range(4):
                    e.waitm(A("sq", b, si, pg))
                    for tc in range(4):
                        if (si, pg, tc) > (0, 0, 1):
                            e.waitm(A("bdr", b, *_prev_bd(si, pg, tc)))
                        e.matmul(pq1[0:4, (tc % 2) * 512:(tc % 2) * 512 + 512],
                                 bd4_b[:, :], sq1[:, 512 * tc:512 * (tc + 1)],
                                 start=True, stop=True)
                        c.mark(A("bd", b, si, pg, tc), e)
            e.waitm(A("mx4", b))
            e.transpose(pq1[0:1, 0:4], macc[0:4, 8:9], t["ident_f"][0:4, 0:4])
            c.mark(A("tr", b), e)
            e.waitm(A("mneg", b))
            e.matmul(pq1[:, 4:5], t["onesrow_f"][0:1, :], mrow[0:1, 1:2],
                     start=True, stop=True)
            c.mark(A("mg", b), e)
            for hg in range(4):
                for qc in range(4):
                    cur = (b, hg, qc)
                    prev = _prev_hq(b, hg, qc)
                    for p01 in range(2):
                        if prev is not None:
                            e.waitm(A("ao", *prev, p01))
                        e.matmul(ppv[:, 512 * p01:512 * (p01 + 1)],
                                 zp_b[:, :], sq1[:, 0:512],
                                 start=True, stop=False)
                    for kt in range(16):
                        if prev is not None or kt > 0:
                            pk = (cur + (kt - 1,)) if kt > 0 else (prev + (15,))
                            e.waitm(A("ex", *pk))
                        for s in range(4):
                            e.matmul(psc[:, 512 * s:512 * (s + 1)],
                                     kf[32 * s:32 * (s + 1),
                                        2048 * hg + 128 * kt:
                                        2048 * hg + 128 * (kt + 1)],
                                     qf[32 * s:32 * (s + 1),
                                        2048 * hg + 512 * qc:
                                        2048 * hg + 512 * (qc + 1)],
                                     start=True, stop=True,
                                     tile_position=(32 * s, 0))
                        c.mark(A("sc", *cur, kt), e)
                        e.waitm(A("ex", *cur, kt))
                        for s in range(4):
                            p01 = s // 2
                            base = 1024 * kt + 128 * (2 * hg + p01)
                            e.matmul(ppv[64 * (s % 2):64 * (s % 2) + 33,
                                         512 * p01:512 * (p01 + 1)],
                                     vh[:, base + 64 * (s % 2):
                                        base + 64 * (s % 2) + 33],
                                     atb[:, 2048 * (kt % 4) + 512 * s:
                                         2048 * (kt % 4) + 512 * (s + 1)],
                                     start=False, stop=False,
                                     tile_position=(0, 64 * (s % 2)))
                        c.mark(A("pv", *cur, kt), e)
                    for p01 in range(2):
                        e.matmul(ppv[:, 512 * p01:512 * (p01 + 1)],
                                 zp_b[:, :], sq1[:, 0:512],
                                 start=False, stop=True)
                    c.mark(A("pvs", *cur), e)
                    for p01 in range(2):
                        e.waitm(A("rd", *cur, p01))
                        if prev is not None:
                            e.waitm(A("rbe", *prev, p01))
                        e.waitm(A("s2ld"))
                        e.matmul(pq1[:, 512 * p01:512 * (p01 + 1)],
                                 s2_s[0:65, :], rdb[0:65, 0:512],
                                 start=True, stop=True)
                        c.mark(A("s2", *cur, p01), e)
            for mo in range(8):
                for qc in range(4):
                    idx = mo * 4 + qc
                    if idx >= 2:
                        e.waitm(A("s1e", b, idx - 2))
                    if idx < 2:
                        e.waitm(A("rbe", b, 3, 3, idx))
                    e.waitm(A("ao", b, 3, qc, 1))
                    pw = pq1[:, (idx % 2) * 512:(idx % 2 + 1) * 512]
                    for p in range(8):
                        e.matmul(pw, wo_sb[:, 1024 * p + 128 * mo:
                                           1024 * p + 128 * (mo + 1)],
                                 ao[:, 512 * (4 * p + qc):
                                    512 * (4 * p + qc + 1)],
                                 start=(p == 0), stop=(p == 7))
                    c.mark(A("wo", b, idx), e)

    def body_scalar(e):
        c.a1_slot = {}
        for b in range(NB):
            if b >= 1:
                e.waitm(A("sqd", b - 1))
            idx = 0
            for dst, tag in ((qf, "q"), (kf, "k")):
                for mo in range(4):
                    for tc in range(4):
                        e.waitm(A("qmm", b, idx))
                        e.activation(dst[:, 2048 * mo + 512 * tc:
                                         2048 * mo + 512 * (tc + 1)],
                                     pq1[:, (idx % 2) * 512:(idx % 2 + 1) * 512],
                                     AF.Identity)
                        c.mark(A("qkev", b, idx), e)
                        idx += 1
            e.waitm(A("init"))
            for tt in range(16):
                e.waitm(A("qmm", b, idx))
                pp = pq1[:, (idx % 2) * 512:(idx % 2 + 1) * 512]
                ppv3 = pp.rearrange("p (pp d) -> p pp d", d=64)
                vv = vh[:, 1024 * tt:1024 * (tt + 1)].rearrange(
                    "p (pp d) -> p pp d", d=128)
                e.activation(vv[:, :, 1:33], ppv3[:, :, 0:32], AF.Identity)
                e.activation(vv[:, :, 65:97], ppv3[:, :, 32:64], AF.Identity)
                c.mark(A("qkev", b, idx), e)
                idx += 1
            e.waitm(A("mg", b))
            e.activation(mgc[:, 0:1], pq1[:, 4:5], AF.Identity)
            c.mark(A("mge", b), e)
            for hg in range(4):
                for qc in range(4):
                    cur = (b, hg, qc)
                    for kt in range(16):
                        e.waitm(A("sc", *cur, kt))
                        pu = c.a1_slot.get(kt % 4)
                        if pu is not None:
                            e.waitm(A("pv", *pu))
                        c.a1_slot[kt % 4] = cur + (kt,)
                        e.activation(atb[:, 2048 * (kt % 4):
                                         2048 * (kt % 4 + 1)],
                                     psc[:, :], AF.Exp,
                                     bias=mgc[:, 0:1], scale=0.125)
                        c.mark(A("ex", *cur, kt), e)
                    for p01 in range(2):
                        e.waitm(A("s2", *cur, p01))
                        pv2 = _prev_pair(b, hg, qc, p01)
                        if pv2 is not None:
                            e.waitm(A("ao", *pv2))
                        e.activation(rbb[:, :],
                                     pq1[:, 512 * p01:512 * (p01 + 1)],
                                     AF.Identity)
                        c.mark(A("rbe", *cur, p01), e)
            for mo in range(8):
                for qc in range(4):
                    idx = mo * 4 + qc
                    e.waitm(A("wo", b, idx))
                    if idx >= 2:
                        e.waitm(A("s1d", b, idx - 2))
                    elif b >= 1:
                        e.waitm(A("s1d", b - 1, 30 + idx))
                    e.activation(s1b[:, (idx % 2) * 512:(idx % 2 + 1) * 512],
                                 pq1[:, (idx % 2) * 512:(idx % 2 + 1) * 512],
                                 AF.Identity, bias=t["bo_s"][:, mo:mo + 1])
                    c.mark(A("s1e", b, idx), e)

    def body_vector(e):
        for b in range(NB):
            if b >= 1:
                e.waitm(A("pvs", b - 1, 3, 3))
            for si, srcf in ((0, qf), (1, kf)):
                for pg in range(4):
                    e.waitm(A("qkev", b, 16 * si + 4 * pg + 3))
                    e.tensor_tensor(sq1[:, :], srcf[:, 2048 * pg:2048 * (pg + 1)],
                                    srcf[:, 2048 * pg:2048 * (pg + 1)], ALU.mult)
                    c.mark(A("sq", b, si, pg), e)
                    col = 4 * si + pg
                    for tc in range(4):
                        e.waitm(A("bd", b, si, pg, tc))
                        pb = pq1[0:4, (tc % 2) * 512:(tc % 2) * 512 + 512]
                        if si == 0 and pg == 0 and tc == 0:
                            e.tensor_reduce(macc[0:4, 8:9], pb, AX.X, ALU.max)
                        else:
                            e.tensor_reduce(macc[0:4, 9:10], pb, AX.X, ALU.max)
                            e.tensor_tensor(macc[0:4, 8:9], macc[0:4, 8:9],
                                            macc[0:4, 9:10], ALU.max)
                        c.mark(A("bdr", b, si, pg, tc), e)
            c.mark(A("sqd", b), e)
            c.mark(A("mx4", b), e)
            e.waitm(A("tr", b))
            e.tensor_reduce(mrow[0:1, 0:1], pq1[0:1, 0:4], AX.X, ALU.max)
            e.tensor_scalar(mrow[0:1, 1:2], mrow[0:1, 0:1], -0.125, None,
                            ALU.mult)
            c.mark(A("mneg", b), e)
            if b >= 1:
                e.waitm(A("wo", b - 1, 31))
            for hg in range(4):
                for qc in range(4):
                    cur = (b, hg, qc)
                    e.waitm(A("pvs", *cur))
                    for p01 in range(2):
                        p = 2 * hg + p01
                        pv2 = _prev_pair2(b, hg, qc, p01)
                        if pv2 is not None:
                            e.waitm(A("s2", *pv2))
                        e.waitm(A("init"))
                        e.reciprocal(rdb[0:1, 0:512],
                                     ppv[0:1, 512 * p01:512 * (p01 + 1)])
                        e.reciprocal(rdb[64:65, 0:512],
                                     ppv[64:65, 512 * p01:512 * (p01 + 1)])
                        c.mark(A("rd", *cur, p01), e)
                        e.waitm(A("rbe", *cur, p01))
                        e.scalar_tensor_tensor(
                            ao[:, 512 * (4 * p + qc):512 * (4 * p + qc + 1)],
                            ppv[:, 512 * p01:512 * (p01 + 1)], 1.0,
                            rbb[:, :], ALU.mult, ALU.mult)
                        c.mark(A("ao", *cur, p01), e)

    c.stage("attn1", [("sync", body_sync), ("gpsimd", body_gp),
                      ("tensor", body_tensor), ("scalar", body_scalar),
                      ("vector", body_vector)])


def _prev_bd(si, pg, tc):
    n = (si * 4 + pg) * 4 + tc - 2
    return (n // 16, (n // 4) % 4, n % 4)


def _prev_hq(b, hg, qc):
    n = (b * 4 + hg) * 4 + qc - 1
    if n < 0:
        return None
    return (n // 16, (n // 4) % 4, n % 4)


def _prev_pair(b, hg, qc, p01):
    n = ((b * 4 + hg) * 4 + qc) * 2 + p01 - 1
    if n < 0:
        return None
    return (n // 32, (n // 8) % 4, (n // 2) % 4, n % 2)


def _prev_pair2(b, hg, qc, p01):
    n = ((b * 4 + hg) * 4 + qc) * 2 + p01 - 2
    if n < 0:
        return None
    return (n // 32, (n // 8) % 4, (n // 2) % 4, n % 2)



def _sharedffn(c):
    nc = c.nc
    with contextlib.ExitStack() as st:
        sgw = st.enter_context(nc.sbuf_tensor("sgw", [128, 8 * 1024], BF16))
        suw = st.enter_context(nc.sbuf_tensor("suw", [128, 8 * 1024], BF16))
        sdw = st.enter_context(nc.sbuf_tensor("sdw", [128, 8 * 1024], BF16))
        xls = st.enter_context(nc.sbuf_tensor("xls", [128, 2 * 4096], BF16))
        gas = st.enter_context(nc.sbuf_tensor("gas", [128, 2 * 512], F32))
        ybs = st.enter_context(nc.sbuf_tensor("ybs", [128, 8 * 512], BF16))
        shb = st.enter_context(nc.sbuf_tensor("shb", [128, 2 * 512], F32))
        pgs = st.enter_context(nc.psum_tensor("pgs", [128, 4 * 512], F32))
        pds = st.enter_context(nc.psum_tensor("pds2", [128, 2 * 512], F32))
        _sharedffn_body(c, sgw, suw, sdw, xls, gas, ybs, shb, pgs, pds)


def _sharedffn_body(c, sgw, suw, sdw, xls, gas, ybs, shb, pgs, pds):
    t = c.t
    xn1_h = t["xn1_sp"].rearrange("(ki p) tok -> p ki tok", p=128)
    sh_h = t["sh_sp"].rearrange("(mo p) tok -> p mo tok", p=128)
    A = lambda *a: "sf." + ".".join(str(x) for x in a)

    def body_sync(e):
        for tc in range(NC512):
            if tc >= 2:
                e.waitm(A("xdone", tc - 2))
            tok = e.dma(xls[:, (tc % 2) * 4096:(tc % 2 + 1) * 4096]
                        .rearrange("p (ki tk) -> p ki tk", ki=8),
                        xn1_h[:, :, 512 * tc:512 * (tc + 1)], stream="ld")
            c.mark_tok(A("x", tc), tok)
            for mo in range(8):
                e.waitm(A("she", tc, mo))
                tok = e.dma(sh_h[:, mo, 512 * tc:512 * (tc + 1)],
                            shb[:, (mo % 2) * 512:(mo % 2 + 1) * 512],
                            stream="spx")
                c.mark_tok(A("shd", tc, mo), tok)

    def body_gp(e):
        for nm, w_sb in (("sg", sgw), ("su", suw), ("sd", sdw)):
            tok = e.dma(w_sb[:, :].rearrange("p (ki o) -> p ki o", ki=8),
                        t[nm].rearrange("(ki p) o -> p ki o", p=128), stream="w")
            c.mark_tok(A("w", nm), tok)

    def body_tensor(e):
        e.waitm(A("w", "sg"))
        e.waitm(A("w", "su"))
        e.waitm(A("w", "sd"))
        for tc in range(NC512):
            e.waitm(A("x", tc))
            xl = xls[:, (tc % 2) * 4096:(tc % 2 + 1) * 4096]
            for mo in range(8):
                for half, w_sb in ((0, sgw), (1, suw)):
                    bk = (mo * 2 + half) % 4
                    if tc > 0 or mo * 2 + half >= 4:
                        e.waitm(A("guf", *_sf_prev(tc, mo * 2 + half, 4)))
                    pp = pgs[:, 512 * bk:512 * (bk + 1)]
                    for ki in range(8):
                        e.matmul(pp, w_sb[:, 1024 * ki + 128 * mo:
                                          1024 * ki + 128 * (mo + 1)],
                                 xl[:, 512 * ki:512 * (ki + 1)],
                                 start=(ki == 0), stop=(ki == 7))
                    c.mark(A("gu", tc, mo, half), e)
            c.mark(A("xdone", tc), e)
            for mo in range(8):
                e.waitm(A("y", tc, 7))
                if tc > 0 or mo >= 2:
                    e.waitm(A("she", *_sf_prev(tc, mo, 2)))
                pp = pds[:, (mo % 2) * 512:(mo % 2 + 1) * 512]
                for ki in range(8):
                    e.matmul(pp, sdw[:, 1024 * ki + 128 * mo:
                                     1024 * ki + 128 * (mo + 1)],
                             ybs[:, 512 * ki:512 * (ki + 1)],
                             start=(ki == 0), stop=(ki == 7))
                c.mark(A("dn", tc, mo), e)

    def body_scalar(e):
        for tc in range(NC512):
            for mo in range(8):
                e.waitm(A("gu", tc, mo, 0))
                if tc > 0 or mo >= 2:
                    e.waitm(A("yst", *_sf_prev(tc, mo, 2)))
                e.activation(gas[:, (mo % 2) * 512:(mo % 2 + 1) * 512],
                             pgs[:, 512 * ((mo * 2) % 4):
                                 512 * ((mo * 2) % 4) + 512], AF.Silu)
                c.mark(A("ga", tc, mo), e)
            for mo in range(8):
                e.waitm(A("dn", tc, mo))
                if tc > 0 or mo >= 2:
                    e.waitm(A("shd", *_sf_prev(tc, mo, 2)))
                e.activation(shb[:, (mo % 2) * 512:(mo % 2 + 1) * 512],
                             pds[:, (mo % 2) * 512:(mo % 2 + 1) * 512],
                             AF.Identity)
                c.mark(A("she", tc, mo), e)

    def body_vector(e):
        for tc in range(NC512):
            for mo in range(8):
                e.waitm(A("ga", tc, mo))
                e.waitm(A("gu", tc, mo, 1))
                if tc > 0 and mo == 0:
                    e.waitm(A("dn", tc - 1, 7))
                e.scalar_tensor_tensor(ybs[:, 512 * mo:512 * (mo + 1)],
                                       pgs[:, 512 * ((mo * 2 + 1) % 4):
                                           512 * ((mo * 2 + 1) % 4) + 512],
                                       1.0, gas[:, (mo % 2) * 512:
                                                (mo % 2 + 1) * 512],
                                       ALU.mult, ALU.mult)
                c.mark(A("yst", tc, mo), e)
                c.mark(A("y", tc, mo), e)
                c.mark(A("guf", tc, mo * 2), e)
                c.mark(A("guf", tc, mo * 2 + 1), e)

    c.stage("sffn", [("sync", body_sync), ("gpsimd", body_gp),
                     ("tensor", body_tensor), ("scalar", body_scalar),
                     ("vector", body_vector)])


def _sf_prev(tc, i, back):
    n = 16 if back == 4 else 8
    k = tc * n + i - back
    return (k // n, k % n)



def _moe_a(c):
    nc = c.nc
    with contextlib.ExitStack() as st:
        xb2 = st.enter_context(nc.sbuf_tensor("xb2", [128, 8 * 2048], BF16))
        egw = st.enter_context(nc.sbuf_tensor("egw", [128, 2 * 4096], BF16))
        euw = st.enter_context(nc.sbuf_tensor("euw", [128, 2 * 4096], BF16))
        rw_b = st.enter_context(nc.sbuf_tensor("rw_b", [128, 64], F32))
        xf1t = st.enter_context(nc.sbuf_tensor("xf1t", [128, 2 * 1024], F32))
        ind8f = st.enter_context(nc.sbuf_tensor("ind8f", [8, 1024], F32))
        coefT = st.enter_context(nc.sbuf_tensor("coefT", [8, 2048], F32))
        prw = st.enter_context(nc.sbuf_tensor("prw", [128, 48], F32))
        pcols = st.enter_context(nc.sbuf_tensor("pcols", [128, 8], F32))
        cbm2_x = st.enter_context(nc.sbuf_tensor("cbm2_x", [128, 2048], BF16))
        sgm = st.enter_context(nc.sbuf_tensor("sgm", [128, 2 * 512], F32))
        tbm = st.enter_context(nc.sbuf_tensor("tbm", [128, 2 * 512], BF16))
        hbm = st.enter_context(nc.sbuf_tensor("hbm", [128, 2 * 512], BF16))
        pgu = st.enter_context(nc.psum_tensor("pgu", [128, 4 * 512], F32))
        pr1 = st.enter_context(nc.psum_tensor("pr1", [128, 512], F32))
        pr2 = st.enter_context(nc.psum_tensor("pr2", [128, 512], F32))
        _moe_a_body(c, xb2, egw, euw, rw_b, ind8f, coefT, prw, pcols, cbm2_x,
                    sgm, tbm, hbm, pgu, pr1, pr2, xf1t)


def _moe_a_body(c, xb2, egw, euw, rw_b, ind8f, coefT, prw, pcols, cbm2_x,
                sgm, tbm, hbm, pgu, pr1, pr2, xf1t=None):
    t = c.t
    xn1_h = t["xn1_sp"].rearrange("(ki p) tok -> p ki tok", p=128)
    xf_h = t["xn1f_sp"].rearrange("(ki p) tok -> p ki tok", p=128)
    he_h = t["he_sp"].rearrange("(e ki p) tok -> p e ki tok", p=128, ki=4)
    eg_h = t["eg"].rearrange("e (ki p) o -> p e ki o", p=128)
    eu_h = t["eu"].rearrange("e (ki p) o -> p e ki o", p=128)
    A = lambda *a: "ma." + ".".join(str(x) for x in a)

    def body_sync(e):
        tok = e.dma(ind8f[:, :], t["ind8"], stream="ld")
        c.mark_tok(A("ind"), tok)
        tok = e.dma(rw_b[:, :].rearrange("p (ki o) -> p ki o", ki=8),
                    t["router_w"].rearrange("(ki p) o -> p ki o", p=128),
                    stream="ld")
        c.mark_tok(A("rw"), tok)
        for b in range(NB):
            if b >= 1:
                e.waitm(A("gum", b - 1, 7, 15))
            tok = e.dma(xb2[:, :].rearrange("p (ki tk) -> p ki tk", ki=8),
                        xn1_h[:, :, 2048 * b:2048 * (b + 1)], stream="ld")
            c.mark_tok(A("xb", b), tok)
            for tt in range(16):
                if b > 0 or tt >= 2:
                    e.waitm(A("rmm", *_ma_prevtt(b, tt)))
                tok = e.dma(xf1t[:, (tt % 2) * 1024:(tt % 2 + 1) * 1024]
                            .rearrange("p (ki tk) -> p ki tk", ki=8),
                            xf_h[:, :, 2048 * b + 128 * tt:
                                 2048 * b + 128 * (tt + 1)], stream="ld")
                c.mark_tok(A("xf", b, tt), tok)
            for ee in range(8):
                for j in range(16):
                    e.waitm(A("he", b, ee, j))
                    mo, tc = j // 4, j % 4
                    tok = e.dma(he_h[:, ee, mo, 2048 * b + 512 * tc:
                                     2048 * b + 512 * (tc + 1)],
                                hbm[:, (j % 2) * 512:(j % 2 + 1) * 512],
                                stream="sph")
                    c.mark_tok(A("hed", b, ee, j), tok)

    def body_gp(e):
        for b in range(NB):
            for ee in range(8):
                if (b, ee) > (0, 1):
                    e.waitm(A("gum", *_ma_prev(b, ee), 15))
                tok = e.dma(egw[:, (ee % 2) * 4096:(ee % 2 + 1) * 4096]
                            .rearrange("p (ki o) -> p ki o", ki=8),
                            eg_h[:, ee, :, :], stream="w")
                c.mark_tok(A("wg", b, ee), tok)
                tok = e.dma(euw[:, (ee % 2) * 4096:(ee % 2 + 1) * 4096]
                            .rearrange("p (ki o) -> p ki o", ki=8),
                            eu_h[:, ee, :, :], stream="w")
                c.mark_tok(A("wu", b, ee), tok)

    def body_tensor(e):
        e.waitm(A("rw"))
        e.waitm(A("ind"))
        for b in range(NB):
            e.waitm(A("xb", b))
            for tt in range(16):
                if tt >= 2:
                    e.waitm(A("ex", b, tt - 2))
                elif b >= 1:
                    e.waitm(A("ex", b - 1, 14 + tt))
                e.waitm(A("xf", b, tt))
                pp = pr1[:, (tt % 2) * 16:(tt % 2) * 16 + 8]
                for ki in range(8):
                    e.matmul(pp, xf1t[:, (tt % 2) * 1024 + 128 * ki:
                                      (tt % 2) * 1024 + 128 * (ki + 1)],
                             rw_b[:, 8 * ki:8 * (ki + 1)],
                             start=(ki == 0), stop=(ki == 7))
                c.mark(A("rmm", b, tt), e)
                e.waitm(A("coef", b, tt))
                if tt >= 1:
                    e.waitm(A("cte", b, tt - 1))
                elif b >= 1:
                    e.waitm(A("cbm", b - 1, 7, 3))
                e.transpose(pr2[0:8, 0:128],
                            prw[:, (tt % 2) * 24 + 16:(tt % 2) * 24 + 24],
                            t["ident_f"][:, :])
                c.mark(A("ctr", b, tt), e)
            for ee in range(8):
                e.waitm(A("cte", b, 15))
                e.waitm(A("wg", b, ee))
                e.waitm(A("wu", b, ee))
                for tc in range(4):
                    e.waitm(A("cbe", *_ma_prevj(b, ee, tc, 1, 4)))
                    e.matmul(pr2[:, 0:512], ind8f[0:8, 128 * ee:128 * (ee + 1)],
                             coefT[0:8, 512 * tc:512 * (tc + 1)],
                             start=True, stop=True)
                    c.mark(A("cbm", b, ee, tc), e)
                for j in range(16):
                    mo, tc = j // 4, j % 4
                    jj = ee * 16 + j
                    for half, w_sb in ((0, egw), (1, euw)):
                        bk = (2 * jj + half) % 4
                        pj = _ma_prevj(b, ee, 2 * j + half, 4, 32)
                        if pj[0] >= 0:
                            pb2, pe2, ph = pj
                            e.waitm(A("sil" if ph % 2 == 0 else "stt",
                                      pb2, pe2, ph // 2))
                        pp = pgu[:, 512 * bk:512 * (bk + 1)]
                        for ki in range(8):
                            e.matmul(pp, w_sb[:, (ee % 2) * 4096 + 512 * ki +
                                              128 * mo:(ee % 2) * 4096 +
                                              512 * ki + 128 * (mo + 1)],
                                     xb2[:, 2048 * ki + 512 * tc:
                                         2048 * ki + 512 * (tc + 1)],
                                     start=(ki == 0), stop=(ki == 7))
                        c.mark(A("gmm", b, ee, 2 * j + half), e)
                    c.mark(A("gum", b, ee, j), e)

    def body_scalar(e):
        for b in range(NB):
            for tt in range(16):
                e.waitm(A("rmm", b, tt))
                e.waitm(A("ngm", b, tt))
                e.activation(prw[:, (tt % 2) * 24:(tt % 2) * 24 + 8],
                             pr1[:, (tt % 2) * 16:(tt % 2) * 16 + 8],
                             AF.Exp, bias=pcols[:, 1 + 4 * (tt % 2):
                                               2 + 4 * (tt % 2)])
                c.mark(A("ex", b, tt), e)
                e.waitm(A("ctr", b, tt))
                e.activation(coefT[0:8, 128 * tt:128 * (tt + 1)],
                             pr2[0:8, 0:128], AF.Identity)
                c.mark(A("cte", b, tt), e)
            for ee in range(8):
                for tc in range(4):
                    e.waitm(A("cbm", b, ee, tc))
                    if ee >= 1:
                        e.waitm(A("he", b, ee - 1, 15))
                    e.activation(cbm2_x[:, 512 * tc:512 * (tc + 1)],
                                 pr2[:, 0:512], AF.Identity)
                    c.mark(A("cbe", b, ee, tc), e)
                for j in range(16):
                    e.waitm(A("gmm", b, ee, 2 * j))
                    e.waitm(A("stt", *_ma_prevj(b, ee, j, 2, 16)))
                    e.activation(sgm[:, (j % 2) * 512:(j % 2 + 1) * 512],
                                 pgu[:, 512 * ((2 * (ee * 16 + j)) % 4):
                                     512 * ((2 * (ee * 16 + j)) % 4) + 512],
                                 AF.Silu)
                    c.mark(A("sil", b, ee, j), e)

    def body_vector(e):
        for b in range(NB):
            for tt in range(16):
                sl = (tt % 2) * 24
                cs = 4 * (tt % 2)
                e.waitm(A("rmm", b, tt))
                if tt >= 2:
                    e.waitm(A("ctr", b, tt - 2))
                elif b >= 1:
                    e.waitm(A("ctr", b - 1, 14 + tt))
                pp = pr1[:, (tt % 2) * 16:(tt % 2) * 16 + 8]
                e.tensor_reduce(pcols[:, cs:cs + 1], pp, AX.X, ALU.max)
                e.tensor_scalar(pcols[:, cs + 1:cs + 2], pcols[:, cs:cs + 1],
                                -1.0, None, ALU.mult)
                c.mark(A("ngm", b, tt), e)
                e.waitm(A("ex", b, tt))
                pb = prw[:, sl:sl + 8]
                e.tensor_reduce(pcols[:, cs + 2:cs + 3], pb, AX.X, ALU.add)
                e.reciprocal(pcols[:, cs + 2:cs + 3], pcols[:, cs + 2:cs + 3])
                e.tensor_scalar(pb, pb, pcols[:, cs + 2:cs + 3], None, ALU.mult)
                e.tensor_reduce(pcols[:, cs:cs + 1], pb, AX.X, ALU.max)
                e.tensor_scalar(prw[:, sl + 8:sl + 16], pb,
                                pcols[:, cs:cs + 1], None, ALU.is_equal)
                e.tensor_tensor(prw[:, sl + 8:sl + 16], prw[:, sl + 8:sl + 16],
                                pb, ALU.mult)
                e.tensor_tensor(prw[:, sl + 8:sl + 16], pb,
                                prw[:, sl + 8:sl + 16], ALU.subtract)
                e.tensor_reduce(pcols[:, cs + 1:cs + 2],
                                prw[:, sl + 8:sl + 16], AX.X, ALU.max)
                e.tensor_scalar(prw[:, sl + 8:sl + 16], pb,
                                pcols[:, cs + 1:cs + 2], None, ALU.is_ge)
                e.tensor_tensor(prw[:, sl + 16:sl + 24], pb,
                                prw[:, sl + 8:sl + 16], ALU.mult)
                c.mark(A("coef", b, tt), e)
            for ee in range(8):
                for j in range(16):
                    mo, tc = j // 4, j % 4
                    jj = ee * 16 + j
                    e.waitm(A("sil", b, ee, j))
                    e.waitm(A("gmm", b, ee, 2 * j + 1))
                    e.waitm(A("cbe", b, ee, tc))
                    e.scalar_tensor_tensor(
                        tbm[:, (j % 2) * 512:(j % 2 + 1) * 512],
                        pgu[:, 512 * ((2 * jj + 1) % 4):
                            512 * ((2 * jj + 1) % 4) + 512], 1.0,
                        sgm[:, (j % 2) * 512:(j % 2 + 1) * 512],
                        ALU.mult, ALU.mult)
                    c.mark(A("stt", b, ee, j), e)
                    e.waitm(A("hed", *_ma_prevj(b, ee, j, 2, 16)))
                    e.tensor_tensor(hbm[:, (j % 2) * 512:(j % 2 + 1) * 512],
                                    tbm[:, (j % 2) * 512:(j % 2 + 1) * 512],
                                    cbm2_x[:, 512 * tc:512 * (tc + 1)], ALU.mult)
                    c.mark(A("he", b, ee, j), e)

    c.stage("moea", [("sync", body_sync), ("gpsimd", body_gp),
                     ("tensor", body_tensor), ("scalar", body_scalar),
                     ("vector", body_vector)])


def _ma_prevtt(b, tt):
    k = b * 16 + tt - 2
    return (k // 16, k % 16)


def _ma_prev(b, ee):
    k = b * 8 + ee - 2
    return (k // 8, k % 8)


def _ma_prevj(b, ee, j, back, n):
    k = (b * 8 + ee) * n + j - back
    if k < 0:
        return (-1, -1, -1)
    return (k // (8 * n), (k // n) % 8, k % n)



def _moe_b(c):
    nc = c.nc
    with contextlib.ExitStack() as st:
        edw = st.enter_context(nc.sbuf_tensor("edw", [128, 32 * 1024], BF16))
        hel = st.enter_context(nc.sbuf_tensor("hel", [128, 2 * 16384], BF16))
        shl = st.enter_context(nc.sbuf_tensor("shl", [128, 2 * 512], F32))
        x1l = st.enter_context(nc.sbuf_tensor("x1l", [128, 2 * 512], F32))
        s1l = st.enter_context(nc.sbuf_tensor("s1l", [128, 2 * 512], F32))
        fbb = st.enter_context(nc.sbuf_tensor("fbb", [128, 2 * 512], F32))
        g1b = st.enter_context(nc.sbuf_tensor("g1b", [128, 2 * 512], F32))
        otb = st.enter_context(nc.sbuf_tensor("otb", [128, 2 * 512], F32))
        pdw = st.enter_context(nc.psum_tensor("pdw", [128, 2 * 512], F32))
        _moe_b_body(c, edw, hel, shl, x1l, s1l, fbb, g1b, otb, pdw)


def _moe_b_body(c, edw, hel, shl, x1l, s1l, fbb, g1b, otb, pdw):
    t = c.t
    he_h = t["he_sp"].rearrange("(e ki p) tok -> p e ki tok", p=128, ki=4)
    ed_h = t["ed"].rearrange("e (ki p) o -> p e ki o", p=128)
    sh_h = t["sh_sp"].rearrange("(mo p) tok -> p mo tok", p=128)
    x1_h = t["x1_sp"].rearrange("(mo p) tok -> p mo tok", p=128)
    s1_h = t["s1_sp"].rearrange("(mo p) tok -> p mo tok", p=128)
    out_h = t["out_fm"].rearrange("(mo p) tok -> p mo tok", p=128)
    A = lambda *a: "mb." + ".".join(str(x) for x in a)

    def body_sync(e):
        for tc in range(NC512):
            if tc >= 2:
                e.waitm(A("dmm", tc - 2, 7))
            tok = e.dma(hel[:, (tc % 2) * 16384:(tc % 2 + 1) * 16384]
                        .rearrange("p (ee ki tk) -> p ee ki tk", ee=8, ki=4),
                        he_h[:, :, :, 512 * tc:512 * (tc + 1)], stream="ld")
            c.mark_tok(A("he", tc), tok)
            for mo in range(8):
                i = tc * 8 + mo
                if i >= 2:
                    e.waitm(A("stt", *_mb_prev(tc, mo, 2)))
                tok = e.dma(shl[:, (i % 2) * 512:(i % 2 + 1) * 512],
                            sh_h[:, mo, 512 * tc:512 * (tc + 1)], stream="ld")
                c.mark_tok(A("sh", tc, mo), tok)
                if i >= 2:
                    e.waitm(A("gl", *_mb_prev(tc, mo, 2)))
                tok = e.dma(s1l[:, (i % 2) * 512:(i % 2 + 1) * 512],
                            s1_h[:, mo, 512 * tc:512 * (tc + 1)], stream="ld")
                c.mark_tok(A("s1", tc, mo), tok)
                if i >= 2:
                    e.waitm(A("ot", *_mb_prev(tc, mo, 2)))
                tok = e.dma(x1l[:, (i % 2) * 512:(i % 2 + 1) * 512],
                            x1_h[:, mo, 512 * tc:512 * (tc + 1)], stream="ld")
                c.mark_tok(A("x1", tc, mo), tok)
                e.waitm(A("ot", tc, mo))
                tok = e.dma(out_h[:, mo, 512 * tc:512 * (tc + 1)],
                            otb[:, (i % 2) * 512:(i % 2 + 1) * 512],
                            stream="out")
                c.mark_tok(A("od", tc, mo), tok)
        for snm in ("ld", "w", "spx", "sph", "out"):
            e.wait_stream_all(snm)

    def body_gp(e):
        tok = e.dma(edw[:, :].rearrange("p (ee ki o) -> p ee ki o", ee=8, ki=4),
                    ed_h[:, :, :, :], stream="w")
        c.mark_tok(A("ed"), tok)

    def body_tensor(e):
        e.waitm(A("ed"))
        edv = edw[:, :].rearrange("p (ee ki o) -> p ee ki o", ee=8, ki=4)
        for tc in range(NC512):
            e.waitm(A("he", tc))
            hv = hel[:, (tc % 2) * 16384:(tc % 2 + 1) * 16384].rearrange(
                "p (ee ki tk) -> p ee ki tk", ee=8, ki=4)
            for mo in range(8):
                i = tc * 8 + mo
                if i >= 2:
                    e.waitm(A("stt", *_mb_prev(tc, mo, 2)))
                pp = pdw[:, (i % 2) * 512:(i % 2 + 1) * 512]
                for ee in range(8):
                    for ki in range(4):
                        e.matmul(pp, edv[:, ee, ki, 128 * mo:128 * (mo + 1)],
                                 hv[:, ee, ki, :],
                                 start=(ee == 0 and ki == 0),
                                 stop=(ee == 7 and ki == 3))
                c.mark(A("dmm", tc, mo), e)

    def body_scalar(e):
        for tc in range(NC512):
            for mo in range(8):
                i = tc * 8 + mo
                e.waitm(A("stt", tc, mo))
                if i >= 2:
                    e.waitm(A("gl", *_mb_prev(tc, mo, 2)))
                e.activation(g1b[:, (i % 2) * 512:(i % 2 + 1) * 512],
                             fbb[:, (i % 2) * 512:(i % 2 + 1) * 512], AF.Silu)
                c.mark(A("g1", tc, mo), e)

    def body_vector(e):
        for tc in range(NC512):
            for mo in range(8):
                i = tc * 8 + mo
                e.waitm(A("dmm", tc, mo))
                e.waitm(A("sh", tc, mo))
                pp = pdw[:, (i % 2) * 512:(i % 2 + 1) * 512]
                e.scalar_tensor_tensor(fbb[:, (i % 2) * 512:(i % 2 + 1) * 512],
                                       pp, 1.0,
                                       shl[:, (i % 2) * 512:(i % 2 + 1) * 512],
                                       ALU.mult, ALU.add)
                c.mark(A("stt", tc, mo), e)
                e.waitm(A("g1", tc, mo))
                e.waitm(A("s1", tc, mo))
                e.tensor_tensor(g1b[:, (i % 2) * 512:(i % 2 + 1) * 512],
                                s1l[:, (i % 2) * 512:(i % 2 + 1) * 512],
                                g1b[:, (i % 2) * 512:(i % 2 + 1) * 512],
                                ALU.mult)
                c.mark(A("gl", tc, mo), e)
                e.waitm(A("x1", tc, mo))
                if i >= 2:
                    e.waitm(A("od", *_mb_prev(tc, mo, 2)))
                e.tensor_tensor(otb[:, (i % 2) * 512:(i % 2 + 1) * 512],
                                x1l[:, (i % 2) * 512:(i % 2 + 1) * 512],
                                g1b[:, (i % 2) * 512:(i % 2 + 1) * 512],
                                ALU.add)
                c.mark(A("ot", tc, mo), e)

    c.stage("moeb", [("sync", body_sync), ("gpsimd", body_gp),
                     ("tensor", body_tensor), ("scalar", body_scalar),
                     ("vector", body_vector)])


def _mb_prev(tc, mo, back):
    k = tc * 8 + mo - back
    return (k // 8, k % 8)


def _coll_max(c, slot, scale_col, mark_in, extra_div=1.0):
    with c.nc.psum_tensor("pcol%d" % slot, [128, 512], F32) as pc:
        _coll_max_body(c, slot, scale_col, mark_in, extra_div, pc)


def _coll_max_body(c, slot, scale_col, mark_in, extra_div, pc):
    t = c.t
    P = lambda nm: "cm%d.%s" % (slot, nm)

    def body_tensor(e):
        e.waitm(mark_in)
        e.transpose(pc[0:1, 0:128], t["mx_acc"][:, 0:1], t["ident_f"][:, :])
        c.mark(P("tr"), e)
        e.waitm(P("val"))
        e.matmul(pc[:, 128:129], t["onesrow_f"][0:1, :],
                 t["ccv"][0:1, 8 + slot:9 + slot], start=True, stop=True)
        c.mark(P("bc"), e)

    def body_gp(e):
        e.waitm(P("red"))
        tok = e.dma(t["cc_in"][0:1, :], t["ccv"][0:1, 0:16], stream="w")
        e.wait(tok[0], tok[1])
        ctok = e.collective_max(t["cc_in"].opt(), t["cc_out"].opt())
        e.wait(ctok[0], ctok[1])
        tok = e.dma(t["ccv"][0:1, 8 + slot:9 + slot],
                    t["cc_out"][0:1, slot:slot + 1], stream="w")
        c.mark_tok(P("ccout"), tok)

    def body_vector(e):
        e.waitm(P("tr"))
        e.tensor_reduce(t["ccv"][0:1, slot:slot + 1], pc[0:1, 0:128], AX.X, ALU.max)
        c.mark(P("red"), e)
        e.waitm(P("ccout"))
        e.tensor_scalar(t["ccv"][0:1, 8 + slot:9 + slot],
                        t["ccv"][0:1, 8 + slot:9 + slot],
                        1.0 / (QMAX * extra_div), 1e-12, ALU.mult, ALU.max)
        c.mark(P("val"), e)
        e.waitm(P("bc"))
        e.tensor_copy(t["scol"][:, scale_col:scale_col + 1], pc[:, 128:129])
        c.mark(P("done"), e)

    c.stage("coll%d" % slot, [("tensor", body_tensor), ("gpsimd", body_gp),
                              ("vector", body_vector)])


def _finish_stub(c):
    t = c.t

    def body_scalar(e):
        e.waitm("cs0.done")
        e.waitm("sf.shd.15.7")
        tok = e.dma(t["out_fm"][0:128, 0:128], t["ident_f"][:, :], stream="out")
        e.wait(tok[0], tok[1])

    c.stage("finish", [("scalar", body_scalar)])



_WNAMES = ("uv", "projwT", "mlpw1", "mlpw2", "wup_nat", "wgate_nat", "wdown_nat",
           "wupT", "wgateT", "wdownT", "wqT", "wkT", "wvT", "woTp", "router_w",
           "eg", "eu", "ed", "sg", "su", "sd", "projb", "n1w", "n2w", "mlpb1",
           "mlpb2", "bo", "params")
_CNAMES = ("ident", "rampm", "ramp127_col", "mask_keep", "shift4", "shiftprev4",
           "ones_col", "ones_row", "rep32", "s2_65", "blockdiag4", "ind8")


def _run(inputs, debug=()):
    consts = _host_consts()
    w = _prep_weights(inputs)
    x = np.asarray(inputs["L0_attn_h"], np.float32)
    nc, c = build_nc(debug=debug)
    in_maps = []
    for core in range(8):
        xs = np.ascontiguousarray(
            x[4 * core:4 * core + 4].transpose(2, 0, 1).reshape(D, NTOK))
        m = {"x_fm": xs}
        m.update({k: w[k] for k in _WNAMES})
        m.update({k: consts[k] for k in _CNAMES})
        in_maps.append(m)
    return run_bass_kernel_spmd(nc, in_maps, core_ids=list(range(8)))


def kernel(**inputs):
    res = _run(inputs, debug=DBG)
    outs = []
    for core in range(8):
        o = np.asarray(res.results[core]["out_fm"])
        outs.append(o.reshape(D, NB, SEQ).transpose(1, 2, 0))
    return np.ascontiguousarray(np.concatenate(outs, axis=0), dtype=np.float32)


# version 06 (final)
# speedup vs baseline: 1.1022x; 1.1022x over previous
import contextlib
import numpy as np
import concourse.bass as bass
import concourse.mybir as mybir
from concourse.bass_utils import run_bass_kernel_spmd

F32 = mybir.dt.float32
BF16 = mybir.dt.bfloat16
AF = mybir.ActivationFunctionType
ALU = mybir.AluOpType
AX = mybir.AxisListType

NB = 4
SEQ = 2048
D = 1024
NTOK = NB * SEQ
NC512 = NTOK // 512
NT128 = NTOK // 128
DI = 3072
DH = 4096
RBIG = 12582912.0
QMAX = 32767.0

DBG = ()
PE_LAZY = False



def _host_consts():
    c = {}
    c["ident"] = np.eye(128, dtype=np.float32)
    ar = np.arange(128, dtype=np.float32)
    ramps = [ar % 32, ar % 32 + 1.0, -(ar % 32), 31.0 - ar % 32,
             63.0 - ar % 32, 95.0 - ar % 32, ar + 1.0]
    c["rampm"] = np.ascontiguousarray(np.concatenate(
        [np.tile(r.reshape(1, 128), (128, 1)) for r in ramps], axis=1))
    c["ramp127_col"] = (127.0 - ar).reshape(128, 1)
    c["mask_keep"] = np.triu(np.ones((128, 128), dtype=np.float32))
    sh = np.zeros((128, 4 * 128), dtype=np.float32)
    shp = np.zeros((128, 4 * 128), dtype=np.float32)
    for j in range(4):
        for b in range(128):
            if b + j < 128:
                sh[b, 128 * j + b + j] = 1.0
            if b + j - 128 >= 0:
                shp[b, 128 * j + b + j - 128] = 1.0
    c["shift4"] = sh
    c["shiftprev4"] = shp
    c["ones_col"] = np.ones((128, 1), dtype=np.float32)
    c["ones_row"] = np.ones((1, 128), dtype=np.float32)
    rep = np.zeros((32, 128), dtype=np.float32)
    for p in range(128):
        rep[p // 32, p] = 1.0
    c["rep32"] = rep
    s2 = np.zeros((65, 128), dtype=np.float32)
    s2[0, 1:33] = 1.0
    s2[64, 65:97] = 1.0
    c["s2_65"] = s2
    bd = np.zeros((128, 4), dtype=np.float32)
    for j in range(4):
        bd[32 * j:32 * (j + 1), j] = 1.0
    c["blockdiag4"] = bd
    ind = np.zeros((8, 8 * 128), dtype=np.float32)
    for e in range(8):
        ind[e, e * 128:(e + 1) * 128] = 1.0
    c["ind8"] = ind
    return c


def _prep_weights(inp):
    w = {}
    T = lambda a: np.ascontiguousarray(np.asarray(a, np.float32).T)
    A = lambda a: np.ascontiguousarray(np.asarray(a, np.float32))
    w["uv"] = np.ascontiguousarray(np.concatenate(
        [np.asarray(inp["L0_attn_u"], np.float32),
         np.asarray(inp["L0_attn_v"], np.float32)], axis=1))
    w["projwT"] = T(inp["L0_attn_proj_w"])
    w["mlpw1"] = A(inp["L0_attn_mlp_w1"])
    w["mlpw2"] = A(inp["L0_attn_mlp_w2"])
    w["wup_nat"] = A(inp["L0_ffn_w_up"])
    w["wgate_nat"] = A(inp["L0_ffn_w_gate"])
    w["wdown_nat"] = A(inp["L0_ffn_w_down"])
    w["wupT"] = T(inp["L0_ffn_w_up"])
    w["wgateT"] = T(inp["L0_ffn_w_gate"])
    w["wdownT"] = T(inp["L0_ffn_w_down"])
    w["wqT"] = T(inp["L1_attn_Wq"])
    w["wkT"] = T(inp["L1_attn_Wk"])
    w["wvT"] = T(inp["L1_attn_Wv"])
    woT = np.asarray(inp["L1_attn_Wo"], np.float32).T
    wop = np.zeros((8 * 128, 1024), dtype=np.float32)
    for p in range(8):
        wop[p * 128 + 1:p * 128 + 33] = woT[64 * p:64 * p + 32]
        wop[p * 128 + 65:p * 128 + 97] = woT[64 * p + 32:64 * p + 64]
    w["woTp"] = wop
    w["router_w"] = A(inp["L1_ffn_router"])
    w["eg"] = A(inp["L1_ffn_experts_gate"])
    w["eu"] = A(inp["L1_ffn_experts_up"])
    w["ed"] = A(inp["L1_ffn_experts_down"])
    w["sg"] = A(inp["L1_ffn_shared_gate"])
    w["su"] = A(inp["L1_ffn_shared_up"])
    w["sd"] = A(inp["L1_ffn_shared_down"])
    fmc = lambda v, n: np.ascontiguousarray(np.asarray(v, np.float32).reshape(n, 128).T)
    w["projb"] = fmc(inp["L0_attn_proj_b"], 8)
    w["n1w"] = fmc(inp["L0_attn_norm1_w"], 8)
    w["n2w"] = fmc(inp["L0_attn_norm2_w"], 8)
    w["mlpb1"] = fmc(inp["L0_attn_mlp_b1"], 32)
    w["mlpb2"] = fmc(inp["L0_attn_mlp_b2"], 8)
    w["bo"] = fmc(inp["L1_attn_bo"], 8)
    pr = np.zeros((1, 64), dtype=np.float32)
    pr[0, 0:16] = np.asarray(inp["L0_attn_decay_logit"], np.float32)
    pr[0, 16:32] = np.asarray(inp["L0_attn_alpha_logit"], np.float32)
    pr[0, 32] = np.float32(np.asarray(inp["L0_attn_k_base_gate_logit"]))
    pr[0, 33:37] = np.asarray(inp["L0_attn_k_base_kernel"], np.float32)
    w["params"] = pr
    return w



_ENG_SELF = {"vector": "sv", "scalar": "ss", "tensor": "sm", "gpsimd": "sg"}


class E:

    def __init__(self, c, name, real):
        self.c = c
        self.name = name
        self.real = real
        self.selfname = _ENG_SELF.get(name)
        self.pending = None

    @property
    def cnt(self):
        return self.c.ecnt.get(self.name, 0)

    def wait(self, sem_name, val):
        if self.real is not None and val > 0:
            self.real.wait_ge(self.c.sem[sem_name], val)

    def waitm(self, key):
        m = self.c.marks.get(key)
        if m is not None:
            self.wait(m[0], m[1])

    def wait_stream_all(self, stream):
        st = self.c.streams[stream]
        for sn in st["sems"]:
            v = self.c.dcnt.get(sn, 0)
            if v:
                self.wait(sn, v)

    def dma(self, out, in_, stream="ld", **kw):
        st = self.c.streams[stream]
        k = st["n"]
        sem_name = st["sems"][k % len(st["sems"])]
        prev = (k // len(st["sems"])) * 16
        st["n"] = k + 1
        self.c.dcnt[sem_name] = prev + 16
        if self.real is not None:
            if prev > 0:
                self.real.wait_ge(self.c.sem[sem_name], prev)
            self.real.dma_start(out=out, in_=in_, **kw).then_inc(
                self.c.sem[sem_name], 16)
        return (sem_name, prev + 16)

    def dma_t(self, out, in_, stream="ld"):
        st = self.c.streams[stream]
        k = st["n"]
        sem_name = st["sems"][k % len(st["sems"])]
        prev = (k // len(st["sems"])) * 16
        st["n"] = k + 1
        self.c.dcnt[sem_name] = prev + 16
        if self.real is not None:
            if prev > 0:
                self.real.wait_ge(self.c.sem[sem_name], prev)
            self.real.dma_start(out=out, in_=in_, transpose=True).then_inc(
                self.c.sem[sem_name], 16)
        return (sem_name, prev + 16)

    def collective_max(self, ins_ap, outs_ap):
        self.c.dcnt["cc"] += 1
        if self.real is not None:
            self.real.collective_compute(
                "AllReduce", ALU.max, replica_groups=[list(range(8))],
                ins=[ins_ap], outs=[outs_ap]).then_inc(self.c.sem["cc"], 1)
        return ("cc", self.c.dcnt["cc"])

    def __getattr__(self, op):
        if op.startswith("_"):
            raise AttributeError(op)

        if self.name == "tensor" and PE_LAZY:
            def emit_pe(*args, **kw):
                if self.real is not None:
                    self.pending = getattr(self.real, op)(*args, **kw)
                else:
                    self.pending = True
                return None
            return emit_pe

        def emit(*args, **kw):
            cur = self.c.ecnt[self.name]
            self.c.ecnt[self.name] = cur + 1
            if self.real is not None:
                if cur > 0:
                    self.real.wait_ge(self.c.sem[self.selfname], cur)
                getattr(self.real, op)(*args, **kw).then_inc(
                    self.c.sem[self.selfname], 1)
            return cur + 1
        return emit


class Ctx:
    def __init__(self, nc, stack, debug):
        self.nc = nc
        self.stack = stack
        self.debug = set(debug)
        self.t = {}
        self.sem = {}
        self.ecnt = {"vector": 0, "scalar": 0, "tensor": 0, "gpsimd": 0}
        self.dcnt = {"cc": 0}
        self.streams = {}
        self.marks = {}

    def add_sems(self, *names):
        for n in names:
            self.sem[n] = self.stack.enter_context(self.nc.semaphore(n))

    def add_stream(self, name, depth):
        sems = []
        for i in range(depth):
            sn = "D_%s%d" % (name, i)
            self.sem[sn] = self.stack.enter_context(self.nc.semaphore(sn))
            self.dcnt[sn] = 0
            sems.append(sn)
        self.streams[name] = {"sems": sems, "n": 0}

    def sb(self, name, shape, dtype):
        h = self.stack.enter_context(self.nc.sbuf_tensor(name, list(shape), dtype))
        self.t[name] = h
        return h

    def mark(self, key, eng):
        if eng.name == "tensor" and PE_LAZY:
            self.flush_pe(eng)
        self.marks[key] = (eng.selfname, self.ecnt[eng.name])

    def flush_pe(self, eng):
        if eng.pending is not None:
            self.ecnt["tensor"] += 1
            if eng.real is not None:
                eng.pending.then_inc(self.sem["sm"], 1)
            eng.pending = None

    def mark_tok(self, key, tok):
        self.marks[key] = tok

    def stage(self, name, bodies, barrier=True):
        snap_e = dict(self.ecnt)
        snap_d = dict(self.dcnt)
        snap_s = {k: dict(v) for k, v in self.streams.items()}

        def wrap(fn, en):
            def run(eng):
                if barrier:
                    for xn, sn in _ENG_SELF.items():
                        eng.wait(sn, snap_e[xn])
                    for sn, v in snap_d.items():
                        eng.wait(sn, v)
                fn(eng)
            return run

        for en, fn in bodies:
            wrap(fn, en)(E(self, en, None))
        count_e = dict(self.ecnt)
        count_d = dict(self.dcnt)
        self.ecnt = dict(snap_e)
        self.dcnt = dict(snap_d)
        self.streams = {k: dict(v) for k, v in snap_s.items()}
        with self.nc.Block(name) as blk:
            for en, fn in bodies:
                getattr(blk, en)(
                    lambda real, fn=fn, en=en: wrap(fn, en)(E(self, en, real)))
        assert self.ecnt == count_e, (name, self.ecnt, count_e)
        assert self.dcnt == count_d, (name, self.dcnt, count_d)



def build_nc(debug=()):
    nc = bass.Bass("TRN2", target_bir_lowering=False)
    stack = contextlib.ExitStack()
    c = Ctx(nc, stack, debug)

    def din(name, shape, dtype=F32):
        c.t[name] = nc.dram_tensor(name, list(shape), dtype, kind="ExternalInput").ap()

    def dint(name, shape, dtype=F32):
        kind = "ExternalOutput" if name in c.debug else "Internal"
        c.t[name] = nc.dram_tensor(name, list(shape), dtype, kind=kind).ap()

    din("x_fm", [D, NTOK])
    for nm, sh in (("uv", [D, 32]), ("projwT", [D, D]), ("mlpw1", [D, DH]),
                   ("mlpw2", [DH, D]), ("wup_nat", [DI, D]), ("wgate_nat", [DI, D]),
                   ("wdown_nat", [D, DI]), ("wupT", [D, DI]), ("wgateT", [D, DI]),
                   ("wdownT", [DI, D]), ("wqT", [512, 512]), ("wkT", [512, 512]),
                   ("wvT", [512, 512]), ("woTp", [1024, 1024]), ("router_w", [D, 8]),
                   ("eg", [8, D, 512]), ("eu", [8, D, 512]), ("ed", [8, 512, D]),
                   ("sg", [D, D]), ("su", [D, D]), ("sd", [D, D]),
                   ("projb", [128, 8]), ("n1w", [128, 8]), ("n2w", [128, 8]),
                   ("mlpb1", [128, 32]), ("mlpb2", [128, 8]), ("bo", [128, 8]),
                   ("params", [1, 64]), ("ident", [128, 128]),
                   ("rampm", [128, 7 * 128]), ("ramp127_col", [128, 1]),
                   ("mask_keep", [128, 128]), ("shift4", [128, 512]),
                   ("shiftprev4", [128, 512]), ("ones_col", [128, 1]),
                   ("ones_row", [1, 128]), ("rep32", [32, 128]),
                   ("s2_65", [65, 128]), ("blockdiag4", [128, 4]),
                   ("ind8", [8, 1024])):
        din(nm, sh)

    c.t["out_fm"] = nc.dram_tensor("out_fm", [D, NTOK], F32, kind="ExternalOutput").ap()

    dint("xn_sp", [D, NTOK])
    dint("htm_sp", [NTOK, D], BF16)
    dint("h2_sp", [D, NTOK])
    dint("s0_sp", [D, NTOK])
    dint("g0_sp", [D, NTOK])
    dint("x1_sp", [D, NTOK])
    dint("xn1_sp", [D, NTOK], BF16)
    dint("s1_sp", [D, NTOK])
    dint("sh_sp", [D, NTOK])
    dint("upraw_sp", [DI, NTOK])
    dint("gateraw_sp", [DI, NTOK])
    dint("t_sp", [DI, NTOK])
    dint("z_sp", [DI, NTOK])
    dint("wqup_sp", [DI, D], BF16)
    dint("wqgate_sp", [DI, D], BF16)
    dint("wqdown_sp", [D, DI], BF16)
    dint("xn1f_sp", [D, NTOK])
    dint("he_sp", [4096, NTOK], BF16)
    dint("cc_in", [1, 16])
    dint("cc_out", [1, 16])
    dint("mixer_dbg", [D, NTOK], BF16)

    c.sb("ident_b", [128, 128], BF16)
    c.sb("ident_f", [128, 128], F32)
    c.sb("onescol_b", [128, 1], BF16)
    c.sb("onesrow_f", [1, 128], F32)
    c.sb("n1w_s", [128, 8], F32)
    c.sb("n2w_s", [128, 8], F32)
    c.sb("projb_s", [128, 8], F32)
    c.sb("mlpb1_s", [128, 32], F32)
    c.sb("mlpb2_s", [128, 8], F32)
    c.sb("bo_s", [128, 8], F32)
    c.sb("scol", [128, 16], F32)
    c.sb("mx_acc", [128, 8], F32)
    c.sb("ccv", [1, 32], F32)

    c.add_sems("cc", "sv", "ss", "sm", "sg")
    c.add_stream("ld", 3)
    c.add_stream("spx", 2)
    c.add_stream("sph", 2)
    c.add_stream("w", 2)
    c.add_stream("out", 2)

    with contextlib.ExitStack() as l0st:
        for nm, sh, dt_ in (("maskk", [128, 128], F32), ("bandA", [128, 128], BF16),
                            ("bandP", [128, 128], BF16), ("tb_qd", [128, 128], F32),
                            ("tb_qh", [128, 128], F32), ("tb_kd", [128, 128], F32),
                            ("tb_kh", [128, 128], F32), ("tb_kh32", [128, 128], F32),
                            ("tb_kh64", [128, 128], F32), ("tb_q128", [128, 128], F32),
                            ("tb_k128tm", [128, 16], F32), ("gam128", [128, 1], F32),
                            ("uv_s", [128, 8 * 32], BF16),
                            ("qk_tm", [128, NT128 * 32], F32),
                            ("prm", [1, 64], F32), ("srow", [1, 128], F32)):
            c.t[nm] = l0st.enter_context(nc.sbuf_tensor(nm, sh, dt_))
        _prep(c)
        _ln0(c)
        _coll_scales(c, [(0, 0, 1, 1.0)], "ln0.maxdone", "cs0")
        _decay(c)
    _mlp(c)
    _ffn_a(c)
    _coll_scales(c, [(1, 2, 3, 1.0), (2, 4, 5, 1.0)], "fa.red.5.15", "cs1")
    _ffn_b(c)
    _coll_scales(c, [(3, 6, 7, 65536.0)], "fb.t.47", "cs2")
    _ffn_c(c)
    _coll_scales(c, [(4, 8, 9, 1.0)], "fc.z.47", "cs3")
    _ffn_d(c)
    _gate_ln1(c)
    _attn1(c)
    _sharedffn(c)
    _moe_a(c)
    _moe_b(c)

    return nc, c



def _prep(c):
    t = c.t
    with (
        c.nc.sbuf_tensor("rampm_s", [128, 7 * 128], F32) as rampm_s,
        c.nc.sbuf_tensor("r127c", [128, 1], F32) as r127c,
        c.nc.sbuf_tensor("sh_s", [128, 512], F32) as sh_s,
        c.nc.sbuf_tensor("shp_s", [128, 512], F32) as shp_s,
        c.nc.sbuf_tensor("ptmp", [128, 384], F32) as ptmp,
        c.nc.psum_tensor("pprep", [128, 512], F32) as pprep,
    ):
        _prep_body(c, rampm_s, r127c, sh_s, shp_s, ptmp, pprep)


def _prep_body(c, rampm_s, r127c, sh_s, shp_s, ptmp, pprep):
    t = c.t

    def body_sync(e):
        for dst, src in ((t["ident_f"][:, :], t["ident"]),
                         (t["maskk"][:, :], t["mask_keep"]),
                         (t["onesrow_f"][:, :], t["ones_row"]),
                         (t["n1w_s"][:, :], t["n1w"]),
                         (t["n2w_s"][:, :], t["n2w"]),
                         (t["projb_s"][:, :], t["projb"]),
                         (t["mlpb1_s"][:, :], t["mlpb1"]),
                         (t["mlpb2_s"][:, :], t["mlpb2"]),
                         (t["bo_s"][:, :], t["bo"]),
                         (t["prm"][:, :], t["params"]),
                         (rampm_s[:, :], t["rampm"]),
                         (r127c[:, :], t["ramp127_col"]),
                         (sh_s[:, :], t["shift4"]),
                         (shp_s[:, :], t["shiftprev4"])):
            tok = e.dma(dst, src, stream="ld")
        c.mark_tok("prep.ld", tok)

    def body_gp(e):
        e.dma(t["ident_b"][:, :], t["ident"], stream="w")
        e.dma(t["onescol_b"][:, :], t["ones_col"], stream="w")
        tok = e.dma(t["uv_s"][:, :].rearrange("p (ki o) -> p ki o", ki=8),
                    t["uv"].rearrange("(ki p) o -> p ki o", p=128), stream="w")
        c.mark_tok("prep.ldw", tok)

    def body_scalar(e):
        e.wait_stream_all("ld")
        e.activation(ptmp[0:1, 0:33], t["prm"][0:1, 0:33], AF.Sigmoid)
        c.mark("prep.sig", e)
        e.waitm("prep.gam")
        e.activation(t["srow"][0:1, 16:32], t["srow"][0:1, 0:16], AF.Ln)
        c.mark("prep.lng", e)
        e.waitm("prep.cols")
        e.activation(t["gam128"][:, 0:1], t["scol"][:, 1:2], AF.Exp, scale=128.0)
        for k, nm in enumerate(("tb_qd", "tb_qh", "tb_kd", "tb_kh", "tb_kh32",
                                "tb_kh64", "tb_q128")):
            e.activation(t[nm][:, :], rampm_s[:, 128 * k:128 * (k + 1)],
                         AF.Exp, scale=t["scol"][:, 1:2])
        c.mark("prep.tables", e)
        e.waitm("prep.lngb")
        e.activation(t["tb_k128tm"][:, :], pprep[:, 16:32], AF.Exp,
                     scale=r127c[:, 0:1])
        c.mark("prep.k128", e)

    def body_vector(e):
        e.memset(t["ccv"][0:1, :], 0.0)
        e.waitm("prep.sig")
        e.tensor_scalar(t["srow"][0:1, 0:16], ptmp[0:1, 0:16], 0.15, 1.0,
                        ALU.max, ALU.min)
        c.mark("prep.gam", e)
        e.waitm("prep.lng")
        for b in range(4):
            e.tensor_copy(ptmp[0:1, 64 + 32 * b:80 + 32 * b], t["srow"][0:1, 16:32])
            e.memset(ptmp[0:1, 80 + 32 * b:96 + 32 * b], 0.0)
            e.tensor_copy(ptmp[0:1, 192 + 32 * b:208 + 32 * b], ptmp[0:1, 16:32])
            e.memset(ptmp[0:1, 208 + 32 * b:224 + 32 * b], 0.0)
        c.mark("prep.strips", e)
        e.waitm("prep.colmm")
        e.tensor_copy(t["scol"][:, 1:2], pprep[:, 0:1])
        e.tensor_copy(t["scol"][:, 0:1], pprep[:, 1:2])
        c.mark("prep.cols", e)
        e.waitm("prep.tables")
        for nm in ("tb_qd", "tb_qh", "tb_q128"):
            e.tensor_scalar(t[nm][:, :], t[nm][:, :], t["scol"][:, 0:1],
                            None, ALU.mult)
        c.mark("prep.fold", e)
        e.tensor_scalar(t["srow"][0:1, 32:36], t["prm"][0:1, 33:37],
                        ptmp[0:1, 32:33], None, ALU.mult)
        c.mark("prep.gk", e)
        e.waitm("prep.gkmm")
        e.tensor_copy(t["scol"][:, 4:8], pprep[:, 32:36])
        for (src, dst) in ((sh_s, t["bandA"]), (shp_s, t["bandP"])):
            e.tensor_scalar(ptmp[:, 0:128], src[:, 0:128], t["scol"][:, 4:5],
                            None, ALU.mult)
            for j in range(1, 4):
                e.tensor_scalar(ptmp[:, 128:256], src[:, 128 * j:128 * (j + 1)],
                                t["scol"][:, 4 + j:5 + j], None, ALU.mult)
                e.tensor_add(ptmp[:, 0:128], ptmp[:, 0:128], ptmp[:, 128:256])
            e.tensor_copy(dst[:, :], ptmp[:, 0:128])
        c.mark("prep.bands", e)

    def body_tensor(e):
        e.waitm("prep.strips")
        e.wait_stream_all("w")
        e.matmul(pprep[:, 0:1], ptmp[0:1, 64:192], t["onesrow_f"][0:1, 0:1],
                 start=True, stop=True)
        e.matmul(pprep[:, 1:2], ptmp[0:1, 192:320], t["onesrow_f"][0:1, 0:1],
                 start=True, stop=True)
        c.mark("prep.colmm", e)
        e.matmul(pprep[:, 16:32], t["onesrow_f"][0:1, :], t["srow"][0:1, 16:32],
                 start=True, stop=True)
        c.mark("prep.lngb", e)
        e.waitm("prep.gk")
        e.matmul(pprep[:, 32:36], t["onesrow_f"][0:1, :], t["srow"][0:1, 32:36],
                 start=True, stop=True)
        c.mark("prep.gkmm", e)

    c.stage("prep", [("sync", body_sync), ("gpsimd", body_gp),
                     ("scalar", body_scalar), ("vector", body_vector),
                     ("tensor", body_tensor)])



def _ln0(c):
    nc = c.nc
    with (
        nc.sbuf_tensor("xbuf", [128, 2 * 4096], F32) as xbuf,
        nc.sbuf_tensor("xbb", [128, 4096], BF16) as xbb,
        nc.sbuf_tensor("xsq", [128, 4096], BF16) as xsq,
        nc.sbuf_tensor("xnbuf", [128, 2 * 4096], F32) as xnbuf,
        nc.sbuf_tensor("hnb", [128, 4096], BF16) as hnb,
        nc.sbuf_tensor("htmb", [128, 4096], BF16) as htmb,
        nc.sbuf_tensor("rows", [1, 10 * 512], F32) as rows,
        nc.sbuf_tensor("bcast", [128, 3 * 512], F32) as bcast,
        nc.sbuf_tensor("qk_sb", [32, 512], F32) as qk_sb,
        nc.sbuf_tensor("gtmp", [128, 9], F32) as gtmp,
        nc.psum_tensor("pstat", [1, 512], F32) as pstat,
        nc.psum_tensor("pbc", [128, 3 * 512], F32) as pbc,
        nc.psum_tensor("pqk", [32, 512], F32) as pqk,
        nc.psum_tensor("pqt", [128, 512], F32) as pqt,
        nc.psum_tensor("ptr", [128, 2 * 1024], BF16) as ptr,
    ):
        _ln0_body(c, xbuf, xbb, xsq, xnbuf, hnb, htmb, rows, bcast, qk_sb,
                  gtmp, pstat, pbc, pqk, pqt, ptr)


def _ln0_body(c, xbuf, xbb, xsq, xnbuf, hnb, htmb, rows, bcast, qk_sb, gtmp,
              pstat, pbc, pqk, pqt, ptr):
    t = c.t
    x_h = t["x_fm"].rearrange("(ki p) tok -> p ki tok", p=128)
    xn_h = t["xn_sp"].rearrange("(ki p) tok -> p ki tok", p=128)
    htm_h = t["htm_sp"].rearrange("(j p) d -> p j d", p=128)

    RW = {nm: slice(512 * i, 512 * (i + 1)) for i, nm in enumerate(
        ("m", "e2", "var", "vpe", "sd", "tmp", "q", "s2", "rs", "rrs"))}
    K = lambda nm, ci: "ln0.%s.%d" % (nm, ci)

    def body_sync(e):
        for it in range(NC512 + 1):
            if it >= 1:
                ci = it - 1
                e.waitm(K("apply", ci))
                xnb = xnbuf[:, (ci % 2) * 4096:(ci % 2 + 1) * 4096]
                tok = e.dma(xn_h[:, :, 512 * ci:512 * (ci + 1)],
                            xnb.rearrange("p (ki t) -> p ki t", ki=8),
                            stream="spx")
                c.mark_tok(K("spx", ci), tok)
                e.waitm(K("htm", ci))
                tok = e.dma(htm_h[:, 4 * ci:4 * (ci + 1), :],
                            htmb.rearrange("p (j d) -> p j d", j=4),
                            stream="sph")
                c.mark_tok(K("sph", ci), tok)
            if it < NC512:
                ci = it
                if ci >= 2:
                    e.waitm(K("apply", ci - 2))
                buf = xbuf[:, (ci % 2) * 4096:(ci % 2 + 1) * 4096]
                tok = e.dma(buf.rearrange("p (ki t) -> p ki t", ki=8),
                            x_h[:, :, 512 * ci:512 * (ci + 1)], stream="ld")
                c.mark_tok(K("ld", ci), tok)

    def body_vector(e):
        for ci in range(NC512):
            buf = xbuf[:, (ci % 2) * 4096:(ci % 2 + 1) * 4096]
            xnb = xnbuf[:, (ci % 2) * 4096:(ci % 2 + 1) * 4096]
            e.waitm(K("ld", ci))
            e.tensor_copy(xbb[:, :], buf)
            e.tensor_tensor(xsq[:, :], xbb[:, :], xbb[:, :], ALU.mult)
            c.mark(K("cast", ci), e)
            e.waitm(K("stats", ci))
            e.tensor_scalar(rows[0:1, RW["m"]], pstat[0:1, :], 1.0 / 1024.0,
                            None, ALU.mult)
            e.tensor_scalar(rows[0:1, RW["e2"]], pbc[0:1, 0:512], 1.0 / 1024.0,
                            None, ALU.mult)
            c.mark(K("statsfree", ci), e)
            e.tensor_tensor(rows[0:1, RW["tmp"]], rows[0:1, RW["m"]],
                            rows[0:1, RW["m"]], ALU.mult)
            e.tensor_tensor(rows[0:1, RW["var"]], rows[0:1, RW["e2"]],
                            rows[0:1, RW["tmp"]], ALU.subtract)
            e.tensor_scalar(rows[0:1, RW["vpe"]], rows[0:1, RW["var"]], 1e-5,
                            None, ALU.add)
            e.reciprocal(rows[0:1, RW["tmp"]], rows[0:1, RW["vpe"]])
            e.tensor_tensor(rows[0:1, RW["q"]], rows[0:1, RW["var"]],
                            rows[0:1, RW["tmp"]], ALU.mult)
            e.tensor_scalar(rows[0:1, RW["q"]], rows[0:1, RW["q"]], 1e-6,
                            None, ALU.add)
            c.mark(K("rows1", ci), e)
            e.waitm(K("sqrt", ci))
            e.reciprocal(rows[0:1, RW["sd"]], rows[0:1, RW["sd"]])
            e.tensor_tensor(rows[0:1, RW["tmp"]], rows[0:1, RW["sd"]],
                            rows[0:1, RW["sd"]], ALU.mult)
            e.tensor_tensor(rows[0:1, RW["tmp"]], rows[0:1, RW["tmp"]],
                            rows[0:1, RW["vpe"]], ALU.mult)
            e.tensor_scalar(rows[0:1, RW["tmp"]], rows[0:1, RW["tmp"]], -0.5,
                            1.5, ALU.mult, ALU.add)
            e.tensor_tensor(rows[0:1, RW["sd"]], rows[0:1, RW["sd"]],
                            rows[0:1, RW["tmp"]], ALU.mult)
            e.reciprocal(rows[0:1, RW["rs"]], rows[0:1, RW["s2"]])
            e.tensor_tensor(rows[0:1, RW["tmp"]], rows[0:1, RW["rs"]],
                            rows[0:1, RW["rs"]], ALU.mult)
            e.tensor_tensor(rows[0:1, RW["tmp"]], rows[0:1, RW["tmp"]],
                            rows[0:1, RW["q"]], ALU.mult)
            e.tensor_scalar(rows[0:1, RW["tmp"]], rows[0:1, RW["tmp"]], -0.5,
                            1.5, ALU.mult, ALU.add)
            e.tensor_tensor(rows[0:1, RW["rs"]], rows[0:1, RW["rs"]],
                            rows[0:1, RW["tmp"]], ALU.mult)
            e.tensor_tensor(rows[0:1, RW["rrs"]], rows[0:1, RW["sd"]],
                            rows[0:1, RW["rs"]], ALU.mult)
            c.mark(K("rows2", ci), e)
            e.waitm(K("bcmm", ci))
            if ci >= 2:
                e.waitm(K("spx", ci - 2))
            e.tensor_copy(bcast[:, 0:512], pbc[:, 0:512])
            e.tensor_copy(bcast[:, 512:1024], pbc[:, 512:1024])
            e.tensor_copy(bcast[:, 1024:1536], pbc[:, 1024:1536])
            c.mark(K("bcfree", ci), e)
            for ki in range(8):
                sl = slice(512 * ki, 512 * (ki + 1))
                e.tensor_tensor(buf[:, sl], buf[:, sl], bcast[:, 0:512],
                                ALU.subtract)
                e.tensor_tensor(xnb[:, sl], buf[:, sl], bcast[:, 512:1024],
                                ALU.mult)
                e.scalar_tensor_tensor(hnb[:, sl], buf[:, sl],
                                       t["n1w_s"][:, ki:ki + 1],
                                       bcast[:, 1024:1536], ALU.mult, ALU.mult)
            c.mark(K("apply", ci), e)
            e.tensor_reduce(gtmp[:, 8:9], xnb, AX.X, ALU.max,
                            apply_absolute_value=True)
            if ci == 0:
                e.tensor_copy(t["mx_acc"][:, 0:1], gtmp[:, 8:9])
            else:
                e.tensor_tensor(t["mx_acc"][:, 0:1], t["mx_acc"][:, 0:1],
                                gtmp[:, 8:9], ALU.max)
            c.mark(K("max", ci), e)
        c.mark("ln0.maxdone", e)

    def body_tensor(e):
        for ci in range(NC512):
            e.waitm(K("cast", ci))
            if ci >= 1:
                e.waitm(K("statsfree", ci - 1))
                e.waitm(K("bcfree", ci - 1))
            for ki in range(8):
                e.matmul(pstat[0:1, :], t["onescol_b"][:, 0:1],
                         xbb[:, 512 * ki:512 * (ki + 1)],
                         start=(ki == 0), stop=(ki == 7))
            for ki in range(8):
                e.matmul(pbc[0:1, 0:512], t["onescol_b"][:, 0:1],
                         xsq[:, 512 * ki:512 * (ki + 1)],
                         start=(ki == 0), stop=(ki == 7))
            c.mark(K("stats", ci), e)
            e.waitm(K("rows2", ci))
            e.matmul(pbc[:, 0:512], t["onesrow_f"][0:1, :], rows[0:1, RW["m"]],
                     start=True, stop=True)
            e.matmul(pbc[:, 512:1024], t["onesrow_f"][0:1, :],
                     rows[0:1, RW["sd"]], start=True, stop=True)
            e.matmul(pbc[:, 1024:1536], t["onesrow_f"][0:1, :],
                     rows[0:1, RW["rrs"]], start=True, stop=True)
            c.mark(K("bcmm", ci), e)
            e.waitm(K("apply", ci))
            if ci >= 1:
                e.waitm(K("qkevac", ci - 1))
            for ki in range(8):
                e.matmul(pqk[:, :], t["uv_s"][:, 32 * ki:32 * (ki + 1)],
                         hnb[:, 512 * ki:512 * (ki + 1)],
                         start=(ki == 0), stop=(ki == 7))
            c.mark(K("qkmm", ci), e)
            e.waitm(K("qkevac", ci))
            if ci >= 1:
                e.waitm(K("qktm", ci - 1))
            for j in range(4):
                e.transpose(pqt[:, 32 * j:32 * (j + 1)],
                            qk_sb[0:32, 128 * j:128 * (j + 1)],
                            t["ident_f"][0:32, 0:32])
            c.mark(K("qtr", ci), e)
            for j in range(4):
                g = 4 * ci + j
                if g >= 2:
                    gp = g - 2
                    e.waitm(K("htme.%d" % (gp % 4), gp // 4))
                for ki in range(8):
                    e.transpose(ptr[:, 1024 * (j % 2) + 128 * ki:
                                    1024 * (j % 2) + 128 * (ki + 1)],
                                hnb[:, 512 * ki + 128 * j:512 * ki + 128 * (j + 1)],
                                t["ident_b"][:, :])
                c.mark(K("htr.%d" % j, ci), e)

    def body_scalar(e):
        for ci in range(NC512):
            e.waitm(K("rows1", ci))
            e.activation(rows[0:1, RW["sd"]], rows[0:1, RW["vpe"]], AF.Ln)
            e.activation(rows[0:1, RW["sd"]], rows[0:1, RW["sd"]], AF.Exp,
                         scale=0.5)
            e.activation(rows[0:1, RW["s2"]], rows[0:1, RW["q"]], AF.Ln)
            e.activation(rows[0:1, RW["s2"]], rows[0:1, RW["s2"]], AF.Exp,
                         scale=0.5)
            c.mark(K("sqrt", ci), e)
            e.waitm(K("qkmm", ci))
            e.copy(qk_sb[0:32, :], pqk[0:32, :])
            c.mark(K("qkevac", ci), e)
            e.waitm(K("qtr", ci))
            e.copy(t["qk_tm"][:, 128 * ci:128 * (ci + 1)], pqt[:, 0:128])
            c.mark(K("qktm", ci), e)
            for j in range(4):
                e.waitm(K("htr.%d" % j, ci))
                if ci >= 1 and j == 0:
                    e.waitm(K("sph", ci - 1))
                e.copy(htmb[:, 1024 * j:1024 * (j + 1)],
                       ptr[:, 1024 * (j % 2):1024 * (j % 2 + 1)])
                c.mark(K("htme.%d" % j, ci), e)
            c.mark(K("htm", ci), e)

    c.stage("ln0", [("sync", body_sync), ("vector", body_vector),
                    ("tensor", body_tensor), ("scalar", body_scalar)])




def _decay(c):
    nc = c.nc
    with contextlib.ExitStack() as st:
        pw_sb = st.enter_context(nc.sbuf_tensor("pw_sb", [128, 8 * 1024], BF16))
        htm_t = st.enter_context(nc.sbuf_tensor("htm_t", [128, 2 * 4 * 1024], BF16))
        mixer_sb = st.enter_context(nc.sbuf_tensor("mixer_sb", [128, 16 * 1024], BF16))
        at_sb = st.enter_context(nc.sbuf_tensor("at_sb", [128, 512], BF16))
        sqb = st.enter_context(nc.sbuf_tensor("sqb", [128, 128], F32))
        nrm = st.enter_context(nc.sbuf_tensor("nrm", [128, 48], F32))
        tmq = st.enter_context(nc.sbuf_tensor("tmq", [128, 128], F32))
        tmk = st.enter_context(nc.sbuf_tensor("tmk", [128, 128], F32))
        fmq = st.enter_context(nc.sbuf_tensor("fmq", [128, 128], F32))
        fmk = st.enter_context(nc.sbuf_tensor("fmk", [128, 128], F32))
        vq_d = st.enter_context(nc.sbuf_tensor("vq_d", [128, 128], BF16))
        vq_h = st.enter_context(nc.sbuf_tensor("vq_h", [128, 128], BF16))
        vq_128 = st.enter_context(nc.sbuf_tensor("vq_128", [128, 128], BF16))
        vk_d = st.enter_context(nc.sbuf_tensor("vk_d", [128, 128], BF16))
        vk_h = st.enter_context(nc.sbuf_tensor("vk_h", [128, 128], BF16))
        vk_h32 = st.enter_context(nc.sbuf_tensor("vk_h32", [128, 128], BF16))
        vk_h64 = st.enter_context(nc.sbuf_tensor("vk_h64", [128, 128], BF16))
        vk128 = st.enter_context(nc.sbuf_tensor("vk128", [128, 512], BF16))
        zpad = st.enter_context(nc.sbuf_tensor("zpad", [128, 128], BF16))
        s_all = st.enter_context(nc.sbuf_tensor("s_all", [128, 1024], F32))
        sb16 = st.enter_context(nc.sbuf_tensor("sb16", [128, 1024], BF16))
        xnld = st.enter_context(nc.sbuf_tensor("xnld", [128, 2 * 4096], F32))
        pjbuf = st.enter_context(nc.sbuf_tensor("pjbuf", [128, 512], F32))
        h2buf = st.enter_context(nc.sbuf_tensor("h2buf", [128, 2 * 4096], F32))
        pfm = st.enter_context(nc.psum_tensor("pfm", [128, 512], F32))
        pat = st.enter_context(nc.psum_tensor("pat", [128, 512], F32))
        pmx = st.enter_context(nc.psum_tensor("pmx", [128, 2 * 1024], F32))
        pds = st.enter_context(nc.psum_tensor("pds", [128, 1024], F32))
        _decay_body(c, pw_sb, htm_t, mixer_sb, at_sb, sqb, nrm, tmq, tmk, fmq, fmk, vq_d, vq_h, vq_128, vk_d, vk_h, vk_h32, vk_h64, vk128, zpad, s_all, sb16, xnld, pjbuf, h2buf, pfm, pat, pmx, pds)



def _decay_body(c, pw_sb, htm_t, mixer_sb, at_sb, sqb, nrm, tmq, tmk, fmq,
                fmk, vq_d, vq_h, vq_128, vk_d, vk_h, vk_h32, vk_h64, vk128,
                zpad, s_all, sb16, xnld, pjbuf, h2buf, pfm, pat, pmx, pds):
    t = c.t
    htm_h = t["htm_sp"].rearrange("(j p) d -> p j d", p=128)
    xn_h = t["xn_sp"].rearrange("(mo p) tok -> p mo tok", p=128)
    h2_h = t["h2_sp"].rearrange("(mo p) tok -> p mo tok", p=128)
    mxd_h = t["mixer_dbg"].rearrange("(dc p) tok -> p dc tok", p=128)
    K = lambda nm, i: "dk.%s.%d" % (nm, i)
    VAR_OFF = {1: (vq_h, vk_h), 2: (vq_h, vk_h32), 3: (vq_h, vk_h64)}

    def qk_ap(i):
        return t["qk_tm"][:, :].rearrange("p (j o) -> p j o", o=32)[
            :, i:64:16, :]

    def body_sync(e):
        for i in range(NT128 // 4):
            e.waitm("ln0.sph.%d" % ((48 + i) // 4))
            if i >= 2:
                e.waitm(K("dsmm", i - 1))
            tok = e.dma(htm_t[:, (i % 2) * 4096:(i % 2 + 1) * 4096]
                        .rearrange("p (b d) -> p b d", b=4),
                        htm_h[:, i:64:16, :], stream="ld")
            c.mark_tok(K("ld", i), tok)
            if i % 4 == 3:
                g = i // 4
                for b in range(NB):
                    if (g, b) >= (1, 0) if False else (g >= 1 or b >= 2):
                        e.waitm(K("h2add.%d" % ((b - 2) % 4),
                                  g - (1 if b < 2 else 0)))
                    tok = e.dma(xnld[:, (b % 2) * 4096:(b % 2 + 1) * 4096]
                                .rearrange("p (mo t) -> p mo t", mo=8),
                                xn_h[:, :, 2048 * b + 512 * g:
                                     2048 * b + 512 * (g + 1)], stream="spx")
                    c.mark_tok(K("xnl.%d" % b, g), tok)
                    e.waitm(K("h2add.%d" % b, g))
                    tok = e.dma(h2_h[:, :, 2048 * b + 512 * g:
                                     2048 * b + 512 * (g + 1)],
                                h2buf[:, (b % 2) * 4096:(b % 2 + 1) * 4096]
                                .rearrange("p (mo t) -> p mo t", mo=8),
                                stream="sph")
                    c.mark_tok(K("h2s.%d" % b, g), tok)
            if "mixer_dbg" in c.debug:
                e.waitm(K("mxall", i))
                for b in range(NB):
                    e.dma(mxd_h[:, :, 2048 * b + 128 * i:2048 * b + 128 * (i + 1)],
                          mixer_sb[:, ((i % 4) * 4 + b) * 1024:
                                   ((i % 4) * 4 + b + 1) * 1024]
                          .rearrange("p (dc t) -> p dc t", dc=8), stream="out")

    def body_gp(e):
        tok = e.dma(pw_sb[:, :].rearrange("p (ki o) -> p ki o", ki=8),
                    t["projwT"].rearrange("(ki p) o -> p ki o", p=128),
                    stream="w")
        c.mark_tok("dk.pw", tok)

    def body_vector(e):
        e.memset(s_all[:, :], 0.0)
        e.memset(sb16[:, :], 0.0)
        e.memset(tmq[:, :], 0.0)
        e.memset(tmk[:, :], 0.0)
        e.memset(vk128[:, :], 0.0)
        e.memset(zpad[:, :], 0.0)
        c.mark("dk.init", e)
        for i in range(NT128 // 4):
            e.waitm("ln0.qktm.%d" % NC512 if False else "ln0.qktm.15")
            if i >= 1:
                e.waitm(K("fmtr", i - 1))
            qk = qk_ap(i)
            e.tensor_tensor(sqb[:, :], qk, qk, ALU.mult)
            e.tensor_reduce(nrm[:, 0:8],
                            sqb[:, :].rearrange("p (g o) -> p g o", o=16),
                            AX.X, ALU.add)
            c.mark(K("ss", i), e)
            e.waitm(K("sqr", i))
            e.reciprocal(nrm[:, 16:24], nrm[:, 8:16])
            e.tensor_tensor(nrm[:, 24:32], nrm[:, 0:8], nrm[:, 16:24], ALU.mult)
            e.tensor_tensor(nrm[:, 24:32], nrm[:, 24:32], nrm[:, 8:16], ALU.add)
            e.tensor_scalar(nrm[:, 24:32], nrm[:, 24:32], 0.5, 1e-8,
                            ALU.mult, ALU.max)
            e.reciprocal(nrm[:, 32:40], nrm[:, 24:32])
            for b in range(NB):
                e.tensor_scalar(tmq[:, 32 * b:32 * b + 16],
                                sqb[:, 0:1].tensor[0:0, 0:0] if False else
                                qk_ap(i)[:, b, 0:16],
                                nrm[:, 32 + 2 * b:33 + 2 * b], None, ALU.mult)
                e.tensor_scalar(tmk[:, 32 * b:32 * b + 16],
                                qk_ap(i)[:, b, 16:32],
                                nrm[:, 33 + 2 * b:34 + 2 * b], None, ALU.mult)
            c.mark(K("tmq", i), e)
            e.waitm(K("fme", i))
            e.tensor_tensor(vq_d[:, :], fmq[:, :], t["tb_qd"][:, :], ALU.mult)
            e.tensor_tensor(vq_h[:, :], fmq[:, :], t["tb_qh"][:, :], ALU.mult)
            e.tensor_tensor(vq_128[:, :], fmq[:, :], t["tb_q128"][:, :], ALU.mult)
            e.tensor_tensor(vk_d[:, :], fmk[:, :], t["tb_kd"][:, :], ALU.mult)
            e.tensor_tensor(vk_h[:, :], fmk[:, :], t["tb_kh"][:, :], ALU.mult)
            e.tensor_tensor(vk_h32[:, :], fmk[:, :], t["tb_kh32"][:, :], ALU.mult)
            e.tensor_tensor(vk_h64[:, :], fmk[:, :], t["tb_kh64"][:, :], ALU.mult)
            for b in range(NB):
                e.tensor_tensor(vk128[:, 128 * b + 32 * b:128 * b + 32 * b + 16],
                                tmk[:, 32 * b:32 * b + 16], t["tb_k128tm"][:, :],
                                ALU.mult)
            c.mark(K("var", i), e)
            e.waitm(K("atmm", i))
            for b in range(NB):
                e.scalar_tensor_tensor(at_sb[:, 128 * b:128 * (b + 1)],
                                       pat[:, 128 * b:128 * (b + 1)], 1.0,
                                       t["maskk"][:, :], ALU.mult, ALU.mult)
            c.mark(K("atev", i), e)
            e.waitm(K("dsmm", i))
            for b in range(NB):
                e.scalar_tensor_tensor(s_all[32 * b:32 * b + 16, :],
                                       s_all[32 * b:32 * b + 16, :],
                                       t["gam128"][32 * b:32 * b + 16, 0:1],
                                       pds[32 * b:32 * b + 16, :],
                                       ALU.mult, ALU.add)
            for b in range(NB):
                e.tensor_copy(sb16[32 * b:32 * b + 16, :],
                              s_all[32 * b:32 * b + 16, :])
            c.mark(K("st", i), e)
            if i % 4 == 3:
                g = i // 4
                for b in range(NB):
                    if b >= 2:
                        e.waitm(K("h2s.%d" % (b - 2), g))
                    elif g >= 1:
                        e.waitm(K("h2s.%d" % (b + 2), g - 1))
                    for mo in range(8):
                        e.waitm(K("pje.%d.%d" % (b, mo), g))
                        e.waitm(K("xnl.%d" % b, g))
                        e.tensor_tensor(
                            h2buf[:, (b % 2) * 4096 + 512 * mo:
                                  (b % 2) * 4096 + 512 * (mo + 1)],
                            pjbuf[:, :],
                            xnld[:, (b % 2) * 4096 + 512 * mo:
                                 (b % 2) * 4096 + 512 * (mo + 1)], ALU.add)
                        c.mark(K("h2a.%d.%d" % (b, mo), g), e)
                    c.mark(K("h2add.%d" % b, g), e)

    def body_tensor(e):
        for i in range(NT128 // 4):
            e.waitm(K("tmq", i))
            if i >= 1:
                e.waitm(K("fme", i - 1))
            e.transpose(pfm[:, 0:128], tmq[:, :], t["ident_f"][:, :])
            e.transpose(pfm[:, 128:256], tmk[:, :], t["ident_f"][:, :])
            c.mark(K("fmtr", i), e)
            e.waitm(K("var", i))
            if i >= 1:
                e.waitm(K("atev", i - 1))
            for b in range(NB):
                e.matmul(pat[:, 128 * b:128 * (b + 1)], t["ident_b"][:, :],
                         t["bandA"][:, :], start=(b == 0), stop=False)
            for b in range(NB):
                for ca in range(4):
                    for cb in range(ca + 1):
                        if cb == ca:
                            lh, rh = vk_d, vq_d
                        else:
                            rh, lh = VAR_OFF[ca - cb][0], VAR_OFF[ca - cb][1]
                        e.matmul(pat[32 * cb:32 * (cb + 1),
                                     128 * b + 32 * ca:128 * b + 32 * (ca + 1)],
                                 lh[32 * b:32 * b + 16, 32 * cb:32 * (cb + 1)],
                                 rh[32 * b:32 * b + 16, 32 * ca:32 * (ca + 1)],
                                 start=False, stop=False,
                                 tile_position=(32 * b, 32 * cb))
            e.matmul(pat[:, 0:512], zpad[:, :], vk128[:, :],
                     start=False, stop=True)
            c.mark(K("atmm", i), e)
            e.waitm(K("atev", i))
            e.waitm(K("ld", i))
            if i >= 1:
                e.waitm(K("st", i - 1))
            for b in range(NB):
                if b >= 2:
                    e.waitm(K("mxe.%d" % (b - 2), i))
                elif i >= 1:
                    e.waitm(K("mxe.%d" % (b + 2), i - 1))
                elif i % 4 == 0 and i >= 4:
                    pass
                pm = pmx[:, (b % 2) * 1024:(b % 2 + 1) * 1024]
                ht = htm_t[:, (i % 2) * 4096 + 1024 * b:
                           (i % 2) * 4096 + 1024 * (b + 1)]
                htp = htm_t[:, ((i - 1) % 2) * 4096 + 1024 * b:
                            ((i - 1) % 2) * 4096 + 1024 * (b + 1)]
                for dc in range(8):
                    e.matmul(pm[:, 128 * dc:128 * (dc + 1)],
                             ht[:, 128 * dc:128 * (dc + 1)],
                             at_sb[:, 128 * b:128 * (b + 1)],
                             start=(dc % 4 == 0), stop=False)
                if i >= 1:
                    for dc in range(8):
                        e.matmul(pm[:, 128 * dc:128 * (dc + 1)],
                                 htp[:, 128 * dc:128 * (dc + 1)],
                                 t["bandP"][:, :], start=False, stop=False)
                for dc in range(8):
                    e.matmul(pm[:, 128 * dc:128 * (dc + 1)],
                             sb16[32 * b:32 * b + 16, 128 * dc:128 * (dc + 1)],
                             vq_128[32 * b:32 * b + 16, :],
                             start=False, stop=(dc % 4 == 3),
                             tile_position=(32 * b, 0))
                c.mark(K("mxmm", i) + ".%d" % b, e)
            if i >= 1:
                e.waitm(K("st", i - 1))
            for b in range(NB):
                for hh in range(2):
                    e.matmul(pds[:, 512 * hh:512 * (hh + 1)],
                             vk128[:, 128 * b:128 * (b + 1)],
                             htm_t[:, (i % 2) * 4096 + 1024 * b + 512 * hh:
                                   (i % 2) * 4096 + 1024 * b + 512 * (hh + 1)],
                             start=(b == 0), stop=(b == NB - 1))
            c.mark(K("dsmm", i), e)
            if i % 4 == 3:
                g = i // 4
                e.waitm(K("mxall", i))
                e.waitm("dk.pw")
                for b in range(NB):
                    for mo in range(8):
                        if not (b == 0 and mo == 0):
                            e.waitm(K("pje.%d.%d" % ((b * 8 + mo - 2) // 8,
                                                     (b * 8 + mo - 2) % 8), g))
                        pp = pmx[:, (mo % 2) * 512:(mo % 2) * 512 + 512]
                        for ki in range(8):
                            rhs = mixer_sb[:, :].rearrange(
                                "p (s o) -> p s o", o=1024)[
                                :, b:16:4, 128 * ki:128 * (ki + 1)]
                            e.matmul(pp, pw_sb[:, 1024 * ki + 128 * mo:
                                               1024 * ki + 128 * (mo + 1)],
                                     rhs, start=(ki == 0), stop=(ki == 7))
                        c.mark(K("pj.%d.%d" % (b, mo), g), e)

    def body_scalar(e):
        for i in range(NT128 // 4):
            e.waitm(K("ss", i))
            e.activation(nrm[:, 8:16], nrm[:, 0:8], AF.Ln)
            e.activation(nrm[:, 8:16], nrm[:, 8:16], AF.Exp, scale=0.5)
            c.mark(K("sqr", i), e)
            e.waitm(K("fmtr", i))
            e.copy(fmq[:, :], pfm[:, 0:128])
            e.copy(fmk[:, :], pfm[:, 128:256])
            c.mark(K("fme", i), e)
            for b in range(NB):
                e.waitm(K("mxmm", i) + ".%d" % b)
                if i % 4 == 0 and i >= 4 and b == 0:
                    e.waitm(K("pj.3.7", i // 4 - 1))
                e.copy(mixer_sb[:, ((i % 4) * 4 + b) * 1024:
                                ((i % 4) * 4 + b + 1) * 1024],
                       pmx[:, (b % 2) * 1024:(b % 2 + 1) * 1024])
                c.mark(K("mxe.%d" % b, i), e)
            c.mark(K("mxall", i), e)
            if i % 4 == 3:
                g = i // 4
                for b in range(NB):
                    for mo in range(8):
                        e.waitm(K("pj.%d.%d" % (b, mo), g))
                        if not (b == 0 and mo == 0):
                            prev = b * 8 + mo - 1
                            e.waitm(K("h2a.%d.%d" % (prev // 8, prev % 8), g))
                        e.activation(pjbuf[:, :],
                                     pmx[:, (mo % 2) * 512:(mo % 2) * 512 + 512],
                                     AF.Identity, bias=t["projb_s"][:, mo:mo + 1])
                        c.mark(K("pje.%d.%d" % (b, mo), g), e)

    c.stage("decay", [("sync", body_sync), ("gpsimd", body_gp),
                      ("vector", body_vector), ("tensor", body_tensor),
                      ("scalar", body_scalar)])



def _mlp(c):
    nc = c.nc
    with contextlib.ExitStack() as st:
        w1_sb = st.enter_context(nc.sbuf_tensor("w1_sb", [128, 8 * 4096], BF16))
        w2_sb = st.enter_context(nc.sbuf_tensor("w2_sb", [128, 32 * 1024], BF16))
        h2ld = st.enter_context(nc.sbuf_tensor("h2ld", [128, 2 * 4096], F32))
        hbb = st.enter_context(nc.sbuf_tensor("hbb", [128, 4096], BF16))
        hid = st.enter_context(nc.sbuf_tensor("hid", [128, 32 * 512], BF16))
        mrows = st.enter_context(nc.sbuf_tensor("mrows", [1, 3 * 512], F32))
        pstat2 = st.enter_context(nc.psum_tensor("pstat2", [1, 512], F32))
        pbc2 = st.enter_context(nc.psum_tensor("pbc2", [128, 512], F32))
        pm1 = st.enter_context(nc.psum_tensor("pm1", [128, 2 * 512], F32))
        pm2 = st.enter_context(nc.psum_tensor("pm2", [128, 2 * 512], F32))
        _mlp_body(c, w1_sb, w2_sb, h2ld, hbb, hid, mrows,
                  pstat2, pbc2, pm1, pm2)


def _mlp_body(c, w1_sb, w2_sb, h2ld, hbb, hid, mrows,
              pstat2, pbc2, pm1, pm2):
    t = c.t
    h2_h = t["h2_sp"].rearrange("(mo p) tok -> p mo tok", p=128)
    s0_h = t["s0_sp"].rearrange("(mo p) tok -> p mo tok", p=128)
    K = lambda nm, ci: "mlp.%s.%d" % (nm, ci)
    RW = {"e2": slice(0, 512), "vpe": slice(0, 512), "sd": slice(512, 1024),
          "tmp": slice(1024, 1536)}

    def body_gp(e):
        tok = e.dma(w1_sb[:, :].rearrange("p (ki o) -> p ki o", ki=8),
                    t["mlpw1"].rearrange("(ki p) o -> p ki o", p=128), stream="w")
        c.mark_tok("mlp.w1", tok)
        tok = e.dma(w2_sb[:, :].rearrange("p (ki o) -> p ki o", ki=32),
                    t["mlpw2"].rearrange("(ki p) o -> p ki o", p=128), stream="w")
        c.mark_tok("mlp.w2", tok)
        for ci in range(NC512):
            buf = h2ld[:, (ci % 2) * 4096:(ci % 2 + 1) * 4096]
            e.waitm(K("ld", ci))
            if ci >= 1:
                e.waitm(K("hn2used", ci - 1))
            e.tensor_copy(hbb[:, :], buf)
            e.tensor_tensor(hbb[:, :], hbb[:, :], hbb[:, :], ALU.mult)
            c.mark(K("sq", ci), e)
            e.waitm(K("hn2", ci))

    def body_vector(e):
        for ci in range(NC512):
            e.waitm(K("stat", ci))
            e.tensor_scalar(mrows[0:1, RW["vpe"]], pstat2[0:1, :], 1.0 / 1024.0,
                            1e-6, ALU.mult, ALU.add)
            c.mark(K("statfree", ci), e)
            c.mark(K("rows1", ci), e)
            e.waitm(K("sqr", ci))
            e.reciprocal(mrows[0:1, RW["sd"]], mrows[0:1, RW["sd"]])
            e.tensor_tensor(mrows[0:1, RW["tmp"]], mrows[0:1, RW["sd"]],
                            mrows[0:1, RW["sd"]], ALU.mult)
            e.tensor_tensor(mrows[0:1, RW["tmp"]], mrows[0:1, RW["tmp"]],
                            mrows[0:1, RW["vpe"]], ALU.mult)
            e.tensor_scalar(mrows[0:1, RW["tmp"]], mrows[0:1, RW["tmp"]], -0.5,
                            1.5, ALU.mult, ALU.add)
            e.tensor_tensor(mrows[0:1, RW["sd"]], mrows[0:1, RW["sd"]],
                            mrows[0:1, RW["tmp"]], ALU.mult)
            c.mark(K("rows2", ci), e)
            e.waitm(K("bc", ci))
            e.waitm(K("ld", ci))
            if ci >= 1:
                e.waitm(K("hn2used", ci - 1))
            buf = h2ld[:, (ci % 2) * 4096:(ci % 2 + 1) * 4096]
            for ki in range(8):
                sl = slice(512 * ki, 512 * (ki + 1))
                e.scalar_tensor_tensor(hbb[:, sl], buf[:, sl],
                                       t["n2w_s"][:, ki:ki + 1], pbc2[:, :],
                                       ALU.mult, ALU.mult)
            c.mark(K("hn2", ci), e)
            c.mark(K("bce", ci), e)
            for mo in range(8):
                e.waitm(K("m2.%d" % mo, ci))
                e.scalar_tensor_tensor(buf[:, 512 * mo:512 * (mo + 1)],
                                       buf[:, 512 * mo:512 * (mo + 1)],
                                       t["mlpb2_s"][:, mo:mo + 1],
                                       pm2[:, (mo % 2) * 512:(mo % 2) * 512 + 512],
                                       ALU.add, ALU.add)
                c.mark(K("a2.%d" % mo, ci), e)
            c.mark(K("s0", ci), e)

    def body_tensor(e):
        for ci in range(NC512):
            e.waitm(K("sq", ci))
            if ci >= 1:
                e.waitm(K("statfree", ci - 1))
            for ki in range(8):
                e.matmul(pstat2[0:1, :], t["onescol_b"][:, 0:1],
                         hbb[:, 512 * ki:512 * (ki + 1)],
                         start=(ki == 0), stop=(ki == 7))
            c.mark(K("stat", ci), e)
            e.waitm(K("rows2", ci))
            if ci >= 1:
                e.waitm(K("bce", ci - 1))
            e.matmul(pbc2[:, :], t["onesrow_f"][0:1, :], mrows[0:1, RW["sd"]],
                     start=True, stop=True)
            c.mark(K("bc", ci), e)
            e.waitm(K("hn2", ci))
            e.waitm("mlp.w1")
            c.mark(K("hn2used", ci), e)
            for mo4 in range(32):
                if mo4 >= 2:
                    e.waitm(K("g1.%d" % (mo4 - 2), ci))
                elif ci >= 1 and mo4 == 0:
                    e.waitm(K("g1.%d" % 31, ci - 1))
                pp = pm1[:, (mo4 % 2) * 512:(mo4 % 2) * 512 + 512]
                for ki in range(8):
                    e.matmul(pp, w1_sb[:, 4096 * ki + 128 * mo4:
                                       4096 * ki + 128 * (mo4 + 1)],
                             hbb[:, 512 * ki:512 * (ki + 1)],
                             start=(ki == 0), stop=(ki == 7))
                c.mark(K("m1.%d" % mo4, ci), e)
            c.mark(K("hn2used", ci), e)
            e.waitm("mlp.w2")
            e.waitm(K("g1.%d" % 31, ci))
            for mo in range(8):
                if mo >= 2:
                    e.waitm(K("a2.%d" % (mo - 2), ci))
                elif ci >= 1 and mo == 0:
                    e.waitm(K("a2.%d" % 7, ci - 1))
                pp = pm2[:, (mo % 2) * 512:(mo % 2) * 512 + 512]
                for ki in range(32):
                    e.matmul(pp, w2_sb[:, 1024 * ki + 128 * mo:
                                       1024 * ki + 128 * (mo + 1)],
                             hid[:, 512 * ki:512 * (ki + 1)],
                             start=(ki == 0), stop=(ki == 31))
                c.mark(K("m2.%d" % mo, ci), e)

    def body_scalar(e):
        for ci in range(NC512):
            e.waitm(K("rows1", ci))
            e.activation(mrows[0:1, RW["sd"]], mrows[0:1, RW["vpe"]], AF.Ln)
            e.activation(mrows[0:1, RW["sd"]], mrows[0:1, RW["sd"]], AF.Exp,
                         scale=0.5)
            c.mark(K("sqr", ci), e)
            for mo4 in range(32):
                e.waitm(K("m1.%d" % mo4, ci))
                e.activation(hid[:, 512 * mo4:512 * (mo4 + 1)],
                             pm1[:, (mo4 % 2) * 512:(mo4 % 2) * 512 + 512],
                             AF.Gelu_apprx_tanh,
                             bias=t["mlpb1_s"][:, mo4:mo4 + 1])
                c.mark(K("g1.%d" % mo4, ci), e)


    def body_sync(e):
        for it in range(NC512 + 1):
            if it >= 1:
                ci = it - 1
                e.waitm(K("s0", ci))
                buf = h2ld[:, (ci % 2) * 4096:(ci % 2 + 1) * 4096]
                tok = e.dma(s0_h[:, :, 512 * ci:512 * (ci + 1)],
                            buf.rearrange("p (mo tk) -> p mo tk", mo=8),
                            stream="sph")
                c.mark_tok(K("sp", ci), tok)
            if it < NC512:
                ci = it
                if ci >= 2:
                    e.waitm(K("sp", ci - 2))
                buf = h2ld[:, (ci % 2) * 4096:(ci % 2 + 1) * 4096]
                tok = e.dma(buf.rearrange("p (mo tk) -> p mo tk", mo=8),
                            h2_h[:, :, 512 * ci:512 * (ci + 1)], stream="ld")
                c.mark_tok(K("ld", ci), tok)

    c.stage("mlp", [("gpsimd", body_gp), ("vector", body_vector),
                    ("tensor", body_tensor), ("scalar", body_scalar),
                    ("sync", body_sync)])



def _wdq(c, e_split, wnat_h, wsp_h, ot_tiles, iblocks, wld, sctmp, tag):
    pass


def _ffn_a(c):
    nc = c.nc
    with contextlib.ExitStack() as st:
        wld = st.enter_context(nc.sbuf_tensor("wld", [128, 2 * 1024], F32))
        sct = st.enter_context(nc.sbuf_tensor("sct", [128, 4 * 96], F32))
        wuT = st.enter_context(nc.sbuf_tensor("wuT", [128, 8 * 3072], BF16))
        wgT = st.enter_context(nc.sbuf_tensor("wgT", [128, 8 * 3072], BF16))
        xnl = st.enter_context(nc.sbuf_tensor("xnl", [128, 2 * 4096], F32))
        xq = st.enter_context(nc.sbuf_tensor("xq", [128, 4096], BF16))
        rawb = st.enter_context(nc.sbuf_tensor("rawb", [128, 2 * 4096], F32))
        redt = st.enter_context(nc.sbuf_tensor("redt", [128, 4], F32))
        pup = st.enter_context(nc.psum_tensor("pup", [128, 2 * 512], F32))
        _ffn_a_body(c, wld, sct, wuT, wgT, xnl, xq, rawb, redt, pup)


def _ffn_a_body(c, wld, sct, wuT, wgT, xnl, xq, rawb, redt, pup):
    t = c.t
    xn_h = t["xn_sp"].rearrange("(ki p) tok -> p ki tok", p=128)
    K = lambda nm, ci: "fa.%s.%d" % (nm, ci)
    WCFG = [("wup_nat", "wqup_sp", "upraw_sp", wuT, 24, 1),
            ("wgate_nat", "wqgate_sp", "gateraw_sp", wgT, 24, 2)]

    def body_sync(e):
        for wi, (wn, wq, _, _, nt, _) in enumerate(WCFG):
            wnat = t[wn].rearrange("(ot p) i -> p ot i", p=128)
            for ot in range(nt):
                g = wi * 24 + ot
                if g >= 2:
                    e.waitm(K("dqs", g - 2))
                tok = e.dma(wld[:, (g % 2) * 1024:(g % 2 + 1) * 1024],
                            wnat[:, ot, :], stream="ld")
                c.mark_tok(K("wnat", g), tok)
        for wi, (wn, wq, _, wT, nt, _) in enumerate(WCFG):
            for ki in range(8):
                e.waitm(K("dqs", wi * 24 + nt - 1))
                e.waitm(K("dqs", wi * 24 + nt - 2))
                tok = e.dma_t(wT[:, 3072 * ki:3072 * (ki + 1)],
                              t[wq][:, 128 * ki:128 * (ki + 1)])
                c.mark_tok(K("wT.%d" % wi, ki), tok)
        for it in range(NC512 + 1):
            if it >= 1:
                ci = it - 1
                for half in range(6):
                    e.waitm(K("ev.%d" % half, ci))
                    wq = half // 3
                    dst = t[WCFG[wq][2]].rearrange("(mo p) tok -> p mo tok", p=128)
                    tok = e.dma(dst[:, 8 * (half % 3):8 * (half % 3 + 1),
                                    512 * ci:512 * (ci + 1)],
                                rawb[:, (half % 2) * 4096:(half % 2 + 1) * 4096]
                                .rearrange("p (mo tk) -> p mo tk", mo=8),
                                stream="sph" if wq else "spx")
                    c.mark_tok(K("sp.%d" % half, ci), tok)
            if it < NC512:
                ci = it
                if ci >= 2:
                    e.waitm(K("xq", ci - 2))
                tok = e.dma(xnl[:, (ci % 2) * 4096:(ci % 2 + 1) * 4096]
                            .rearrange("p (ki tk) -> p ki tk", ki=8),
                            xn_h[:, :, 512 * ci:512 * (ci + 1)], stream="ld")
                c.mark_tok(K("xn", ci), tok)

    def body_vector(e):
        e.memset(t["mx_acc"][:, 1:5], 0.0)
        for wi, (wn, wq, _, _, nt, _) in enumerate(WCFG):
            for ot in range(nt):
                g = wi * 24 + ot
                buf = wld[:, (g % 2) * 1024:(g % 2 + 1) * 1024]
                e.waitm(K("wnat", g))
                e.tensor_reduce(sct[:, 0:32],
                                buf.rearrange("p (b q) -> p b q", q=32), AX.X,
                                ALU.max, apply_absolute_value=True)
                e.tensor_scalar(sct[:, 96:128], sct[:, 0:32], 1.0 / 7.0, 1e-12,
                                ALU.mult, ALU.max)
                e.reciprocal(sct[:, 192:224], sct[:, 96:128])
                b3 = buf.rearrange("p (b q) -> p b q", q=32)
                e.tensor_tensor(b3, b3,
                                sct[:, 192:224].rearrange("p (b o) -> p b o", o=1)
                                .broadcast_to([128, 32, 32]), ALU.mult)
                e.tensor_scalar(buf, buf, RBIG, RBIG, ALU.add, ALU.subtract)
                e.tensor_tensor(b3, b3,
                                sct[:, 96:128].rearrange("p (b o) -> p b o", o=1)
                                .broadcast_to([128, 32, 32]), ALU.mult)
                c.mark(K("dq", g), e)
        for ci in range(NC512):
            for half in range(6):
                e.waitm(K("ev.%d" % half, ci))
                e.tensor_reduce(redt[:, 0:1],
                                rawb[:, (half % 2) * 4096:(half % 2 + 1) * 4096],
                                AX.X, ALU.max, apply_absolute_value=True)
                e.tensor_tensor(t["mx_acc"][:, 1 + half // 3:2 + half // 3],
                                t["mx_acc"][:, 1 + half // 3:2 + half // 3],
                                redt[:, 0:1], ALU.max)
                c.mark(K("red.%d" % half, ci), e)

    def body_gp(e):
        for wi, (wn, wq, _, _, nt, _) in enumerate(WCFG):
            wqh = t[wq].rearrange("(ot p) i -> p ot i", p=128)
            for ot in range(nt):
                g = wi * 24 + ot
                e.waitm(K("dq", g))
                tok = e.dma(wqh[:, ot, :],
                            wld[:, (g % 2) * 1024:(g % 2 + 1) * 1024],
                            stream="w")
                c.mark_tok(K("dqs", g), tok)
        for ci in range(NC512):
            e.waitm(K("xn", ci))
            e.waitm("cs0.done")
            if ci >= 1:
                e.waitm(K("mm.5", ci - 1))
            buf = xnl[:, (ci % 2) * 4096:(ci % 2 + 1) * 4096]
            e.tensor_scalar(buf, buf, t["scol"][:, 1:2], RBIG, ALU.mult, ALU.add)
            e.tensor_scalar(xq[:, :], buf, RBIG, t["scol"][:, 0:1],
                            ALU.subtract, ALU.mult)
            c.mark(K("xq", ci), e)

    def body_scalar(e):
        for ci in range(NC512):
            for half in range(6):
                for mo8 in range(8):
                    e.waitm(K("mm.%d.%d" % (half, mo8), ci))
                    if (ci, half) not in ((0, 0), (0, 1)):
                        pc, ph = (ci, half - 2) if half >= 2 else (ci - 1, half + 4)
                        e.waitm(K("sp.%d" % ph, pc))
                        e.waitm(K("red.%d" % ph, pc))
                    pp = pup[:, ((half * 8 + mo8) % 2) * 512:
                             ((half * 8 + mo8) % 2 + 1) * 512]
                    e.copy(rawb[:, (half % 2) * 4096 + 512 * mo8:
                                (half % 2) * 4096 + 512 * (mo8 + 1)], pp)
                    c.mark(K("evt.%d.%d" % (half, mo8), ci), e)
                c.mark(K("ev.%d" % half, ci), e)

    def body_tensor(e):
        for ci in range(NC512):
            e.waitm(K("xq", ci))
            for half in range(6):
                wq = half // 3
                wT = WCFG[wq][3]
                for ki in range(8):
                    e.waitm(K("wT.%d" % wq, ki))
                for mo8 in range(8):
                    mo = (half % 3) * 8 + mo8
                    pp = pup[:, ((half * 8 + mo8) % 2) * 512:
                             ((half * 8 + mo8) % 2 + 1) * 512]
                    if (ci, half, mo8) != (0, 0, 0):
                        ptile = half * 8 + mo8 - 2
                        pci = ci
                        if ptile < 0:
                            ptile += 48
                            pci -= 1
                        if pci >= 0:
                            e.waitm(K("evt.%d.%d" % (ptile // 8, ptile % 8), pci))
                    for ki in range(8):
                        e.matmul(pp, wT[:, 3072 * ki + 128 * mo:
                                        3072 * ki + 128 * (mo + 1)],
                                 xq[:, 512 * ki:512 * (ki + 1)],
                                 start=(ki == 0), stop=(ki == 7))
                    c.mark(K("mm.%d.%d" % (half, mo8), ci), e)
                c.mark(K("mm.%d" % half, ci), e)

    c.stage("ffa", [("sync", body_sync), ("gpsimd", body_gp),
                    ("vector", body_vector), ("tensor", body_tensor),
                    ("scalar", body_scalar)])


def _coll_scales(c, slots, mark_in, name):
    t = c.t
    with c.nc.psum_tensor("pc_" + name, [128, 512], F32) as pc:
        P = lambda nm: "%s.%s" % (name, nm)

        def body_tensor(e):
            e.waitm(mark_in)
            for j, (mxc, _, _, _) in enumerate(slots):
                e.transpose(pc[0:1, 128 * j:128 * j + 128],
                            t["mx_acc"][:, mxc:mxc + 1], t["ident_f"][:, :])
            c.mark(P("tr"), e)
            e.waitm(P("val"))
            for j, (mxc, sc_col, inv_col, _) in enumerate(slots):
                e.matmul(pc[:, 256 + 2 * j:256 + 2 * j + 1],
                         t["onesrow_f"][0:1, :], t["ccv"][0:1, 8 + mxc:9 + mxc],
                         start=True, stop=True)
                e.matmul(pc[:, 257 + 2 * j:258 + 2 * j],
                         t["onesrow_f"][0:1, :], t["ccv"][0:1, 16 + mxc:17 + mxc],
                         start=True, stop=True)
            c.mark(P("bc"), e)

        def body_gp(e):
            e.waitm(P("red"))
            tok = e.dma(t["cc_in"][0:1, :], t["ccv"][0:1, 0:16], stream="w")
            e.wait(tok[0], tok[1])
            ctok = e.collective_max(t["cc_in"].opt(), t["cc_out"].opt())
            e.wait(ctok[0], ctok[1])
            tok = e.dma(t["ccv"][0:1, 0:16], t["cc_out"][0:1, :], stream="w")
            c.mark_tok(P("ccout"), tok)

        def body_vector(e):
            for j, (mxc, _, _, _) in enumerate(slots):
                e.waitm(P("tr"))
                e.tensor_reduce(t["ccv"][0:1, mxc:mxc + 1],
                                pc[0:1, 128 * j:128 * j + 128], AX.X, ALU.max)
            c.mark(P("red"), e)
            e.waitm(P("ccout"))
            for j, (mxc, _, _, xdiv) in enumerate(slots):
                e.tensor_scalar(t["ccv"][0:1, 8 + mxc:9 + mxc],
                                t["ccv"][0:1, mxc:mxc + 1],
                                1.0 / (QMAX * xdiv), 1e-12, ALU.mult, ALU.max)
                e.tensor_scalar(t["ccv"][0:1, 16 + mxc:17 + mxc],
                                t["ccv"][0:1, 8 + mxc:9 + mxc], float(xdiv),
                                None, ALU.mult)
                e.reciprocal(t["ccv"][0:1, 16 + mxc:17 + mxc],
                             t["ccv"][0:1, 16 + mxc:17 + mxc])
            c.mark(P("val"), e)
            e.waitm(P("bc"))
            for j, (mxc, sc_col, inv_col, _) in enumerate(slots):
                e.tensor_copy(t["scol"][:, sc_col:sc_col + 1],
                              pc[:, 256 + 2 * j:257 + 2 * j])
                e.tensor_copy(t["scol"][:, inv_col:inv_col + 1],
                              pc[:, 257 + 2 * j:258 + 2 * j])
            c.mark(P("done"), e)

        c.stage(name, [("tensor", body_tensor), ("gpsimd", body_gp),
                       ("vector", body_vector)])


def _ffn_bc(c, src_names, dst_name, mx_col, stage_name, ops):
    pass


def _ffn_b(c):
    nc = c.nc
    with contextlib.ExitStack() as st:
        gld = st.enter_context(nc.sbuf_tensor("gld", [128, 2 * 4096], F32))
        gfb = st.enter_context(nc.sbuf_tensor("gfb", [128, 4096], F32))
        sgb = st.enter_context(nc.sbuf_tensor("sgb", [128, 2 * 4096], F32))
        tsb = st.enter_context(nc.sbuf_tensor("tsb", [128, 2 * 4096], F32))
        redb = st.enter_context(nc.sbuf_tensor("redb", [128, 4], F32))
        _ffn_b_body(c, gld, gfb, sgb, tsb, redb)


def _ffn_b_body(c, gld, gfb, sgb, tsb, redb):
    t = c.t
    gr_h = t["gateraw_sp"].rearrange("(mo p) tok -> p mo tok", p=128)
    t_h = t["t_sp"].rearrange("(mo p) tok -> p mo tok", p=128)
    K = lambda nm, g: "fb.%s.%d" % (nm, g)
    NG = NC512 * 3

    def body_sync(e):
        for it in range(NG + 1):
            if it >= 1:
                g = it - 1
                e.waitm(K("t", g))
                tok = e.dma(t_h[:, 8 * (g % 3):8 * (g % 3 + 1),
                                512 * (g // 3):512 * (g // 3 + 1)],
                            tsb[:, (g % 2) * 4096:(g % 2 + 1) * 4096]
                            .rearrange("p (mo tk) -> p mo tk", mo=8),
                            stream="sph")
                c.mark_tok(K("sp", g), tok)
            if it < NG:
                g = it
                if g >= 2:
                    e.waitm(K("q", g - 2))
                tok = e.dma(gld[:, (g % 2) * 4096:(g % 2 + 1) * 4096]
                            .rearrange("p (mo tk) -> p mo tk", mo=8),
                            gr_h[:, 8 * (g % 3):8 * (g % 3 + 1),
                                 512 * (g // 3):512 * (g // 3 + 1)],
                            stream="ld")
                c.mark_tok(K("ld", g), tok)

    def body_scalar(e):
        e.waitm("fb.pre")
        for g in range(NG):
            sg = sgb[:, (g % 2) * 4096:(g % 2 + 1) * 4096]
            e.waitm(K("q", g))
            if g >= 1:
                e.waitm(K("t", g - 1))
            e.activation(sg, gfb[:, :], AF.Sigmoid)
            c.mark(K("sig", g), e)
            e.activation(sg, sg, AF.Identity, scale=65536.0,
                         bias=t["scol"][:, 13:14])
            c.mark(K("si", g), e)

    def body_vector(e):
        e.memset(t["scol"][:, 13:14], RBIG)
        c.mark("fb.pre", e)
        for g in range(NG):
            e.waitm(K("ld", g))
            e.waitm("cs1.done")
            if g >= 1:
                e.waitm(K("sig", g - 1))
            buf = gld[:, (g % 2) * 4096:(g % 2 + 1) * 4096]
            e.tensor_scalar(gfb[:, :], buf, t["scol"][:, 5:6], RBIG,
                            ALU.mult, ALU.add)
            e.tensor_scalar(gfb[:, :], gfb[:, :], RBIG, t["scol"][:, 4:5],
                            ALU.subtract, ALU.mult)
            c.mark(K("q", g), e)
            sg = sgb[:, (g % 2) * 4096:(g % 2 + 1) * 4096]
            e.waitm(K("si", g))
            e.tensor_scalar(sg, sg, RBIG, 65535.0, ALU.subtract, ALU.min)
            if g >= 2:
                e.waitm(K("sp", g - 2))
            e.tensor_tensor(tsb[:, (g % 2) * 4096:(g % 2 + 1) * 4096],
                            gfb[:, :], sg, ALU.mult)
            e.tensor_reduce(redb[:, 0:1],
                            tsb[:, (g % 2) * 4096:(g % 2 + 1) * 4096],
                            AX.X, ALU.max, apply_absolute_value=True)
            e.tensor_tensor(t["mx_acc"][:, 3:4], t["mx_acc"][:, 3:4],
                            redb[:, 0:1], ALU.max)
            c.mark(K("t", g), e)

    c.stage("ffb", [("sync", body_sync), ("scalar", body_scalar),
                    ("vector", body_vector)])


def _ffn_c(c):
    nc = c.nc
    with contextlib.ExitStack() as st:
        tld = st.enter_context(nc.sbuf_tensor("tld", [128, 2 * 4096], F32))
        uld = st.enter_context(nc.sbuf_tensor("uld", [128, 2 * 4096], F32))
        g2b = st.enter_context(nc.sbuf_tensor("g2b", [128, 2 * 4096], F32))
        zbb = st.enter_context(nc.sbuf_tensor("zbb", [128, 2 * 4096], F32))
        redc = st.enter_context(nc.sbuf_tensor("redc", [128, 4], F32))
        _ffn_c_body(c, tld, uld, g2b, zbb, redc)


def _ffn_c_body(c, tld, uld, g2b, zbb, redc):
    t = c.t
    t_h = t["t_sp"].rearrange("(mo p) tok -> p mo tok", p=128)
    u_h = t["upraw_sp"].rearrange("(mo p) tok -> p mo tok", p=128)
    z_h = t["z_sp"].rearrange("(mo p) tok -> p mo tok", p=128)
    K = lambda nm, g: "fc.%s.%d" % (nm, g)
    NG = NC512 * 3

    def body_sync(e):
        for it in range(NG + 1):
            if it >= 1:
                g = it - 1
                e.waitm(K("z", g))
                tok = e.dma(z_h[:, 8 * (g % 3):8 * (g % 3 + 1),
                                512 * (g // 3):512 * (g // 3 + 1)],
                            zbb[:, (g % 2) * 4096:(g % 2 + 1) * 4096]
                            .rearrange("p (mo tk) -> p mo tk", mo=8),
                            stream="sph")
                c.mark_tok(K("sp", g), tok)
            if it < NG:
                g = it
                if g >= 2:
                    e.waitm(K("g2", g - 2))
                tok = e.dma(tld[:, (g % 2) * 4096:(g % 2 + 1) * 4096]
                            .rearrange("p (mo tk) -> p mo tk", mo=8),
                            t_h[:, 8 * (g % 3):8 * (g % 3 + 1),
                                512 * (g // 3):512 * (g // 3 + 1)], stream="ld")
                c.mark_tok(K("ldt", g), tok)
                if g >= 2:
                    e.waitm(K("z", g - 2))
                tok = e.dma(uld[:, (g % 2) * 4096:(g % 2 + 1) * 4096]
                            .rearrange("p (mo tk) -> p mo tk", mo=8),
                            u_h[:, 8 * (g % 3):8 * (g % 3 + 1),
                                512 * (g // 3):512 * (g // 3 + 1)], stream="spx")
                c.mark_tok(K("ldu", g), tok)

    def body_scalar(e):
        for g in range(NG):
            buf = uld[:, (g % 2) * 4096:(g % 2 + 1) * 4096]
            e.waitm(K("ldu", g))
            e.waitm("cs1.done")
            e.activation(buf, buf, AF.Identity, scale=t["scol"][:, 3:4],
                         bias=t["scol"][:, 13:14])
            c.mark(K("u1", g), e)
            e.waitm(K("ldt", g))
            e.waitm("cs2.done")
            if g >= 1:
                e.waitm(K("zv", g - 1))
            e.activation(g2b[:, (g % 2) * 4096:(g % 2 + 1) * 4096],
                         tld[:, (g % 2) * 4096:(g % 2 + 1) * 4096],
                         AF.Identity, scale=t["scol"][:, 7:8],
                         bias=t["scol"][:, 13:14])
            c.mark(K("g2", g), e)

    def body_vector(e):
        for g in range(NG):
            buf = uld[:, (g % 2) * 4096:(g % 2 + 1) * 4096]
            g2 = g2b[:, (g % 2) * 4096:(g % 2 + 1) * 4096]
            e.waitm(K("u1", g))
            e.tensor_scalar(buf, buf, RBIG, t["scol"][:, 2:3],
                            ALU.subtract, ALU.mult)
            c.mark(K("u", g), e)
            e.waitm(K("g2", g))
            e.tensor_scalar(g2, g2, RBIG, t["scol"][:, 6:7],
                            ALU.subtract, ALU.mult)
            if g >= 2:
                e.waitm(K("sp", g - 2))
            e.tensor_tensor(zbb[:, (g % 2) * 4096:(g % 2 + 1) * 4096], g2,
                            buf, ALU.mult)
            c.mark(K("zv", g), e)
            e.tensor_reduce(redc[:, 0:1],
                            zbb[:, (g % 2) * 4096:(g % 2 + 1) * 4096],
                            AX.X, ALU.max, apply_absolute_value=True)
            e.tensor_tensor(t["mx_acc"][:, 4:5], t["mx_acc"][:, 4:5],
                            redc[:, 0:1], ALU.max)
            c.mark(K("z", g), e)

    c.stage("ffc", [("sync", body_sync), ("scalar", body_scalar),
                    ("vector", body_vector)])


def _ffn_d(c):
    nc = c.nc
    with contextlib.ExitStack() as st:
        wdld = st.enter_context(nc.sbuf_tensor("wdld", [128, 2 * 3072], F32))
        scd = st.enter_context(nc.sbuf_tensor("scd", [128, 4 * 96], F32))
        wdT = st.enter_context(nc.sbuf_tensor("wdT", [128, 24 * 1024], BF16))
        zld = st.enter_context(nc.sbuf_tensor("zld", [128, 2 * 4096], F32))
        ob = st.enter_context(nc.sbuf_tensor("ob", [128, 3 * 4096], BF16))
        g0b = st.enter_context(nc.sbuf_tensor("g0b", [128, 2 * 4096], F32))
        pdn = st.enter_context(nc.psum_tensor("pdn", [128, 2 * 512], F32))
        _ffn_d_body(c, wdld, scd, wdT, zld, ob, g0b, pdn)


def _ffn_d_body(c, wdld, scd, wdT, zld, ob, g0b, pdn):
    t = c.t
    z_h = t["z_sp"].rearrange("(mo p) tok -> p mo tok", p=128)
    g0_h = t["g0_sp"].rearrange("(mo p) tok -> p mo tok", p=128)
    wn_h = t["wdown_nat"].rearrange("(ot p) i -> p ot i", p=128)
    wq_h = t["wqdown_sp"].rearrange("(ot p) i -> p ot i", p=128)
    K = lambda nm, g: "fd.%s.%d" % (nm, g)

    def body_sync(e):
        for ot in range(8):
            if ot >= 2:
                e.waitm(K("dqs", ot - 2))
            tok = e.dma(wdld[:, (ot % 2) * 3072:(ot % 2 + 1) * 3072],
                        wn_h[:, ot, :], stream="ld")
            c.mark_tok(K("wnat", ot), tok)
        for ki in range(24):
            e.waitm(K("dqs", 7))
            e.waitm(K("dqs", 6))
            tok = e.dma_t(wdT[:, 1024 * ki:1024 * (ki + 1)],
                          t["wqdown_sp"][:, 128 * ki:128 * (ki + 1)])
            c.mark_tok(K("wT", ki), tok)
        for it in range(NC512 + 1):
            if it >= 1:
                ci = it - 1
                e.waitm(K("g0", ci))
                tok = e.dma(g0_h[:, :, 512 * ci:512 * (ci + 1)],
                            g0b[:, (ci % 2) * 4096:(ci % 2 + 1) * 4096]
                            .rearrange("p (mo tk) -> p mo tk", mo=8),
                            stream="sph")
                c.mark_tok(K("sp", ci), tok)
            if it < NC512:
                ci = it
                for half in range(3):
                    g = 3 * ci + half
                    if g >= 2:
                        e.waitm(K("oq.%d" % ((g - 2) % 3), (g - 2) // 3))
                    tok = e.dma(zld[:, (g % 2) * 4096:(g % 2 + 1) * 4096]
                                .rearrange("p (mo tk) -> p mo tk", mo=8),
                                z_h[:, 8 * half:8 * (half + 1),
                                    512 * ci:512 * (ci + 1)], stream="ld")
                    c.mark_tok(K("ldz.%d" % half, ci), tok)

    def body_gp(e):
        for ot in range(8):
            e.waitm(K("dq", ot))
            buf = wdld[:, (ot % 2) * 3072:(ot % 2 + 1) * 3072]
            tok = e.dma(wq_h[:, ot, :], buf, stream="w")
            c.mark_tok(K("dqs", ot), tok)

    def body_vector(e):
        for ot in range(8):
            buf = wdld[:, (ot % 2) * 3072:(ot % 2 + 1) * 3072]
            e.waitm(K("wnat", ot))
            if ot >= 2:
                e.waitm(K("dqs", ot - 2))
            e.tensor_reduce(scd[:, 0:96],
                            buf.rearrange("p (b q) -> p b q", q=32), AX.X,
                            ALU.max, apply_absolute_value=True)
            e.tensor_scalar(scd[:, 96:192], scd[:, 0:96], 1.0 / 7.0, 1e-12,
                            ALU.mult, ALU.max)
            e.reciprocal(scd[:, 192:288], scd[:, 96:192])
            e.tensor_tensor(buf.rearrange("p (b q) -> p b q", q=32),
                            buf.rearrange("p (b q) -> p b q", q=32),
                            scd[:, 192:288].rearrange("p (b o) -> p b o", o=1)
                            .broadcast_to([128, 96, 32]), ALU.mult)
            e.tensor_scalar(buf, buf, RBIG, RBIG, ALU.add, ALU.subtract)
            e.tensor_tensor(buf.rearrange("p (b q) -> p b q", q=32),
                            buf.rearrange("p (b q) -> p b q", q=32),
                            scd[:, 96:192].rearrange("p (b o) -> p b o", o=1)
                            .broadcast_to([128, 96, 32]), ALU.mult)
            c.mark(K("dq", ot), e)
        for ci in range(NC512):
            for half in range(3):
                g = 3 * ci + half
                e.waitm(K("ldz.%d" % half, ci))
                e.waitm("cs3.done")
                buf = zld[:, (g % 2) * 4096:(g % 2 + 1) * 4096]
                e.tensor_scalar(buf, buf, t["scol"][:, 9:10], RBIG,
                                ALU.mult, ALU.add)
                e.tensor_scalar(ob[:, half * 4096:(half + 1) * 4096], buf,
                                RBIG, t["scol"][:, 8:9], ALU.subtract, ALU.mult)
                c.mark(K("oq.%d" % half, ci), e)

    def body_tensor(e):
        for ki in range(24):
            e.waitm(K("wT", ki))
        for ci in range(NC512):
            for half in range(3):
                e.waitm(K("oq.%d" % half, ci))
            for mo in range(8):
                if (ci, mo) != (0, 0):
                    pm = mo - 2
                    pci = ci
                    if pm < 0:
                        pm += 8
                        pci -= 1
                    if pci >= 0:
                        e.waitm(K("g0e.%d" % pm, pci))
                pp = pdn[:, (mo % 2) * 512:(mo % 2 + 1) * 512]
                for ki in range(24):
                    e.matmul(pp, wdT[:, 1024 * ki + 128 * mo:
                                     1024 * ki + 128 * (mo + 1)],
                             ob[:, 512 * ki:512 * (ki + 1)],
                             start=(ki == 0), stop=(ki == 23))
                c.mark(K("mm.%d" % mo, ci), e)

    def body_scalar(e):
        for ci in range(NC512):
            for mo in range(8):
                e.waitm(K("mm.%d" % mo, ci))
                if ci >= 2 and mo == 0:
                    e.waitm(K("sp", ci - 2))
                e.activation(g0b[:, (ci % 2) * 4096 + 512 * mo:
                                 (ci % 2) * 4096 + 512 * (mo + 1)],
                             pdn[:, (mo % 2) * 512:(mo % 2 + 1) * 512], AF.Silu)
                c.mark(K("g0e.%d" % mo, ci), e)
            c.mark(K("g0", ci), e)

    c.stage("ffd", [("sync", body_sync), ("gpsimd", body_gp),
                    ("vector", body_vector), ("tensor", body_tensor),
                    ("scalar", body_scalar)])



def _gate_ln1(c):
    nc = c.nc
    with contextlib.ExitStack() as st:
        xld = st.enter_context(nc.sbuf_tensor("xld9", [128, 2 * 4096], F32))
        sld = st.enter_context(nc.sbuf_tensor("sld9", [128, 2 * 4096], F32))
        gld = st.enter_context(nc.sbuf_tensor("gld9", [128, 2 * 4096], F32))
        dbf = st.enter_context(nc.sbuf_tensor("dbf9", [128, 2 * 4096], F32))
        xbb = st.enter_context(nc.sbuf_tensor("xbb9", [128, 4096], BF16))
        xsq = st.enter_context(nc.sbuf_tensor("xsq9", [128, 4096], BF16))
        xn1b = st.enter_context(nc.sbuf_tensor("xn1b", [128, 2 * 4096], BF16))
        rws = st.enter_context(nc.sbuf_tensor("rws9", [1, 4 * 512], F32))
        bcs = st.enter_context(nc.sbuf_tensor("bcs9", [128, 2 * 512], F32))
        pst = st.enter_context(nc.psum_tensor("pst9", [1, 512], F32))
        psq = st.enter_context(nc.psum_tensor("psq9", [1, 512], F32))
        pbc = st.enter_context(nc.psum_tensor("pbc9", [128, 2 * 512], F32))
        _gate_ln1_body(c, xld, sld, gld, dbf, xbb, xsq, xn1b, rws, bcs,
                       pst, psq, pbc)


def _gate_ln1_body(c, xld, sld, gld, dbf, xbb, xsq, xn1b, rws, bcs,
                   pst, psq, pbc):
    t = c.t
    x_h = t["x_fm"].rearrange("(ki p) tok -> p ki tok", p=128)
    s_h = t["s0_sp"].rearrange("(ki p) tok -> p ki tok", p=128)
    g_h = t["g0_sp"].rearrange("(ki p) tok -> p ki tok", p=128)
    x1_h = t["x1_sp"].rearrange("(ki p) tok -> p ki tok", p=128)
    xn1_h = t["xn1_sp"].rearrange("(ki p) tok -> p ki tok", p=128)
    xf_h = t["xn1f_sp"].rearrange("(ki p) tok -> p ki tok", p=128)
    K = lambda nm, ci: "g9.%s.%d" % (nm, ci)
    RW = {"m": slice(0, 512), "e2": slice(512, 1024), "sd": slice(1024, 1536),
          "tmp": slice(1536, 2048)}

    def body_sync(e):
        for it in range(NC512 + 1):
            if it >= 1:
                ci = it - 1
                e.waitm(K("x1", ci))
                tok = e.dma(x1_h[:, :, 512 * ci:512 * (ci + 1)],
                            xld[:, (ci % 2) * 4096:(ci % 2 + 1) * 4096]
                            .rearrange("p (ki tk) -> p ki tk", ki=8),
                            stream="sph")
                c.mark_tok(K("spx1", ci), tok)
                e.waitm(K("xn1", ci))
                tok = e.dma(xn1_h[:, :, 512 * ci:512 * (ci + 1)],
                            xn1b[:, (ci % 2) * 4096:(ci % 2 + 1) * 4096]
                            .rearrange("p (ki tk) -> p ki tk", ki=8),
                            stream="spx")
                c.mark_tok(K("spn", ci), tok)
                e.waitm(K("xf", ci))
                tok = e.dma(xf_h[:, :, 512 * ci:512 * (ci + 1)],
                            dbf[:, (ci % 2) * 4096:(ci % 2 + 1) * 4096]
                            .rearrange("p (ki tk) -> p ki tk", ki=8),
                            stream="sph")
                c.mark_tok(K("spf", ci), tok)
            if it < NC512:
                ci = it
                if ci >= 2:
                    e.waitm(K("apl", ci - 2))
                    e.waitm(K("spx1", ci - 2))
                tok = e.dma(xld[:, (ci % 2) * 4096:(ci % 2 + 1) * 4096]
                            .rearrange("p (ki tk) -> p ki tk", ki=8),
                            x_h[:, :, 512 * ci:512 * (ci + 1)], stream="ld")
                c.mark_tok(K("ldx", ci), tok)
                if ci >= 2:
                    e.waitm(K("sg", ci - 2))
                    e.waitm(K("x1", ci - 2))
                tok = e.dma(sld[:, (ci % 2) * 4096:(ci % 2 + 1) * 4096]
                            .rearrange("p (ki tk) -> p ki tk", ki=8),
                            s_h[:, :, 512 * ci:512 * (ci + 1)], stream="ld")
                c.mark_tok(K("lds", ci), tok)
                tok = e.dma(gld[:, (ci % 2) * 4096:(ci % 2 + 1) * 4096]
                            .rearrange("p (ki tk) -> p ki tk", ki=8),
                            g_h[:, :, 512 * ci:512 * (ci + 1)], stream="ld")
                c.mark_tok(K("ldg", ci), tok)

    def body_gp(e):
        for ci in range(NC512):
            e.waitm(K("lds", ci))
            e.waitm(K("ldg", ci))
            buf = sld[:, (ci % 2) * 4096:(ci % 2 + 1) * 4096]
            e.tensor_tensor(buf, buf,
                            gld[:, (ci % 2) * 4096:(ci % 2 + 1) * 4096], ALU.mult)
            c.mark(K("sg", ci), e)

    def body_vector(e):
        for ci in range(NC512):
            e.waitm(K("ldx", ci))
            e.waitm(K("sg", ci))
            if ci >= 2:
                e.waitm(K("spx1", ci - 2))
                e.waitm(K("spn", ci - 2))
            buf = xld[:, (ci % 2) * 4096:(ci % 2 + 1) * 4096]
            e.tensor_tensor(buf, buf,
                            sld[:, (ci % 2) * 4096:(ci % 2 + 1) * 4096], ALU.add)
            c.mark(K("x1", ci), e)
            e.tensor_copy(xbb[:, :], buf)
            e.tensor_tensor(xsq[:, :], xbb[:, :], xbb[:, :], ALU.mult)
            c.mark(K("cast", ci), e)
            e.waitm(K("stat", ci))
            e.tensor_scalar(rws[0:1, RW["m"]], pst[0:1, :], 1.0 / 1024.0,
                            None, ALU.mult)
            e.tensor_scalar(rws[0:1, RW["e2"]], psq[0:1, :], 1.0 / 1024.0,
                            None, ALU.mult)
            c.mark(K("statfree", ci), e)
            e.tensor_tensor(rws[0:1, RW["tmp"]], rws[0:1, RW["m"]],
                            rws[0:1, RW["m"]], ALU.mult)
            e.tensor_tensor(rws[0:1, RW["e2"]], rws[0:1, RW["e2"]],
                            rws[0:1, RW["tmp"]], ALU.subtract)
            e.tensor_scalar(rws[0:1, RW["e2"]], rws[0:1, RW["e2"]], 1e-5,
                            None, ALU.add)
            c.mark(K("rows1", ci), e)
            e.waitm(K("sqr", ci))
            e.reciprocal(rws[0:1, RW["sd"]], rws[0:1, RW["sd"]])
            e.tensor_tensor(rws[0:1, RW["tmp"]], rws[0:1, RW["sd"]],
                            rws[0:1, RW["sd"]], ALU.mult)
            e.tensor_tensor(rws[0:1, RW["tmp"]], rws[0:1, RW["tmp"]],
                            rws[0:1, RW["e2"]], ALU.mult)
            e.tensor_scalar(rws[0:1, RW["tmp"]], rws[0:1, RW["tmp"]], -0.5,
                            1.5, ALU.mult, ALU.add)
            e.tensor_tensor(rws[0:1, RW["sd"]], rws[0:1, RW["sd"]],
                            rws[0:1, RW["tmp"]], ALU.mult)
            c.mark(K("rows2", ci), e)
            e.waitm(K("bc", ci))
            if ci >= 2:
                e.waitm(K("spf", ci - 2))
            nb = xn1b[:, (ci % 2) * 4096:(ci % 2 + 1) * 4096]
            db = dbf[:, (ci % 2) * 4096:(ci % 2 + 1) * 4096]
            for ki in range(8):
                sl = slice(512 * ki, 512 * (ki + 1))
                e.tensor_tensor(db[:, sl], buf[:, sl], pbc[:, 0:512],
                                ALU.subtract)
                e.tensor_tensor(nb[:, sl], db[:, sl], pbc[:, 512:1024],
                                ALU.mult)
                e.tensor_tensor(db[:, sl], db[:, sl], pbc[:, 512:1024],
                                ALU.mult)
            c.mark(K("apl", ci), e)
            c.mark(K("xn1", ci), e)
            c.mark(K("xf", ci), e)
            c.mark(K("bcfree", ci), e)

    def body_tensor(e):
        for ci in range(NC512):
            e.waitm(K("cast", ci))
            if ci >= 1:
                e.waitm(K("statfree", ci - 1))
            for ki in range(8):
                e.matmul(pst[0:1, :], t["onescol_b"][:, 0:1],
                         xbb[:, 512 * ki:512 * (ki + 1)],
                         start=(ki == 0), stop=(ki == 7))
            for ki in range(8):
                e.matmul(psq[0:1, :], t["onescol_b"][:, 0:1],
                         xsq[:, 512 * ki:512 * (ki + 1)],
                         start=(ki == 0), stop=(ki == 7))
            c.mark(K("stat", ci), e)
            e.waitm(K("rows2", ci))
            if ci >= 1:
                e.waitm(K("bcfree", ci - 1))
            e.matmul(pbc[:, 0:512], t["onesrow_f"][0:1, :], rws[0:1, RW["m"]],
                     start=True, stop=True)
            e.matmul(pbc[:, 512:1024], t["onesrow_f"][0:1, :],
                     rws[0:1, RW["sd"]], start=True, stop=True)
            c.mark(K("bc", ci), e)

    def body_scalar(e):
        for ci in range(NC512):
            e.waitm(K("rows1", ci))
            e.activation(rws[0:1, RW["sd"]], rws[0:1, RW["e2"]], AF.Ln)
            e.activation(rws[0:1, RW["sd"]], rws[0:1, RW["sd"]], AF.Exp,
                         scale=0.5)
            c.mark(K("sqr", ci), e)

    c.stage("gln1", [("sync", body_sync), ("gpsimd", body_gp),
                     ("vector", body_vector), ("tensor", body_tensor),
                     ("scalar", body_scalar)])



def _attn1(c):
    nc = c.nc
    with contextlib.ExitStack() as st:
        wq_sb = st.enter_context(nc.sbuf_tensor("wq_sb", [128, 4 * 512], BF16))
        wk_sb = st.enter_context(nc.sbuf_tensor("wk_sb", [128, 4 * 512], BF16))
        wv_sb = st.enter_context(nc.sbuf_tensor("wv_sb", [128, 4 * 512], BF16))
        wo_sb = st.enter_context(nc.sbuf_tensor("wo_sb", [128, 8 * 1024], BF16))
        xb = st.enter_context(nc.sbuf_tensor("xb", [128, 4 * 2048], BF16))
        qf = st.enter_context(nc.sbuf_tensor("qf", [128, 4 * 2048], BF16))
        kf = st.enter_context(nc.sbuf_tensor("kf", [128, 4 * 2048], BF16))
        vh = st.enter_context(nc.sbuf_tensor("vh", [128, 16 * 1024], BF16))
        atb = st.enter_context(nc.sbuf_tensor("atb", [128, 4 * 2048], BF16))
        ao = st.enter_context(nc.sbuf_tensor("ao", [128, 32 * 512], BF16))
        sq1 = st.enter_context(nc.sbuf_tensor("sq1", [128, 2048], BF16))
        s2_s = st.enter_context(nc.sbuf_tensor("s2_s", [65, 128], F32))
        bd4_b = st.enter_context(nc.sbuf_tensor("bd4_b", [128, 4], BF16))
        zp_b = st.enter_context(nc.sbuf_tensor("zp_b", [128, 128], BF16))
        macc = st.enter_context(nc.sbuf_tensor("macc", [128, 12], F32))
        mrow = st.enter_context(nc.sbuf_tensor("mrow", [1, 8], F32))
        mgc = st.enter_context(nc.sbuf_tensor("mgc", [128, 1], F32))
        rdb = st.enter_context(nc.sbuf_tensor("rdb", [128, 512], F32))
        rbb = st.enter_context(nc.sbuf_tensor("rbb", [128, 512], F32))
        s1b = st.enter_context(nc.sbuf_tensor("s1b", [128, 2 * 512], F32))
        psc = st.enter_context(nc.psum_tensor("psc", [128, 2048], F32))
        ppv = st.enter_context(nc.psum_tensor("ppv", [128, 1024], F32))
        pq1 = st.enter_context(nc.psum_tensor("pq1", [128, 1024], F32))
        _attn1_body(c, wq_sb, wk_sb, wv_sb, wo_sb, xb, qf, kf, vh, atb, ao,
                    sq1, s2_s, bd4_b, zp_b, macc, mrow, mgc, rdb, rbb, s1b,
                    psc, ppv, pq1)


def _attn1_body(c, wq_sb, wk_sb, wv_sb, wo_sb, xb, qf, kf, vh, atb, ao,
                sq1, s2_s, bd4_b, zp_b, macc, mrow, mgc, rdb, rbb, s1b,
                psc, ppv, pq1):
    t = c.t
    xn1_h = t["xn1_sp"].rearrange("(ki p) tok -> p ki tok", p=128)
    s1_h = t["s1_sp"].rearrange("(mo p) tok -> p mo tok", p=128)
    A = lambda *a: "a1." + ".".join(str(x) for x in a)

    def body_sync(e):
        tok = e.dma(s2_s[:, :], t["s2_65"], stream="ld")
        c.mark_tok(A("s2ld"), tok)
        for b in range(NB):
            if b >= 1:
                e.waitm(A("vmmd", b - 1))
            tok = e.dma(xb[:, :].rearrange("p (ki tk) -> p ki tk", ki=4),
                        xn1_h[:, 0:4, 2048 * b:2048 * (b + 1)], stream="ld")
            c.mark_tok(A("xb", b), tok)
            for mo in range(8):
                for qc in range(4):
                    idx = mo * 4 + qc
                    e.waitm(A("s1e", b, idx))
                    tok = e.dma(s1_h[:, mo, 2048 * b + 512 * qc:
                                     2048 * b + 512 * (qc + 1)],
                                s1b[:, (idx % 2) * 512:(idx % 2 + 1) * 512],
                                stream="out")
                    c.mark_tok(A("s1d", b, idx), tok)

    def body_gp(e):
        tok = e.dma(wq_sb[:, :].rearrange("p (ki o) -> p ki o", ki=4),
                    t["wqT"].rearrange("(ki p) o -> p ki o", p=128), stream="w")
        c.mark_tok(A("wq"), tok)
        tok = e.dma(wk_sb[:, :].rearrange("p (ki o) -> p ki o", ki=4),
                    t["wkT"].rearrange("(ki p) o -> p ki o", p=128), stream="w")
        c.mark_tok(A("wk"), tok)
        tok = e.dma(wv_sb[:, :].rearrange("p (ki o) -> p ki o", ki=4),
                    t["wvT"].rearrange("(ki p) o -> p ki o", p=128), stream="w")
        c.mark_tok(A("wv"), tok)
        tok = e.dma(wo_sb[:, :].rearrange("p (pp o) -> p pp o", pp=8),
                    t["woTp"].rearrange("(pp r) o -> r pp o", r=128), stream="w")
        c.mark_tok(A("wo"), tok)
        tok = e.dma(bd4_b[:, :], t["blockdiag4"], stream="w")
        c.mark_tok(A("bd4"), tok)
        e.memset(vh[:, :], 0.0)
        e.memset(vh[:, :].rearrange("p (s d) -> p s d", d=64)[:, :, 0:1], 1.0)
        e.memset(zp_b[:, :], 0.0)
        e.memset(rdb[:, :], 0.0)
        c.mark(A("init"), e)

    def body_tensor(e):
        e.waitm(A("wq"))
        e.waitm(A("wk"))
        e.waitm(A("wv"))
        e.waitm(A("wo"))
        e.waitm(A("bd4"))
        for b in range(NB):
            e.waitm(A("xb", b))
            idx = 0
            for w_sb, tag in ((wq_sb, "q"), (wk_sb, "k")):
                for mo in range(4):
                    for tc in range(4):
                        if idx >= 2:
                            e.waitm(A("qkev", b, idx - 2))
                        elif b >= 1:
                            e.waitm(A("s1e", b - 1, 30 + idx))
                        pp = pq1[:, (idx % 2) * 512:(idx % 2 + 1) * 512]
                        for ki in range(4):
                            e.matmul(pp, w_sb[:, 512 * ki + 128 * mo:
                                              512 * ki + 128 * (mo + 1)],
                                     xb[:, 2048 * ki + 512 * tc:
                                        2048 * ki + 512 * (tc + 1)],
                                     start=(ki == 0), stop=(ki == 3))
                        c.mark(A("qmm", b, idx), e)
                        idx += 1
            for tt in range(16):
                e.waitm(A("qkev", b, idx - 2))
                pp = pq1[:, (idx % 2) * 512:(idx % 2 + 1) * 512]
                for ki in range(4):
                    e.matmul(pp, xb[:, 2048 * ki + 128 * tt:
                                    2048 * ki + 128 * (tt + 1)],
                             wv_sb[:, 512 * ki:512 * (ki + 1)],
                             start=(ki == 0), stop=(ki == 3))
                c.mark(A("qmm", b, idx), e)
                if tt == 15:
                    c.mark(A("vmmd", b), e)
                idx += 1
            e.waitm(A("qkev", b, 46))
            e.waitm(A("qkev", b, 47))
            for si in range(2):
                for pg in range(4):
                    e.waitm(A("sq", b, si, pg))
                    for tc in range(4):
                        if (si, pg, tc) > (0, 0, 1):
                            e.waitm(A("bdr", b, *_prev_bd(si, pg, tc)))
                        e.matmul(pq1[0:4, (tc % 2) * 512:(tc % 2) * 512 + 512],
                                 bd4_b[:, :], sq1[:, 512 * tc:512 * (tc + 1)],
                                 start=True, stop=True)
                        c.mark(A("bd", b, si, pg, tc), e)
            e.waitm(A("mx4", b))
            e.transpose(pq1[0:1, 0:4], macc[0:4, 8:9], t["ident_f"][0:4, 0:4])
            c.mark(A("tr", b), e)
            e.waitm(A("mneg", b))
            e.matmul(pq1[:, 4:5], t["onesrow_f"][0:1, :], mrow[0:1, 1:2],
                     start=True, stop=True)
            c.mark(A("mg", b), e)
            for hg in range(4):
                for qc in range(4):
                    cur = (b, hg, qc)
                    prev = _prev_hq(b, hg, qc)
                    for p01 in range(2):
                        if prev is not None:
                            e.waitm(A("ao", *prev, p01))
                        e.matmul(ppv[:, 512 * p01:512 * (p01 + 1)],
                                 zp_b[:, :], sq1[:, 0:512],
                                 start=True, stop=False)
                    for kt in range(16):
                        if prev is not None or kt > 0:
                            pk = (cur + (kt - 1,)) if kt > 0 else (prev + (15,))
                            e.waitm(A("ex", *pk))
                        for s in range(4):
                            e.matmul(psc[:, 512 * s:512 * (s + 1)],
                                     kf[32 * s:32 * (s + 1),
                                        2048 * hg + 128 * kt:
                                        2048 * hg + 128 * (kt + 1)],
                                     qf[32 * s:32 * (s + 1),
                                        2048 * hg + 512 * qc:
                                        2048 * hg + 512 * (qc + 1)],
                                     start=True, stop=True,
                                     tile_position=(32 * s, 0))
                        c.mark(A("sc", *cur, kt), e)
                        e.waitm(A("ex", *cur, kt))
                        for s in range(4):
                            p01 = s // 2
                            base = 1024 * kt + 128 * (2 * hg + p01)
                            e.matmul(ppv[64 * (s % 2):64 * (s % 2) + 33,
                                         512 * p01:512 * (p01 + 1)],
                                     vh[:, base + 64 * (s % 2):
                                        base + 64 * (s % 2) + 33],
                                     atb[:, 2048 * (kt % 4) + 512 * s:
                                         2048 * (kt % 4) + 512 * (s + 1)],
                                     start=False, stop=False,
                                     tile_position=(0, 64 * (s % 2)))
                        c.mark(A("pv", *cur, kt), e)
                    for p01 in range(2):
                        e.matmul(ppv[:, 512 * p01:512 * (p01 + 1)],
                                 zp_b[:, :], sq1[:, 0:512],
                                 start=False, stop=True)
                    c.mark(A("pvs", *cur), e)
                    for p01 in range(2):
                        e.waitm(A("rd", *cur, p01))
                        if prev is not None:
                            e.waitm(A("rbe", *prev, p01))
                        e.waitm(A("s2ld"))
                        e.matmul(pq1[:, 512 * p01:512 * (p01 + 1)],
                                 s2_s[0:65, :], rdb[0:65, 0:512],
                                 start=True, stop=True)
                        c.mark(A("s2", *cur, p01), e)
            for mo in range(8):
                for qc in range(4):
                    idx = mo * 4 + qc
                    if idx >= 2:
                        e.waitm(A("s1e", b, idx - 2))
                    if idx < 2:
                        e.waitm(A("rbe", b, 3, 3, idx))
                    e.waitm(A("ao", b, 3, qc, 1))
                    pw = pq1[:, (idx % 2) * 512:(idx % 2 + 1) * 512]
                    for p in range(8):
                        e.matmul(pw, wo_sb[:, 1024 * p + 128 * mo:
                                           1024 * p + 128 * (mo + 1)],
                                 ao[:, 512 * (4 * p + qc):
                                    512 * (4 * p + qc + 1)],
                                 start=(p == 0), stop=(p == 7))
                    c.mark(A("wo", b, idx), e)

    def body_scalar(e):
        c.a1_slot = {}
        for b in range(NB):
            if b >= 1:
                e.waitm(A("sqd", b - 1))
            idx = 0
            for dst, tag in ((qf, "q"), (kf, "k")):
                for mo in range(4):
                    for tc in range(4):
                        e.waitm(A("qmm", b, idx))
                        e.activation(dst[:, 2048 * mo + 512 * tc:
                                         2048 * mo + 512 * (tc + 1)],
                                     pq1[:, (idx % 2) * 512:(idx % 2 + 1) * 512],
                                     AF.Identity)
                        c.mark(A("qkev", b, idx), e)
                        idx += 1
            e.waitm(A("init"))
            for tt in range(16):
                e.waitm(A("qmm", b, idx))
                pp = pq1[:, (idx % 2) * 512:(idx % 2 + 1) * 512]
                ppv3 = pp.rearrange("p (pp d) -> p pp d", d=64)
                vv = vh[:, 1024 * tt:1024 * (tt + 1)].rearrange(
                    "p (pp d) -> p pp d", d=128)
                e.activation(vv[:, :, 1:33], ppv3[:, :, 0:32], AF.Identity)
                e.activation(vv[:, :, 65:97], ppv3[:, :, 32:64], AF.Identity)
                c.mark(A("qkev", b, idx), e)
                idx += 1
            e.waitm(A("mg", b))
            e.activation(mgc[:, 0:1], pq1[:, 4:5], AF.Identity)
            c.mark(A("mge", b), e)
            for hg in range(4):
                for qc in range(4):
                    cur = (b, hg, qc)
                    for kt in range(16):
                        e.waitm(A("sc", *cur, kt))
                        pu = c.a1_slot.get(kt % 4)
                        if pu is not None:
                            e.waitm(A("pv", *pu))
                        c.a1_slot[kt % 4] = cur + (kt,)
                        e.activation(atb[:, 2048 * (kt % 4):
                                         2048 * (kt % 4 + 1)],
                                     psc[:, :], AF.Exp,
                                     bias=mgc[:, 0:1], scale=0.125)
                        c.mark(A("ex", *cur, kt), e)
                    for p01 in range(2):
                        e.waitm(A("s2", *cur, p01))
                        pv2 = _prev_pair(b, hg, qc, p01)
                        if pv2 is not None:
                            e.waitm(A("ao", *pv2))
                        e.activation(rbb[:, :],
                                     pq1[:, 512 * p01:512 * (p01 + 1)],
                                     AF.Identity)
                        c.mark(A("rbe", *cur, p01), e)
            for mo in range(8):
                for qc in range(4):
                    idx = mo * 4 + qc
                    e.waitm(A("wo", b, idx))
                    if idx >= 2:
                        e.waitm(A("s1d", b, idx - 2))
                    elif b >= 1:
                        e.waitm(A("s1d", b - 1, 30 + idx))
                    e.activation(s1b[:, (idx % 2) * 512:(idx % 2 + 1) * 512],
                                 pq1[:, (idx % 2) * 512:(idx % 2 + 1) * 512],
                                 AF.Identity, bias=t["bo_s"][:, mo:mo + 1])
                    c.mark(A("s1e", b, idx), e)

    def body_vector(e):
        for b in range(NB):
            if b >= 1:
                e.waitm(A("pvs", b - 1, 3, 3))
            for si, srcf in ((0, qf), (1, kf)):
                for pg in range(4):
                    e.waitm(A("qkev", b, 16 * si + 4 * pg + 3))
                    e.tensor_tensor(sq1[:, :], srcf[:, 2048 * pg:2048 * (pg + 1)],
                                    srcf[:, 2048 * pg:2048 * (pg + 1)], ALU.mult)
                    c.mark(A("sq", b, si, pg), e)
                    col = 4 * si + pg
                    for tc in range(4):
                        e.waitm(A("bd", b, si, pg, tc))
                        pb = pq1[0:4, (tc % 2) * 512:(tc % 2) * 512 + 512]
                        if si == 0 and pg == 0 and tc == 0:
                            e.tensor_reduce(macc[0:4, 8:9], pb, AX.X, ALU.max)
                        else:
                            e.tensor_reduce(macc[0:4, 9:10], pb, AX.X, ALU.max)
                            e.tensor_tensor(macc[0:4, 8:9], macc[0:4, 8:9],
                                            macc[0:4, 9:10], ALU.max)
                        c.mark(A("bdr", b, si, pg, tc), e)
            c.mark(A("sqd", b), e)
            c.mark(A("mx4", b), e)
            e.waitm(A("tr", b))
            e.tensor_reduce(mrow[0:1, 0:1], pq1[0:1, 0:4], AX.X, ALU.max)
            e.tensor_scalar(mrow[0:1, 1:2], mrow[0:1, 0:1], -0.125, None,
                            ALU.mult)
            c.mark(A("mneg", b), e)
            if b >= 1:
                e.waitm(A("wo", b - 1, 31))
            for hg in range(4):
                for qc in range(4):
                    cur = (b, hg, qc)
                    e.waitm(A("pvs", *cur))
                    for p01 in range(2):
                        p = 2 * hg + p01
                        pv2 = _prev_pair2(b, hg, qc, p01)
                        if pv2 is not None:
                            e.waitm(A("s2", *pv2))
                        e.waitm(A("init"))
                        e.reciprocal(rdb[0:1, 0:512],
                                     ppv[0:1, 512 * p01:512 * (p01 + 1)])
                        e.reciprocal(rdb[64:65, 0:512],
                                     ppv[64:65, 512 * p01:512 * (p01 + 1)])
                        c.mark(A("rd", *cur, p01), e)
                        e.waitm(A("rbe", *cur, p01))
                        e.scalar_tensor_tensor(
                            ao[:, 512 * (4 * p + qc):512 * (4 * p + qc + 1)],
                            ppv[:, 512 * p01:512 * (p01 + 1)], 1.0,
                            rbb[:, :], ALU.mult, ALU.mult)
                        c.mark(A("ao", *cur, p01), e)

    c.stage("attn1", [("sync", body_sync), ("gpsimd", body_gp),
                      ("tensor", body_tensor), ("scalar", body_scalar),
                      ("vector", body_vector)])


def _prev_bd(si, pg, tc):
    n = (si * 4 + pg) * 4 + tc - 2
    return (n // 16, (n // 4) % 4, n % 4)


def _prev_hq(b, hg, qc):
    n = (b * 4 + hg) * 4 + qc - 1
    if n < 0:
        return None
    return (n // 16, (n // 4) % 4, n % 4)


def _prev_pair(b, hg, qc, p01):
    n = ((b * 4 + hg) * 4 + qc) * 2 + p01 - 1
    if n < 0:
        return None
    return (n // 32, (n // 8) % 4, (n // 2) % 4, n % 2)


def _prev_pair2(b, hg, qc, p01):
    n = ((b * 4 + hg) * 4 + qc) * 2 + p01 - 2
    if n < 0:
        return None
    return (n // 32, (n // 8) % 4, (n // 2) % 4, n % 2)



def _sharedffn(c):
    nc = c.nc
    with contextlib.ExitStack() as st:
        sgw = st.enter_context(nc.sbuf_tensor("sgw", [128, 8 * 1024], BF16))
        suw = st.enter_context(nc.sbuf_tensor("suw", [128, 8 * 1024], BF16))
        sdw = st.enter_context(nc.sbuf_tensor("sdw", [128, 8 * 1024], BF16))
        xls = st.enter_context(nc.sbuf_tensor("xls", [128, 2 * 4096], BF16))
        gas = st.enter_context(nc.sbuf_tensor("gas", [128, 2 * 512], F32))
        ybs = st.enter_context(nc.sbuf_tensor("ybs", [128, 8 * 512], BF16))
        shb = st.enter_context(nc.sbuf_tensor("shb", [128, 2 * 512], F32))
        pgs = st.enter_context(nc.psum_tensor("pgs", [128, 4 * 512], F32))
        pds = st.enter_context(nc.psum_tensor("pds2", [128, 2 * 512], F32))
        _sharedffn_body(c, sgw, suw, sdw, xls, gas, ybs, shb, pgs, pds)


def _sharedffn_body(c, sgw, suw, sdw, xls, gas, ybs, shb, pgs, pds):
    t = c.t
    xn1_h = t["xn1_sp"].rearrange("(ki p) tok -> p ki tok", p=128)
    sh_h = t["sh_sp"].rearrange("(mo p) tok -> p mo tok", p=128)
    A = lambda *a: "sf." + ".".join(str(x) for x in a)

    def body_sync(e):
        for tc in range(NC512):
            if tc >= 2:
                e.waitm(A("xdone", tc - 2))
            tok = e.dma(xls[:, (tc % 2) * 4096:(tc % 2 + 1) * 4096]
                        .rearrange("p (ki tk) -> p ki tk", ki=8),
                        xn1_h[:, :, 512 * tc:512 * (tc + 1)], stream="ld")
            c.mark_tok(A("x", tc), tok)
            for mo in range(8):
                e.waitm(A("she", tc, mo))
                tok = e.dma(sh_h[:, mo, 512 * tc:512 * (tc + 1)],
                            shb[:, (mo % 2) * 512:(mo % 2 + 1) * 512],
                            stream="spx")
                c.mark_tok(A("shd", tc, mo), tok)

    def body_gp(e):
        for nm, w_sb in (("sg", sgw), ("su", suw), ("sd", sdw)):
            tok = e.dma(w_sb[:, :].rearrange("p (ki o) -> p ki o", ki=8),
                        t[nm].rearrange("(ki p) o -> p ki o", p=128), stream="w")
            c.mark_tok(A("w", nm), tok)

    def body_tensor(e):
        e.waitm(A("w", "sg"))
        e.waitm(A("w", "su"))
        e.waitm(A("w", "sd"))
        for tc in range(NC512):
            e.waitm(A("x", tc))
            xl = xls[:, (tc % 2) * 4096:(tc % 2 + 1) * 4096]
            for mo in range(8):
                for half, w_sb in ((0, sgw), (1, suw)):
                    bk = (mo * 2 + half) % 4
                    if tc > 0 or mo * 2 + half >= 4:
                        e.waitm(A("guf", *_sf_prev(tc, mo * 2 + half, 4)))
                    pp = pgs[:, 512 * bk:512 * (bk + 1)]
                    for ki in range(8):
                        e.matmul(pp, w_sb[:, 1024 * ki + 128 * mo:
                                          1024 * ki + 128 * (mo + 1)],
                                 xl[:, 512 * ki:512 * (ki + 1)],
                                 start=(ki == 0), stop=(ki == 7))
                    c.mark(A("gu", tc, mo, half), e)
            c.mark(A("xdone", tc), e)
            for mo in range(8):
                e.waitm(A("y", tc, 7))
                if tc > 0 or mo >= 2:
                    e.waitm(A("she", *_sf_prev(tc, mo, 2)))
                pp = pds[:, (mo % 2) * 512:(mo % 2 + 1) * 512]
                for ki in range(8):
                    e.matmul(pp, sdw[:, 1024 * ki + 128 * mo:
                                     1024 * ki + 128 * (mo + 1)],
                             ybs[:, 512 * ki:512 * (ki + 1)],
                             start=(ki == 0), stop=(ki == 7))
                c.mark(A("dn", tc, mo), e)

    def body_scalar(e):
        for tc in range(NC512):
            for mo in range(8):
                e.waitm(A("gu", tc, mo, 0))
                if tc > 0 or mo >= 2:
                    e.waitm(A("yst", *_sf_prev(tc, mo, 2)))
                e.activation(gas[:, (mo % 2) * 512:(mo % 2 + 1) * 512],
                             pgs[:, 512 * ((mo * 2) % 4):
                                 512 * ((mo * 2) % 4) + 512], AF.Silu)
                c.mark(A("ga", tc, mo), e)
            for mo in range(8):
                e.waitm(A("dn", tc, mo))
                if tc > 0 or mo >= 2:
                    e.waitm(A("shd", *_sf_prev(tc, mo, 2)))
                e.activation(shb[:, (mo % 2) * 512:(mo % 2 + 1) * 512],
                             pds[:, (mo % 2) * 512:(mo % 2 + 1) * 512],
                             AF.Identity)
                c.mark(A("she", tc, mo), e)

    def body_vector(e):
        for tc in range(NC512):
            for mo in range(8):
                e.waitm(A("ga", tc, mo))
                e.waitm(A("gu", tc, mo, 1))
                if tc > 0 and mo == 0:
                    e.waitm(A("dn", tc - 1, 7))
                e.scalar_tensor_tensor(ybs[:, 512 * mo:512 * (mo + 1)],
                                       pgs[:, 512 * ((mo * 2 + 1) % 4):
                                           512 * ((mo * 2 + 1) % 4) + 512],
                                       1.0, gas[:, (mo % 2) * 512:
                                                (mo % 2 + 1) * 512],
                                       ALU.mult, ALU.mult)
                c.mark(A("yst", tc, mo), e)
                c.mark(A("y", tc, mo), e)
                c.mark(A("guf", tc, mo * 2), e)
                c.mark(A("guf", tc, mo * 2 + 1), e)

    c.stage("sffn", [("sync", body_sync), ("gpsimd", body_gp),
                     ("tensor", body_tensor), ("scalar", body_scalar),
                     ("vector", body_vector)])


def _sf_prev(tc, i, back):
    n = 16 if back == 4 else 8
    k = tc * n + i - back
    return (k // n, k % n)



def _moe_a(c):
    nc = c.nc
    with contextlib.ExitStack() as st:
        xb2 = st.enter_context(nc.sbuf_tensor("xb2", [128, 8 * 2048], BF16))
        egw = st.enter_context(nc.sbuf_tensor("egw", [128, 2 * 4096], BF16))
        euw = st.enter_context(nc.sbuf_tensor("euw", [128, 2 * 4096], BF16))
        rw_b = st.enter_context(nc.sbuf_tensor("rw_b", [128, 64], F32))
        xf1t = st.enter_context(nc.sbuf_tensor("xf1t", [128, 2 * 1024], F32))
        ind8f = st.enter_context(nc.sbuf_tensor("ind8f", [8, 1024], F32))
        coefT = st.enter_context(nc.sbuf_tensor("coefT", [8, 2048], F32))
        prw = st.enter_context(nc.sbuf_tensor("prw", [128, 48], F32))
        pcols = st.enter_context(nc.sbuf_tensor("pcols", [128, 8], F32))
        cbm2_x = st.enter_context(nc.sbuf_tensor("cbm2_x", [128, 2048], BF16))
        sgm = st.enter_context(nc.sbuf_tensor("sgm", [128, 2 * 512], F32))
        tbm = st.enter_context(nc.sbuf_tensor("tbm", [128, 2 * 512], BF16))
        hbm = st.enter_context(nc.sbuf_tensor("hbm", [128, 2 * 512], BF16))
        pgu = st.enter_context(nc.psum_tensor("pgu", [128, 4 * 512], F32))
        pr1 = st.enter_context(nc.psum_tensor("pr1", [128, 512], F32))
        pr2 = st.enter_context(nc.psum_tensor("pr2", [128, 512], F32))
        _moe_a_body(c, xb2, egw, euw, rw_b, ind8f, coefT, prw, pcols, cbm2_x,
                    sgm, tbm, hbm, pgu, pr1, pr2, xf1t)


def _moe_a_body(c, xb2, egw, euw, rw_b, ind8f, coefT, prw, pcols, cbm2_x,
                sgm, tbm, hbm, pgu, pr1, pr2, xf1t=None):
    t = c.t
    xn1_h = t["xn1_sp"].rearrange("(ki p) tok -> p ki tok", p=128)
    xf_h = t["xn1f_sp"].rearrange("(ki p) tok -> p ki tok", p=128)
    he_h = t["he_sp"].rearrange("(e ki p) tok -> p e ki tok", p=128, ki=4)
    eg_h = t["eg"].rearrange("e (ki p) o -> p e ki o", p=128)
    eu_h = t["eu"].rearrange("e (ki p) o -> p e ki o", p=128)
    A = lambda *a: "ma." + ".".join(str(x) for x in a)

    def body_sync(e):
        tok = e.dma(ind8f[:, :], t["ind8"], stream="ld")
        c.mark_tok(A("ind"), tok)
        tok = e.dma(rw_b[:, :].rearrange("p (ki o) -> p ki o", ki=8),
                    t["router_w"].rearrange("(ki p) o -> p ki o", p=128),
                    stream="ld")
        c.mark_tok(A("rw"), tok)
        for b in range(NB):
            if b >= 1:
                e.waitm(A("gum", b - 1, 7, 15))
            tok = e.dma(xb2[:, :].rearrange("p (ki tk) -> p ki tk", ki=8),
                        xn1_h[:, :, 2048 * b:2048 * (b + 1)], stream="ld")
            c.mark_tok(A("xb", b), tok)
            for tt in range(16):
                if b > 0 or tt >= 2:
                    e.waitm(A("rmm", *_ma_prevtt(b, tt)))
                tok = e.dma(xf1t[:, (tt % 2) * 1024:(tt % 2 + 1) * 1024]
                            .rearrange("p (ki tk) -> p ki tk", ki=8),
                            xf_h[:, :, 2048 * b + 128 * tt:
                                 2048 * b + 128 * (tt + 1)], stream="ld")
                c.mark_tok(A("xf", b, tt), tok)
            for ee in range(8):
                for j in range(16):
                    e.waitm(A("he", b, ee, j))
                    mo, tc = j // 4, j % 4
                    tok = e.dma(he_h[:, ee, mo, 2048 * b + 512 * tc:
                                     2048 * b + 512 * (tc + 1)],
                                hbm[:, (j % 2) * 512:(j % 2 + 1) * 512],
                                stream="sph")
                    c.mark_tok(A("hed", b, ee, j), tok)

    def body_gp(e):
        for b in range(NB):
            for ee in range(8):
                if (b, ee) > (0, 1):
                    e.waitm(A("gum", *_ma_prev(b, ee), 15))
                tok = e.dma(egw[:, (ee % 2) * 4096:(ee % 2 + 1) * 4096]
                            .rearrange("p (ki o) -> p ki o", ki=8),
                            eg_h[:, ee, :, :], stream="w")
                c.mark_tok(A("wg", b, ee), tok)
                tok = e.dma(euw[:, (ee % 2) * 4096:(ee % 2 + 1) * 4096]
                            .rearrange("p (ki o) -> p ki o", ki=8),
                            eu_h[:, ee, :, :], stream="w")
                c.mark_tok(A("wu", b, ee), tok)

    def body_tensor(e):
        e.waitm(A("rw"))
        e.waitm(A("ind"))
        for b in range(NB):
            e.waitm(A("xb", b))
            for tt in range(16):
                if tt >= 2:
                    e.waitm(A("ex", b, tt - 2))
                elif b >= 1:
                    e.waitm(A("ex", b - 1, 14 + tt))
                e.waitm(A("xf", b, tt))
                pp = pr1[:, (tt % 2) * 16:(tt % 2) * 16 + 8]
                for ki in range(8):
                    e.matmul(pp, xf1t[:, (tt % 2) * 1024 + 128 * ki:
                                      (tt % 2) * 1024 + 128 * (ki + 1)],
                             rw_b[:, 8 * ki:8 * (ki + 1)],
                             start=(ki == 0), stop=(ki == 7))
                c.mark(A("rmm", b, tt), e)
                e.waitm(A("coef", b, tt))
                if tt >= 1:
                    e.waitm(A("cte", b, tt - 1))
                elif b >= 1:
                    e.waitm(A("cbm", b - 1, 7, 3))
                e.transpose(pr2[0:8, 0:128],
                            prw[:, (tt % 2) * 24 + 16:(tt % 2) * 24 + 24],
                            t["ident_f"][:, :])
                c.mark(A("ctr", b, tt), e)
            for ee in range(8):
                e.waitm(A("cte", b, 15))
                e.waitm(A("wg", b, ee))
                e.waitm(A("wu", b, ee))
                for tc in range(4):
                    e.waitm(A("cbe", *_ma_prevj(b, ee, tc, 1, 4)))
                    e.matmul(pr2[:, 0:512], ind8f[0:8, 128 * ee:128 * (ee + 1)],
                             coefT[0:8, 512 * tc:512 * (tc + 1)],
                             start=True, stop=True)
                    c.mark(A("cbm", b, ee, tc), e)
                for j in range(16):
                    mo, tc = j // 4, j % 4
                    jj = ee * 16 + j
                    for half, w_sb in ((0, egw), (1, euw)):
                        bk = (2 * jj + half) % 4
                        pj = _ma_prevj(b, ee, 2 * j + half, 4, 32)
                        if pj[0] >= 0:
                            pb2, pe2, ph = pj
                            e.waitm(A("sil" if ph % 2 == 0 else "stt",
                                      pb2, pe2, ph // 2))
                        pp = pgu[:, 512 * bk:512 * (bk + 1)]
                        for ki in range(8):
                            e.matmul(pp, w_sb[:, (ee % 2) * 4096 + 512 * ki +
                                              128 * mo:(ee % 2) * 4096 +
                                              512 * ki + 128 * (mo + 1)],
                                     xb2[:, 2048 * ki + 512 * tc:
                                         2048 * ki + 512 * (tc + 1)],
                                     start=(ki == 0), stop=(ki == 7))
                        c.mark(A("gmm", b, ee, 2 * j + half), e)
                    c.mark(A("gum", b, ee, j), e)

    def body_scalar(e):
        for b in range(NB):
            for tt in range(16):
                e.waitm(A("rmm", b, tt))
                e.waitm(A("ngm", b, tt))
                e.activation(prw[:, (tt % 2) * 24:(tt % 2) * 24 + 8],
                             pr1[:, (tt % 2) * 16:(tt % 2) * 16 + 8],
                             AF.Exp, bias=pcols[:, 1 + 4 * (tt % 2):
                                               2 + 4 * (tt % 2)])
                c.mark(A("ex", b, tt), e)
                e.waitm(A("ctr", b, tt))
                e.activation(coefT[0:8, 128 * tt:128 * (tt + 1)],
                             pr2[0:8, 0:128], AF.Identity)
                c.mark(A("cte", b, tt), e)
            for ee in range(8):
                for tc in range(4):
                    e.waitm(A("cbm", b, ee, tc))
                    if ee >= 1:
                        e.waitm(A("he", b, ee - 1, 15))
                    e.activation(cbm2_x[:, 512 * tc:512 * (tc + 1)],
                                 pr2[:, 0:512], AF.Identity)
                    c.mark(A("cbe", b, ee, tc), e)
                for j in range(16):
                    e.waitm(A("gmm", b, ee, 2 * j))
                    e.waitm(A("stt", *_ma_prevj(b, ee, j, 2, 16)))
                    e.activation(sgm[:, (j % 2) * 512:(j % 2 + 1) * 512],
                                 pgu[:, 512 * ((2 * (ee * 16 + j)) % 4):
                                     512 * ((2 * (ee * 16 + j)) % 4) + 512],
                                 AF.Silu)
                    c.mark(A("sil", b, ee, j), e)

    def body_vector(e):
        for b in range(NB):
            for tt in range(16):
                sl = (tt % 2) * 24
                cs = 4 * (tt % 2)
                e.waitm(A("rmm", b, tt))
                if tt >= 2:
                    e.waitm(A("ctr", b, tt - 2))
                elif b >= 1:
                    e.waitm(A("ctr", b - 1, 14 + tt))
                pp = pr1[:, (tt % 2) * 16:(tt % 2) * 16 + 8]
                e.tensor_reduce(pcols[:, cs:cs + 1], pp, AX.X, ALU.max)
                e.tensor_scalar(pcols[:, cs + 1:cs + 2], pcols[:, cs:cs + 1],
                                -1.0, None, ALU.mult)
                c.mark(A("ngm", b, tt), e)
                e.waitm(A("ex", b, tt))
                pb = prw[:, sl:sl + 8]
                e.tensor_reduce(pcols[:, cs + 2:cs + 3], pb, AX.X, ALU.add)
                e.reciprocal(pcols[:, cs + 2:cs + 3], pcols[:, cs + 2:cs + 3])
                e.tensor_scalar(pb, pb, pcols[:, cs + 2:cs + 3], None, ALU.mult)
                e.tensor_reduce(pcols[:, cs:cs + 1], pb, AX.X, ALU.max)
                e.tensor_scalar(prw[:, sl + 8:sl + 16], pb,
                                pcols[:, cs:cs + 1], None, ALU.is_equal)
                e.tensor_tensor(prw[:, sl + 8:sl + 16], prw[:, sl + 8:sl + 16],
                                pb, ALU.mult)
                e.tensor_tensor(prw[:, sl + 8:sl + 16], pb,
                                prw[:, sl + 8:sl + 16], ALU.subtract)
                e.tensor_reduce(pcols[:, cs + 1:cs + 2],
                                prw[:, sl + 8:sl + 16], AX.X, ALU.max)
                e.tensor_scalar(prw[:, sl + 8:sl + 16], pb,
                                pcols[:, cs + 1:cs + 2], None, ALU.is_ge)
                e.tensor_tensor(prw[:, sl + 16:sl + 24], pb,
                                prw[:, sl + 8:sl + 16], ALU.mult)
                c.mark(A("coef", b, tt), e)
            for ee in range(8):
                for j in range(16):
                    mo, tc = j // 4, j % 4
                    jj = ee * 16 + j
                    e.waitm(A("sil", b, ee, j))
                    e.waitm(A("gmm", b, ee, 2 * j + 1))
                    e.waitm(A("cbe", b, ee, tc))
                    e.scalar_tensor_tensor(
                        tbm[:, (j % 2) * 512:(j % 2 + 1) * 512],
                        pgu[:, 512 * ((2 * jj + 1) % 4):
                            512 * ((2 * jj + 1) % 4) + 512], 1.0,
                        sgm[:, (j % 2) * 512:(j % 2 + 1) * 512],
                        ALU.mult, ALU.mult)
                    c.mark(A("stt", b, ee, j), e)
                    e.waitm(A("hed", *_ma_prevj(b, ee, j, 2, 16)))
                    e.tensor_tensor(hbm[:, (j % 2) * 512:(j % 2 + 1) * 512],
                                    tbm[:, (j % 2) * 512:(j % 2 + 1) * 512],
                                    cbm2_x[:, 512 * tc:512 * (tc + 1)], ALU.mult)
                    c.mark(A("he", b, ee, j), e)

    c.stage("moea", [("sync", body_sync), ("gpsimd", body_gp),
                     ("tensor", body_tensor), ("scalar", body_scalar),
                     ("vector", body_vector)])


def _ma_prevtt(b, tt):
    k = b * 16 + tt - 2
    return (k // 16, k % 16)


def _ma_prev(b, ee):
    k = b * 8 + ee - 2
    return (k // 8, k % 8)


def _ma_prevj(b, ee, j, back, n):
    k = (b * 8 + ee) * n + j - back
    if k < 0:
        return (-1, -1, -1)
    return (k // (8 * n), (k // n) % 8, k % n)



def _moe_b(c):
    nc = c.nc
    with contextlib.ExitStack() as st:
        edw = st.enter_context(nc.sbuf_tensor("edw", [128, 32 * 1024], BF16))
        hel = st.enter_context(nc.sbuf_tensor("hel", [128, 2 * 16384], BF16))
        shl = st.enter_context(nc.sbuf_tensor("shl", [128, 2 * 512], F32))
        x1l = st.enter_context(nc.sbuf_tensor("x1l", [128, 2 * 512], F32))
        s1l = st.enter_context(nc.sbuf_tensor("s1l", [128, 2 * 512], F32))
        fbb = st.enter_context(nc.sbuf_tensor("fbb", [128, 2 * 512], F32))
        g1b = st.enter_context(nc.sbuf_tensor("g1b", [128, 2 * 512], F32))
        otb = st.enter_context(nc.sbuf_tensor("otb", [128, 2 * 512], F32))
        pdw = st.enter_context(nc.psum_tensor("pdw", [128, 2 * 512], F32))
        _moe_b_body(c, edw, hel, shl, x1l, s1l, fbb, g1b, otb, pdw)


def _moe_b_body(c, edw, hel, shl, x1l, s1l, fbb, g1b, otb, pdw):
    t = c.t
    he_h = t["he_sp"].rearrange("(e ki p) tok -> p e ki tok", p=128, ki=4)
    ed_h = t["ed"].rearrange("e (ki p) o -> p e ki o", p=128)
    sh_h = t["sh_sp"].rearrange("(mo p) tok -> p mo tok", p=128)
    x1_h = t["x1_sp"].rearrange("(mo p) tok -> p mo tok", p=128)
    s1_h = t["s1_sp"].rearrange("(mo p) tok -> p mo tok", p=128)
    out_h = t["out_fm"].rearrange("(mo p) tok -> p mo tok", p=128)
    A = lambda *a: "mb." + ".".join(str(x) for x in a)

    def body_sync(e):
        for tc in range(NC512):
            if tc >= 2:
                e.waitm(A("dmm", tc - 2, 7))
            tok = e.dma(hel[:, (tc % 2) * 16384:(tc % 2 + 1) * 16384]
                        .rearrange("p (ee ki tk) -> p ee ki tk", ee=8, ki=4),
                        he_h[:, :, :, 512 * tc:512 * (tc + 1)], stream="ld")
            c.mark_tok(A("he", tc), tok)
            for mo in range(8):
                i = tc * 8 + mo
                if i >= 2:
                    e.waitm(A("stt", *_mb_prev(tc, mo, 2)))
                tok = e.dma(shl[:, (i % 2) * 512:(i % 2 + 1) * 512],
                            sh_h[:, mo, 512 * tc:512 * (tc + 1)], stream="ld")
                c.mark_tok(A("sh", tc, mo), tok)
                if i >= 2:
                    e.waitm(A("gl", *_mb_prev(tc, mo, 2)))
                tok = e.dma(s1l[:, (i % 2) * 512:(i % 2 + 1) * 512],
                            s1_h[:, mo, 512 * tc:512 * (tc + 1)], stream="ld")
                c.mark_tok(A("s1", tc, mo), tok)
                if i >= 2:
                    e.waitm(A("ot", *_mb_prev(tc, mo, 2)))
                tok = e.dma(x1l[:, (i % 2) * 512:(i % 2 + 1) * 512],
                            x1_h[:, mo, 512 * tc:512 * (tc + 1)], stream="ld")
                c.mark_tok(A("x1", tc, mo), tok)
                e.waitm(A("ot", tc, mo))
                tok = e.dma(out_h[:, mo, 512 * tc:512 * (tc + 1)],
                            otb[:, (i % 2) * 512:(i % 2 + 1) * 512],
                            stream="out")
                c.mark_tok(A("od", tc, mo), tok)
        for snm in ("ld", "w", "spx", "sph", "out"):
            e.wait_stream_all(snm)

    def body_gp(e):
        tok = e.dma(edw[:, :].rearrange("p (ee ki o) -> p ee ki o", ee=8, ki=4),
                    ed_h[:, :, :, :], stream="w")
        c.mark_tok(A("ed"), tok)

    def body_tensor(e):
        e.waitm(A("ed"))
        edv = edw[:, :].rearrange("p (ee ki o) -> p ee ki o", ee=8, ki=4)
        for tc in range(NC512):
            e.waitm(A("he", tc))
            hv = hel[:, (tc % 2) * 16384:(tc % 2 + 1) * 16384].rearrange(
                "p (ee ki tk) -> p ee ki tk", ee=8, ki=4)
            for mo in range(8):
                i = tc * 8 + mo
                if i >= 2:
                    e.waitm(A("stt", *_mb_prev(tc, mo, 2)))
                pp = pdw[:, (i % 2) * 512:(i % 2 + 1) * 512]
                for ee in range(8):
                    for ki in range(4):
                        e.matmul(pp, edv[:, ee, ki, 128 * mo:128 * (mo + 1)],
                                 hv[:, ee, ki, :],
                                 start=(ee == 0 and ki == 0),
                                 stop=(ee == 7 and ki == 3))
                c.mark(A("dmm", tc, mo), e)

    def body_scalar(e):
        for tc in range(NC512):
            for mo in range(8):
                i = tc * 8 + mo
                e.waitm(A("stt", tc, mo))
                if i >= 2:
                    e.waitm(A("gl", *_mb_prev(tc, mo, 2)))
                e.activation(g1b[:, (i % 2) * 512:(i % 2 + 1) * 512],
                             fbb[:, (i % 2) * 512:(i % 2 + 1) * 512], AF.Silu)
                c.mark(A("g1", tc, mo), e)

    def body_vector(e):
        for tc in range(NC512):
            for mo in range(8):
                i = tc * 8 + mo
                e.waitm(A("dmm", tc, mo))
                e.waitm(A("sh", tc, mo))
                pp = pdw[:, (i % 2) * 512:(i % 2 + 1) * 512]
                e.scalar_tensor_tensor(fbb[:, (i % 2) * 512:(i % 2 + 1) * 512],
                                       pp, 1.0,
                                       shl[:, (i % 2) * 512:(i % 2 + 1) * 512],
                                       ALU.mult, ALU.add)
                c.mark(A("stt", tc, mo), e)
                e.waitm(A("g1", tc, mo))
                e.waitm(A("s1", tc, mo))
                e.tensor_tensor(g1b[:, (i % 2) * 512:(i % 2 + 1) * 512],
                                s1l[:, (i % 2) * 512:(i % 2 + 1) * 512],
                                g1b[:, (i % 2) * 512:(i % 2 + 1) * 512],
                                ALU.mult)
                c.mark(A("gl", tc, mo), e)
                e.waitm(A("x1", tc, mo))
                if i >= 2:
                    e.waitm(A("od", *_mb_prev(tc, mo, 2)))
                e.tensor_tensor(otb[:, (i % 2) * 512:(i % 2 + 1) * 512],
                                x1l[:, (i % 2) * 512:(i % 2 + 1) * 512],
                                g1b[:, (i % 2) * 512:(i % 2 + 1) * 512],
                                ALU.add)
                c.mark(A("ot", tc, mo), e)

    c.stage("moeb", [("sync", body_sync), ("gpsimd", body_gp),
                     ("tensor", body_tensor), ("scalar", body_scalar),
                     ("vector", body_vector)])


def _mb_prev(tc, mo, back):
    k = tc * 8 + mo - back
    return (k // 8, k % 8)


def _coll_max(c, slot, scale_col, mark_in, extra_div=1.0):
    with c.nc.psum_tensor("pcol%d" % slot, [128, 512], F32) as pc:
        _coll_max_body(c, slot, scale_col, mark_in, extra_div, pc)


def _coll_max_body(c, slot, scale_col, mark_in, extra_div, pc):
    t = c.t
    P = lambda nm: "cm%d.%s" % (slot, nm)

    def body_tensor(e):
        e.waitm(mark_in)
        e.transpose(pc[0:1, 0:128], t["mx_acc"][:, 0:1], t["ident_f"][:, :])
        c.mark(P("tr"), e)
        e.waitm(P("val"))
        e.matmul(pc[:, 128:129], t["onesrow_f"][0:1, :],
                 t["ccv"][0:1, 8 + slot:9 + slot], start=True, stop=True)
        c.mark(P("bc"), e)

    def body_gp(e):
        e.waitm(P("red"))
        tok = e.dma(t["cc_in"][0:1, :], t["ccv"][0:1, 0:16], stream="w")
        e.wait(tok[0], tok[1])
        ctok = e.collective_max(t["cc_in"].opt(), t["cc_out"].opt())
        e.wait(ctok[0], ctok[1])
        tok = e.dma(t["ccv"][0:1, 8 + slot:9 + slot],
                    t["cc_out"][0:1, slot:slot + 1], stream="w")
        c.mark_tok(P("ccout"), tok)

    def body_vector(e):
        e.waitm(P("tr"))
        e.tensor_reduce(t["ccv"][0:1, slot:slot + 1], pc[0:1, 0:128], AX.X, ALU.max)
        c.mark(P("red"), e)
        e.waitm(P("ccout"))
        e.tensor_scalar(t["ccv"][0:1, 8 + slot:9 + slot],
                        t["ccv"][0:1, 8 + slot:9 + slot],
                        1.0 / (QMAX * extra_div), 1e-12, ALU.mult, ALU.max)
        c.mark(P("val"), e)
        e.waitm(P("bc"))
        e.tensor_copy(t["scol"][:, scale_col:scale_col + 1], pc[:, 128:129])
        c.mark(P("done"), e)

    c.stage("coll%d" % slot, [("tensor", body_tensor), ("gpsimd", body_gp),
                              ("vector", body_vector)])


def _finish_stub(c):
    t = c.t

    def body_scalar(e):
        e.waitm("cs0.done")
        e.waitm("sf.shd.15.7")
        tok = e.dma(t["out_fm"][0:128, 0:128], t["ident_f"][:, :], stream="out")
        e.wait(tok[0], tok[1])

    c.stage("finish", [("scalar", body_scalar)])



_WNAMES = ("uv", "projwT", "mlpw1", "mlpw2", "wup_nat", "wgate_nat", "wdown_nat",
           "wupT", "wgateT", "wdownT", "wqT", "wkT", "wvT", "woTp", "router_w",
           "eg", "eu", "ed", "sg", "su", "sd", "projb", "n1w", "n2w", "mlpb1",
           "mlpb2", "bo", "params")
_CNAMES = ("ident", "rampm", "ramp127_col", "mask_keep", "shift4", "shiftprev4",
           "ones_col", "ones_row", "rep32", "s2_65", "blockdiag4", "ind8")


def _run(inputs, debug=()):
    consts = _host_consts()
    w = _prep_weights(inputs)
    x = np.asarray(inputs["L0_attn_h"], np.float32)
    nc, c = build_nc(debug=debug)
    in_maps = []
    for core in range(8):
        xs = np.ascontiguousarray(
            x[4 * core:4 * core + 4].transpose(2, 0, 1).reshape(D, NTOK))
        m = {"x_fm": xs}
        m.update({k: w[k] for k in _WNAMES})
        m.update({k: consts[k] for k in _CNAMES})
        in_maps.append(m)
    return run_bass_kernel_spmd(nc, in_maps, core_ids=list(range(8)))


def kernel(**inputs):
    res = _run(inputs, debug=DBG)
    outs = []
    for core in range(8):
        o = np.asarray(res.results[core]["out_fm"])
        outs.append(o.reshape(D, NB, SEQ).transpose(1, 2, 0))
    return np.ascontiguousarray(np.concatenate(outs, axis=0), dtype=np.float32)
